# version 10
# baseline (speedup 1.0000x reference)
import contextlib
import numpy as np
import ml_dtypes
import concourse.bass as bass
import concourse.mybir as mybir
from concourse.bass_utils import run_bass_kernel_spmd
from concourse.ap import AP

F32 = mybir.dt.float32
BF16 = mybir.dt.bfloat16
I32 = mybir.dt.int32
AF = mybir.ActivationFunctionType
ALU = mybir.AluOpType
AX = mybir.AxisListType

NCORES = 8
S = 8192
D = 2048
TOK = S // NCORES
NT = TOK // 128
ND = D // 128
HEADS = 16
HD = 128
HPC = HEADS // NCORES
DFF0 = 8192
DIN = 4096
DSTATE = 128
NH_M = 64
PH = 64
DXBC = 4352
DIP = 8512
CHUNK = 64
NCHUNK = TOK // CHUNK
HID1 = 5632

ENGS = ("sync", "tensor", "scalar", "vector", "gpsimd")


class Ctx:

    def __init__(self, eng_name, eng):
        self.en = eng_name
        self.eng = eng
        self.c = {}

    def val(self, sem):
        return self.c.get(sem.name, 0)

    def _bump(self, sem, inc):
        self.c[sem.name] = self.c.get(sem.name, 0) + inc
        return self.c[sem.name]

    def do(self, which, f, sem=None, inc=1, wait=()):
        if self.en == which:
            for (s, v) in wait:
                self.eng.wait_ge(s, v)
            ins = f(self.eng)
            if sem is not None:
                ins.then_inc(sem, inc)
        if sem is not None:
            return self._bump(sem, inc)
        return None

    def burst(self, which, fns, sem, wait=None):
        tgt = self.val(sem) if wait is None else wait
        if self.en == which:
            self.eng.wait_ge(sem, tgt)
            for fn in fns:
                fn(self.eng).then_inc(sem, 16)
        for _ in fns:
            self._bump(sem, 16)

    def wait(self, which, sem, v):
        if self.en == which:
            self.eng.wait_ge(sem, v)

    def barrier(self, bar_sem, engines=ENGS):
        n = len(engines)
        base = self.val(bar_sem)
        for e in engines:
            self.do(e, lambda eng: eng.nop(), sem=bar_sem, inc=1)
        for e in engines:
            self.wait(e, bar_sem, base + n)



class Arena:

    def __init__(self, nc, base=None):
        self.nc = nc
        self.base = ((nc.sbuf_base + 31) // 32 * 32) if base is None else base
        self.off = 0
        self.pfx = ""

    def reset(self, pfx):
        self.off = 0
        self.pfx = pfx

    def sb(self, name, shape, dtype):
        import math
        elem = {F32: 4, BF16: 2, I32: 4}[dtype]
        per_part = shape[1] * elem if len(shape) == 2 else int(np.prod(shape[1:])) * elem
        per_part = (per_part + 31) // 32 * 32
        t = self.nc.alloc_sbuf_tensor_at(self.pfx + name, list(shape), dtype, offset=self.base + self.off)
        self.off += per_part
        return t


def build_identity(ctx, g_ident):
    ctx.do("gpsimd", lambda e: e.memset(g_ident[:], 1.0))
    ctx.do("gpsimd", lambda e: e.affine_select(
        out=g_ident[:], in_=g_ident[:], compare_op=ALU.is_equal, fill=0.0,
        base=0, pattern=[[-1, 128]], channel_multiplier=1))


def emit_ln_fm(ctx, sem, x_fm, out_fm, hp, nd=ND, ntok=TOK):
    d_model = nd * 128
    w = lambda: [(sem, ctx.val(sem))]
    tb0, tb1 = hp["tb0"], hp["tb1"]
    abc, bbc = hp["abc"], hp["bbc"]
    row = hp["row"]
    psA, psB = hp["psA"], hp["psB"]
    oc, orow = hp["ones_col_b"], hp["ones_row"]
    for tc in range(ntok // 512):
        def xs(d):
            return x_fm[:, d * ntok + tc * 512: d * ntok + tc * 512 + 512]
        for d in range(nd):
            ctx.do("scalar", lambda e, d=d: e.activation(out=tb0[:], in_=xs(d), func=AF.Identity), sem=sem, wait=w())
            ctx.do("scalar", lambda e, d=d: e.activation(out=tb1[:], in_=xs(d), func=AF.Square), sem=sem, wait=w())
            ctx.do("tensor", lambda e, d=d: e.matmul(psA[0:1, 0:512], oc[:], tb0[:], start=(d == 0), stop=(d == nd - 1)), sem=sem, wait=w())
            ctx.do("tensor", lambda e, d=d: e.matmul(psA[0:1, 512:1024], oc[:], tb1[:], start=(d == 0), stop=(d == nd - 1)), sem=sem, wait=w())
        ctx.do("vector", lambda e: e.tensor_scalar(out=row[0:1, 0:512], in0=psA[0:1, 0:512], scalar1=1.0 / d_model, scalar2=None, op0=ALU.mult), sem=sem, wait=w())
        ctx.do("vector", lambda e: e.tensor_scalar(out=row[0:1, 512:1024], in0=psA[0:1, 512:1024], scalar1=1.0 / d_model, scalar2=None, op0=ALU.mult), sem=sem, wait=w())
        ctx.do("vector", lambda e: e.scalar_tensor_tensor(out=row[0:1, 512:1024], in0=row[0:1, 0:512], scalar=-1.0, in1=row[0:1, 512:1024], op0=ALU.mult if False else ALU.bypass, op1=ALU.bypass) if False else None, sem=None)
        ctx.do("vector", lambda e: e.tensor_tensor(out=row[0:1, 1024:1536], in0=row[0:1, 0:512], in1=row[0:1, 0:512], op=ALU.mult), sem=sem, wait=w())
        ctx.do("vector", lambda e: e.tensor_tensor(out=row[0:1, 512:1024], in0=row[0:1, 512:1024], in1=row[0:1, 1024:1536], op=ALU.subtract), sem=sem, wait=w())
        ctx.do("vector", lambda e: e.tensor_scalar(out=row[0:1, 512:1024], in0=row[0:1, 512:1024], scalar1=1e-5, scalar2=None, op0=ALU.add), sem=sem, wait=w())
        ctx.do("scalar", lambda e: e.activation(out=row[0:1, 512:1024], in_=row[0:1, 512:1024], func=AF.Ln), sem=sem, wait=w())
        ctx.do("scalar", lambda e: e.activation(out=row[0:1, 512:1024], in_=row[0:1, 512:1024], func=AF.Exp, scale=-0.5), sem=sem, wait=w())
        ctx.do("vector", lambda e: e.tensor_tensor(out=row[0:1, 1024:1536], in0=row[0:1, 0:512], in1=row[0:1, 512:1024], op=ALU.mult), sem=sem, wait=w())
        ctx.do("vector", lambda e: e.tensor_scalar(out=row[0:1, 1024:1536], in0=row[0:1, 1024:1536], scalar1=-1.0, scalar2=None, op0=ALU.mult), sem=sem, wait=w())
        ctx.do("tensor", lambda e: e.matmul(psB[:], orow[:], row[0:1, 512:1024], start=True, stop=True), sem=sem, wait=w())
        ctx.do("scalar", lambda e: e.activation(out=abc[:], in_=psB[:], func=AF.Copy), sem=sem, wait=w())
        ctx.do("tensor", lambda e: e.matmul(psB[:], orow[:], row[0:1, 1024:1536], start=True, stop=True), sem=sem, wait=w())
        ctx.do("scalar", lambda e: e.activation(out=bbc[:], in_=psB[:], func=AF.Copy), sem=sem, wait=w())
        for d in range(nd):
            ctx.do("vector", lambda e, d=d: e.tensor_tensor(out=tb0[:] if False else hp["tf"][:], in0=xs(d), in1=abc[:], op=ALU.mult), sem=sem, wait=w())
            ctx.do("vector", lambda e, d=d, tc=tc: e.tensor_tensor(out=out_fm[:, d * ntok + tc * 512: d * ntok + tc * 512 + 512], in0=hp["tf"][:], in1=bbc[:], op=ALU.add), sem=sem, wait=w())


def ln_helpers(ab, PSB):
    return {
        "tb0": ab.sb("tb0", [128, 512], BF16),
        "tb1": ab.sb("tb1", [128, 512], BF16),
        "tf": ab.sb("tf", [128, 512], F32),
        "abc": ab.sb("abc", [128, 512], F32),
        "bbc": ab.sb("bbc", [128, 512], F32),
        "row": ab.sb("row", [1, 1536], F32),
        "ones_col_b": ab.sb("ones_col_b", [128, 1], BF16),
        "ones_row": ab.sb("ones_row", [1, 128], F32),
        "psA": PSB[0:1, 0:1024],
        "psB": PSB[:, 1024:1536],
    }


def make_u1(nc, io, ab):
    ab.reset("u1_")
    x_d, lnx_d, PSB = io["x0"], io["lnx_c"], io["PSB"]
    x_fm = ab.sb("x_fm", [128, ND * TOK], F32)
    lnxfm = ab.sb("lnxfm", [128, ND * TOK], BF16)
    hp = ln_helpers(ab, PSB)

    def emit(ctx, sem):
        w = lambda: [(sem, ctx.val(sem))]
        ctx.do("gpsimd", lambda e: e.memset(hp["ones_col_b"][:], 1.0), sem=sem, wait=w())
        ctx.do("gpsimd", lambda e: e.memset(hp["ones_row"][:], 1.0), sem=sem, wait=w())
        ctx.burst("sync", [
            (lambda e, d=d: e.dma_start(out=x_fm[:, d * TOK:(d + 1) * TOK], in_=x_d[d * 128:(d + 1) * 128, :]))
            for d in range(ND)], sem)
        emit_ln_fm2(ctx, sem, lambda d, tc: x_fm[:, d * TOK + tc * 512: d * TOK + tc * 512 + 512], lnxfm, hp, [0, 1], TOK, TOK)
        ctx.burst("sync", [
            (lambda e, d=d: e.dma_start(out=lnx_d[d * 128:(d + 1) * 128, :], in_=lnxfm[:, d * TOK:(d + 1) * TOK]))
            for d in range(ND)], sem)
    return emit


def run_phases(nc, emits):
    st2 = contextlib.ExitStack()
    sem = st2.enter_context(nc.semaphore("s"))

    def prog(eng_name):
        def f(eng):
            ctx = Ctx(eng_name, eng)
            for em in emits:
                em(ctx, sem)
            ctx.wait("sync", sem, ctx.val(sem))
        return f

    with nc.Block() as block:
        for en in ENGS:
            getattr(block, en)(prog(en))
    st2.close()


def build_u1():
    nc = bass.Bass("TRN2", target_bir_lowering=False)
    io = {
        "x0": nc.dram_tensor("x", [D, TOK], F32, kind="ExternalInput").ap(),
        "lnx_c": nc.dram_tensor("lnx", [D, TOK], BF16, kind="ExternalOutput").ap(),
    }
    st = contextlib.ExitStack()
    io["PSA"] = st.enter_context(nc.psum_tensor("PSA", [128, 2048], F32))
    io["PSB"] = st.enter_context(nc.psum_tensor("PSB", [128, 2048], F32))
    ab = Arena(nc)
    run_phases(nc, [make_u1(nc, io, ab)])
    st.close()
    return nc


ATT_SCALE = float(HD) ** -0.5


def make_u2(nc, io, ab):
    ab.reset("u2_")
    lnx_tile = io["lnx_tile"]
    wqT_d, wkT_d, wvT_d = io["wqT"], io["wkT"], io["wvT"]
    bq_d, bk_d, bv_d = io["bq"], io["bk"], io["bv"]
    attn_d = io["attn_c"]
    PSA, PSB = io["PSA"], io["PSB"]
    sb = ab.sb
    wld_f = sb("wld_f", [128, 256], F32)
    wq_b = sb("wq_b", [128, ND * 256], BF16)
    wk_b = sb("wk_b", [128, ND * 256], BF16)
    wv_b = sb("wv_b", [128, ND * 256], BF16)
    bias_sb = sb("bias_sb", [128, 8], F32)
    bvbc = sb("bvbc", [128, 256], F32)
    lnx_sb = [sb(f"lnx_sb{i}", [128, ND * 512], BF16) for i in range(2)]
    q_fm = sb("q_fm", [128, HPC * S], BF16)
    k_fm = sb("k_fm", [128, HPC * S], BF16)
    v_sb = sb("v_sb", [128, (S // 128) * 260], BF16)
    pt_sb = sb("pt_sb", [128, 1024], BF16)
    atile2 = [sb(f"atile2_{i}", [128, 512], BF16) for i in range(2)]
    sq_sb = sb("sq_sb", [128, 2048], BF16)
    scal = sb("scal", [1, 16], F32)
    scal2 = sb("scal2", [1, 512], F32)
    negb = sb("negb", [128, 2], F32)
    ones_col = sb("ones_col", [128, 1], BF16)
    ones_row = sb("ones_row", [1, 128], F32)
    ident = sb("ident", [128, 128], BF16)
    pq_on = "w1q" in io
    if pq_on:
        pq_win = [sb(f"pq_win{i}", [128, 1024], F32) for i in range(2)]
        pq_ma = sb("pq_ma", [128, 1024], F32)
        pq_q = [sb(f"pq_q{i}", [128, 1024], BF16) for i in range(2)]
        pq_acc8 = sb("pq_acc8", [8, 2], F32)
        pq_ones8 = sb("pq_ones8", [8, 1], F32)
        pq_s4 = sb("pq_s4", [1, 4], F32)
        pq_thrc = sb("pq_thrc", [128, 4], F32)
    psS = PSA[:, 0:1024]
    psK2 = PSA[:, 1024:2048]
    psV = PSB[:, 0:1024]
    psT = PSB[:, 1024:1088].bitcast(BF16)
    psM = PSB[:, 1536:2048]
    NQC = S // 512

    def emit(ctx, sem):
        w = lambda: [(sem, ctx.val(sem))]

        def chain(which, fns, inc_last=1):
            if ctx.en == which:
                ctx.eng.wait_ge(sem, ctx.val(sem))
                for fn in fns:
                    ins = fn(ctx.eng)
                ins.then_inc(sem, inc_last)
            ctx._bump(sem, inc_last)

        build_identity(ctx, ident)
        ctx.do("gpsimd", lambda e: e.memset(ones_col[:], 1.0), sem=sem, wait=w())
        ctx.do("gpsimd", lambda e: e.memset(ones_row[:], 1.0), sem=sem, wait=w())
        ctx.do("gpsimd", lambda e: e.memset(v_sb[:], 0.0), sem=sem, wait=w())

        if pq_on:
            pq_in, pq_v, pq_out = io["pq_in"], io["pq_v"], io["pq_out"]
            ctx.do("gpsimd", lambda e: e.memset(pq_ones8[:], 1.0), sem=sem, wait=w())
            ctx.do("sync", lambda e: e.dma_start(out=pq_acc8[:, :], in_=io["wsum_all"][:, :]), sem=sem, inc=16, wait=w())
            ctx.do("tensor", lambda e: e.matmul(psM[0:1, 0:2], pq_ones8[:], pq_acc8[0:8, 0:2], start=True, stop=True), sem=sem, wait=w())
            ctx.do("vector", lambda e: e.tensor_scalar(out=pq_s4[0:1, 0:2], in0=psM[0:1, 0:2], scalar1=0.7 / (D * DFF0), scalar2=None, op0=ALU.mult), sem=sem, wait=w())
            ctx.do("vector", lambda e: e.tensor_scalar(out=pq_s4[0:1, 2:4], in0=pq_s4[0:1, 0:2], scalar1=-1.0, scalar2=None, op0=ALU.mult), sem=sem, wait=w())
            ctx.do("tensor", lambda e: e.matmul(psM[:, 0:4], ones_row[:], pq_s4[0:1, 0:4], start=True, stop=True), sem=sem, wait=w())
            ctx.do("scalar", lambda e: e.activation(out=pq_thrc[:], in_=psM[:, 0:4], func=AF.Copy), sem=sem, wait=w())
            pq_rounds = []
            for r in range(D // 128):
                for cc in range(DFF0 // 1024):
                    pq_rounds.append((io["w1T"][r * 128:(r + 1) * 128, cc * 1024:(cc + 1) * 1024],
                                      io["w1q"][r * 128:(r + 1) * 128, cc * 1024:(cc + 1) * 1024], 0))
            for r in range(DFF0 // 128):
                for cc in range(D // 1024):
                    pq_rounds.append((io["w2T"][r * 128:(r + 1) * 128, cc * 1024:(cc + 1) * 1024],
                                      io["w2q"][r * 128:(r + 1) * 128, cc * 1024:(cc + 1) * 1024], 1))
            pq_st = {"i": 0}

            def pq_step():
                i = pq_st["i"]
                if i >= len(pq_rounds):
                    return
                src, dst, tc_ = pq_rounds[i]
                wt_in = ([] if i < 2 else [(pq_v, 2 * (i - 1))]) + ([] if i == 0 else [(pq_in, 16 * i)])
                ctx.do("gpsimd", lambda e, src=src, i=i: e.dma_start(out=pq_win[i % 2][:], in_=src), sem=pq_in, inc=16, wait=wt_in)
                ctx.do("vector", lambda e, i=i, tc_=tc_: e.tensor_scalar(out=pq_ma[:], in0=pq_win[i % 2][:], scalar1=pq_thrc[:, tc_:tc_ + 1], scalar2=-1.0, op0=ALU.is_ge, op1=ALU.add), sem=pq_v, inc=1, wait=[(pq_in, 16 * (i + 1))])
                wt_q = [] if i < 2 else [(pq_out, 16 * (i - 1))]
                ctx.do("vector", lambda e, i=i, tc_=tc_: e.scalar_tensor_tensor(out=pq_q[i % 2][:], in0=pq_win[i % 2][:], scalar=pq_thrc[:, 2 + tc_:3 + tc_], in1=pq_ma[:], op0=ALU.is_ge, op1=ALU.add), sem=pq_v, inc=1, wait=wt_q)
                wt_o = [(pq_v, 2 * (i + 1))] + ([] if i == 0 else [(pq_out, 16 * i)])
                ctx.do("sync", lambda e, dst=dst, i=i: e.dma_start(out=dst, in_=pq_q[i % 2][:]), sem=pq_out, inc=16, wait=wt_o)
                pq_st["i"] += 1
        else:
            def pq_step():
                return
        for h in range(2):
            ctx.do("gpsimd", lambda e, h=h: e.memset(v_sb[:].rearrange("p (t c) -> p t c", c=260)[:, :, h * 129 + 128: h * 129 + 129], 1.0), sem=sem, wait=w())
        ctx.burst("gpsimd", [
            (lambda e, d=d, wT_d=wT_d, w_b=w_b: e.dma_start(out=w_b[:, d * 256:(d + 1) * 256], in_=wT_d[d * 128:(d + 1) * 128, :]))
            for d in range(ND) for (wT_d, w_b) in ((wqT_d, wq_b), (wkT_d, wk_b), (wvT_d, wv_b))], sem)
        ctx.burst("sync", [
            (lambda e, h=h: e.dma_start(out=bias_sb[:, h:h + 1], in_=bq_d[h * 128:(h + 1) * 128].unsqueeze(1)))
            for h in range(2)] + [
            (lambda e, h=h: e.dma_start(out=bias_sb[:, 2 + h:3 + h], in_=bk_d[h * 128:(h + 1) * 128].unsqueeze(1)))
            for h in range(2)], sem)
        ctx.do("vector", lambda e: e.tensor_scalar(out=bias_sb[:, 0:2], in0=bias_sb[:, 0:2], scalar1=ATT_SCALE, scalar2=None, op0=ALU.mult), sem=sem, wait=w())
        ctx.do("sync", lambda e: e.dma_start(out=bvbc[0:1, 0:256], in_=bv_d.unsqueeze(0)), sem=sem, inc=16, wait=w())
        ctx.do("tensor", lambda e: e.matmul(psM[:, 0:256], ones_row[:], bvbc[0:1, 0:256], start=True, stop=True), sem=sem, wait=w())
        ctx.do("scalar", lambda e: e.activation(out=bvbc[:], in_=psM[:, 0:256], func=AF.Copy), sem=sem, wait=w())

        for qc in range(NQC):
            buf = lnx_sb[qc % 2]
            ctx.burst("sync", [
                (lambda e, d=d, qc=qc, buf=buf: e.dma_start(out=buf[:, d * 512:(d + 1) * 512], in_=lnx_tile(d, qc)))
                for d in range(ND)], sem)
            for h in range(2):
                chain("tensor", [
                    (lambda e, h=h, d=d, buf=buf: e.matmul(psS[:, h * 512:(h + 1) * 512], wq_b[:, d * 256 + h * 128: d * 256 + h * 128 + 128], buf[:, d * 512:(d + 1) * 512], start=(d == 0), stop=(d == ND - 1)))
                    for d in range(ND)])
                chain("tensor", [
                    (lambda e, h=h, d=d, buf=buf: e.matmul(psK2[:, h * 512:(h + 1) * 512], wk_b[:, d * 256 + h * 128: d * 256 + h * 128 + 128], buf[:, d * 512:(d + 1) * 512], start=(d == 0), stop=(d == ND - 1)))
                    for d in range(ND)])
                ctx.do("scalar", lambda e, h=h, qc=qc: e.activation(out=q_fm[:, h * S + qc * 512: h * S + qc * 512 + 512], in_=psS[:, h * 512:(h + 1) * 512], func=AF.Identity, scale=ATT_SCALE, bias=bias_sb[:, h:h + 1]), sem=sem, wait=w())
                ctx.do("scalar", lambda e, h=h, qc=qc: e.activation(out=k_fm[:, h * S + qc * 512: h * S + qc * 512 + 512], in_=psK2[:, h * 512:(h + 1) * 512], func=AF.Identity, scale=1.0, bias=bias_sb[:, 2 + h:3 + h]), sem=sem, wait=w())
            for j in range(4):
                chain("tensor", [
                    (lambda e, j=j, d=d, buf=buf: e.matmul(psV[:, j * 256:(j + 1) * 256], buf[:, d * 512 + j * 128: d * 512 + j * 128 + 128], wv_b[:, d * 256:(d + 1) * 256], start=(d == 0), stop=(d == ND - 1)))
                    for d in range(ND)])
                ctx.do("vector", lambda e, j=j: e.tensor_add(out=psV[:, j * 256:(j + 1) * 256], in0=psV[:, j * 256:(j + 1) * 256], in1=bvbc[:]), sem=sem, wait=w())
                tt = qc * 4 + j
                for h in range(2):
                    ctx.do("vector", lambda e, j=j, h=h, tt=tt: e.tensor_copy(out=v_sb[:, tt * 260 + h * 129: tt * 260 + h * 129 + 128], in_=psV[:, j * 256 + h * 128: j * 256 + (h + 1) * 128]), sem=sem, wait=w())

        for h in range(2):
            for (srcb, outc) in ((q_fm, 0), (k_fm, 1)):
                for ch in range(S // 2048):
                    ctx.do("scalar", lambda e, h=h, ch=ch, srcb=srcb: e.activation(out=sq_sb[:], in_=srcb[:, h * S + ch * 2048: h * S + ch * 2048 + 2048], func=AF.Square), sem=sem, wait=w())
                    for sub in range(4):
                        ctx.do("tensor", lambda e, sub=sub: e.matmul(psM[0:1, 0:512], ones_col[:], sq_sb[:, sub * 512:(sub + 1) * 512], start=True, stop=True), sem=sem, wait=w())
                        first = (ch == 0 and sub == 0)
                        if first:
                            ctx.do("vector", lambda e, outc=outc: e.reduce_max(out=scal[0:1, 8 + outc:9 + outc], in_=psM[0:1, 0:512], axis=AX.X), sem=sem, wait=w())
                            ctx.do("vector", lambda e, outc=outc: e.tensor_copy(out=scal[0:1, outc:outc + 1], in_=scal[0:1, 8 + outc:9 + outc]), sem=sem, wait=w())
                        else:
                            ctx.do("vector", lambda e, outc=outc: e.reduce_max(out=scal[0:1, 8 + outc:9 + outc], in_=psM[0:1, 0:512], axis=AX.X), sem=sem, wait=w())
                            ctx.do("vector", lambda e, outc=outc: e.tensor_tensor(out=scal[0:1, outc:outc + 1], in0=scal[0:1, outc:outc + 1], in1=scal[0:1, 8 + outc:9 + outc], op=ALU.max), sem=sem, wait=w())
            ctx.do("scalar", lambda e: e.activation(out=scal[0:1, 2:3], in_=scal[0:1, 0:1], func=AF.Ln), sem=sem, wait=w())
            ctx.do("scalar", lambda e: e.activation(out=scal[0:1, 3:4], in_=scal[0:1, 1:2], func=AF.Ln), sem=sem, wait=w())
            ctx.do("vector", lambda e: e.tensor_add(out=scal[0:1, 4:5], in0=scal[0:1, 2:3], in1=scal[0:1, 3:4]), sem=sem, wait=w())
            ctx.do("scalar", lambda e: e.activation(out=scal[0:1, 5:6], in_=scal[0:1, 4:5], func=AF.Exp, scale=0.5), sem=sem, wait=w())
            ctx.do("vector", lambda e: e.tensor_scalar(out=scal[0:1, 6:7], in0=scal[0:1, 5:6], scalar1=-1.0, scalar2=None, op0=ALU.mult), sem=sem, wait=w())
            ctx.do("tensor", lambda e: e.matmul(psM[:, 0:1], ones_row[:], scal[0:1, 6:7], start=True, stop=True), sem=sem, wait=w())
            ctx.do("scalar", lambda e, h=h: e.activation(out=negb[:, h:h + 1], in_=psM[:, 0:1], func=AF.Copy), sem=sem, wait=w())

        bcast_f = sq_sb[:, 0:1024].bitcast(F32)
        sstS = io["sst"]
        outdn = []
        for h in range(2):
            for qc in range(NQC):
                kvt = 4 * (qc + 1)
                for pair in range(kvt // 2):
                    k0 = 2 * pair
                    chain("tensor", [
                        (lambda e, h=h, qc=qc, half=half: e.matmul(psS[:, half * 512:(half + 1) * 512], k_fm[:, h * S + (k0 + half) * 128: h * S + (k0 + half) * 128 + 128], q_fm[:, h * S + qc * 512: h * S + qc * 512 + 512], start=True, stop=True))
                        for half in range(2)])
                    ctx.do("scalar", lambda e, h=h: e.activation(out=pt_sb[:], in_=psS[:], func=AF.Exp, bias=negb[:, h:h + 1]), sem=sem, wait=w())
                    for half in range(2):
                        kt = k0 + half
                        if kt * 128 + 127 >= qc * 512:
                            ctx.do("gpsimd", lambda e, kt=kt, qc=qc, half=half: e.affine_select(
                                out=pt_sb[:, half * 512:(half + 1) * 512], in_=pt_sb[:, half * 512:(half + 1) * 512],
                                compare_op=ALU.is_ge, fill=0.0,
                                base=qc * 512 - kt * 128, pattern=[[1, 512]], channel_multiplier=-1), sem=sem, wait=w())
                    av = []
                    for half in range(2):
                        kt = k0 + half
                        av.append((lambda e, h=h, kt=kt, half=half, kvt=kvt: e.matmul(
                            psV[:, 0:512],
                            v_sb[:, kt * 260 + h * 129: kt * 260 + h * 129 + 128],
                            pt_sb[:, half * 512:(half + 1) * 512],
                            start=(kt == 0), stop=(kt == kvt - 1))))
                        av.append((lambda e, kt=kt, half=half, kvt=kvt: e.matmul(
                            psM[0:1, 0:512],
                            ones_col[:],
                            pt_sb[:, half * 512:(half + 1) * 512],
                            start=(kt == 0), stop=(kt == kvt - 1))))
                    chain("tensor", av)
                    pq_step()
                ctx.do("vector", lambda e: e.reciprocal(out=scal2[0:1, :], in_=psM[0:1, 0:512]), sem=sem, wait=w())
                ctx.do("tensor", lambda e: e.matmul(psK2[:, 0:512], ones_row[:], scal2[0:1, :], start=True, stop=True), sem=sem, wait=w())
                ctx.do("scalar", lambda e: e.activation(out=bcast_f[:], in_=psK2[:, 0:512], func=AF.Copy), sem=sem, wait=w())
                t = h * NQC + qc
                wt_a = w() if t < 2 else (w() + [(sstS, outdn[t - 2])])
                ctx.do("vector", lambda e, t=t: e.tensor_tensor(out=atile2[t % 2][:], in0=psV[:, 0:512], in1=bcast_f[:], op=ALU.mult), sem=sem, wait=wt_a)
                wt_d = w() + ([] if t == 0 else [(sstS, outdn[t - 1])])
                outdn.append(ctx.do("sync", lambda e, h=h, qc=qc, t=t: e.dma_start(out=attn_d[h * 128:(h + 1) * 128, qc * 512:(qc + 1) * 512], in_=atile2[t % 2][:]), sem=sstS, inc=16, wait=wt_d))

        if pq_on:
            while pq_st["i"] < len(pq_rounds):
                pq_step()
            for en in ("gpsimd", "sync", "vector"):
                ctx.wait(en, io["pq_out"], 16 * len(pq_rounds))
        for en in ("gpsimd", "sync"):
            ctx.wait(en, sstS, ctx.val(sstS))
        ctx.do("gpsimd", lambda e: e.nop(), sem=sem, wait=w())
    return emit


def build_u2():
    nc = bass.Bass("TRN2", target_bir_lowering=False)
    io = {
        "lnx": nc.dram_tensor("lnx", [D, S], BF16, kind="ExternalInput").ap(),
        "wqT": nc.dram_tensor("wqT", [D, HPC * HD], F32, kind="ExternalInput").ap(),
        "wkT": nc.dram_tensor("wkT", [D, HPC * HD], F32, kind="ExternalInput").ap(),
        "wvT": nc.dram_tensor("wvT", [D, HPC * HD], F32, kind="ExternalInput").ap(),
        "bq": nc.dram_tensor("bq", [HPC * HD], F32, kind="ExternalInput").ap(),
        "bk": nc.dram_tensor("bk", [HPC * HD], F32, kind="ExternalInput").ap(),
        "bv": nc.dram_tensor("bv", [HPC * HD], F32, kind="ExternalInput").ap(),
        "attn_c": nc.dram_tensor("attn", [HPC * HD, S], BF16, kind="ExternalOutput").ap(),
    }
    io["lnx_tile"] = lambda d, qc: io["lnx"][d * 128:(d + 1) * 128, qc * 512:(qc + 1) * 512]
    st = contextlib.ExitStack()
    io["PSA"] = st.enter_context(nc.psum_tensor("PSA", [128, 2048], F32))
    io["PSB"] = st.enter_context(nc.psum_tensor("PSB", [128, 2048], F32))
    io["sst"] = st.enter_context(nc.semaphore("sst"))
    ab = Arena(nc)
    run_phases(nc, [make_u2(nc, io, ab)])
    st.close()
    return nc


def make_u3(nc, io, ab):
    ab.reset("u3_")
    x_d = io["x0"]
    attn_tile = io.get("attn_tile")
    woT_d, bo_d = io["woT"], io["bo"]
    b1_d, b2_d = io["b1"], io["b2"]
    w1q_d, w2q_d = io["w1q"], io["w2q"]
    x2_d = io["x2"]
    PSA, PSB = io["PSA"], io["PSB"]
    sb = ab.sb
    x_fm = sb("x_fm", [128, ND * TOK], F32)
    A = sb("A", [128, ND * TOK], BF16)
    B = sb("B", [128, 32768], BF16)
    rbuf = [sb(f"rbuf{i}", [128, 2048], BF16) for i in range(2)]
    wf_sb = sb("wf_sb", [128, 1024], F32)
    bcol = sb("bcol", [128, 96], F32)
    if "wgq" in io:
        cin = [sb(f"cin{i}", [128, 512], I32) for i in range(2)]
        cout = [sb(f"cout{i}", [128, 512], BF16) for i in range(2)]
    hp = ln_helpers(ab, PSB)
    tf_sb = hp["tf"]
    psC = PSA[:, 0:512]

    def emit(ctx, sem):
        if True:
            w = lambda: [(sem, ctx.val(sem))]

            def chain(which, fns, inc_last=1, extra=()):
                if ctx.en == which:
                    ctx.eng.wait_ge(sem, ctx.val(sem))
                    for (s2, v2) in extra:
                        ctx.eng.wait_ge(s2, v2)
                    for fn in fns:
                        ins = fn(ctx.eng)
                    ins.then_inc(sem, inc_last)
                ctx._bump(sem, inc_last)

            cq_on = "wgq" in io
            if cq_on:
                pq_in, pq_v, pq_out = io["pq_in"], io["pq_v"], io["pq_out"]
                CQ = []
                for (srcT, dstT, rows, cols) in ((io["wgT"], io["wgq"], D, HID1), (io["wuT"], io["wuq"], D, HID1), (io["wdT"], io["wdq"], HID1, D)):
                    for r in range(rows // 128):
                        for c in range(cols // 512):
                            CQ.append((srcT[r * 128:(r + 1) * 128, c * 512:(c + 1) * 512], dstT[r * 128:(r + 1) * 128, c * 512:(c + 1) * 512]))
                cq_bases = {}
                cq_st = {"i": 0}

                def cq_step():
                    i = cq_st["i"]
                    if i >= len(CQ):
                        return
                    if i == 0:
                        cq_bases["in"] = ctx.val(pq_in)
                        cq_bases["v"] = ctx.val(pq_v)
                        cq_bases["out"] = ctx.val(pq_out)
                    bi, bv, bo = cq_bases["in"], cq_bases["v"], cq_bases["out"]
                    src, dst = CQ[i]
                    wt_in = ([] if i < 2 else [(pq_v, bv + (i - 1))]) + ([] if i == 0 else [(pq_in, bi + 16 * i)])
                    ctx.do("gpsimd", lambda e, src=src, i=i: e.dma_start(out=cin[i % 2][:], in_=src), sem=pq_in, inc=16, wait=wt_in)
                    wt_c = [(pq_in, bi + 16 * (i + 1))] + ([] if i < 2 else [(pq_out, bo + 16 * (i - 1))])
                    ctx.do("scalar", lambda e, i=i: e.activation(out=cout[i % 2][:], in_=cin[i % 2][:], func=AF.Copy, bias=-128.0), sem=pq_v, inc=1, wait=wt_c)
                    wt_o = [(pq_v, bv + i + 1)] + ([] if i == 0 else [(pq_out, bo + 16 * i)])
                    ctx.do("sync", lambda e, dst=dst, i=i: e.dma_start(out=dst, in_=cout[i % 2][:]), sem=pq_out, inc=16, wait=wt_o)
                    cq_st["i"] += 1
            else:
                def cq_step():
                    return

            ctx.do("gpsimd", lambda e: e.memset(hp["ones_col_b"][:], 1.0), sem=sem, wait=w())
            ctx.do("gpsimd", lambda e: e.memset(hp["ones_row"][:], 1.0), sem=sem, wait=w())
            ctx.burst("sync", [
                (lambda e, d=d: e.dma_start(out=x_fm[:, d * TOK:(d + 1) * TOK], in_=x_d[d * 128:(d + 1) * 128, :]))
                for d in range(ND)], sem)
            if io.get("attn_dyn") is not None:
                if ctx.en == "sync":
                    ctx.eng.wait_ge(sem, ctx.val(sem))
                    ins = io["attn_dyn"](ctx.eng, A)
                    ins.then_inc(sem, 16)
                ctx._bump(sem, 16)
            else:
                for d in range(ND):
                    ctx.do("sync", lambda e, d=d: e.dma_start(out=A[:, d * TOK:(d + 1) * TOK], in_=attn_tile(d)), sem=sem, inc=16, wait=w())
            ctx.burst("sync", [
                (lambda e, d=d: e.dma_start(out=bcol[:, d:d + 1], in_=bo_d[d * 128:(d + 1) * 128].unsqueeze(1)))
                for d in range(ND)] + [
                (lambda e, d=d: e.dma_start(out=bcol[:, 16 + d:17 + d], in_=b2_d[d * 128:(d + 1) * 128].unsqueeze(1)))
                for d in range(ND)] + [
                (lambda e, ff=ff: e.dma_start(out=bcol[:, 32 + ff:33 + ff], in_=b1_d[ff * 128:(ff + 1) * 128].unsqueeze(1)))
                for ff in range(64)], sem)
            ctx.burst("gpsimd", [
                (lambda e, dp=dp: e.dma_start(out=B[:, dp * 2048:(dp + 1) * 2048], in_=woT_d[dp * 128:(dp + 1) * 128, :]))
                for dp in range(ND)], sem)

            for ft in range(ND):
                for tc in range(2):
                    chain("tensor", [
                        (lambda e, dp=dp, ft=ft, tc=tc: e.matmul(PSA[:, tc * 512:(tc + 1) * 512], B[:, dp * 2048 + ft * 128: dp * 2048 + ft * 128 + 128], A[:, dp * TOK + tc * 512: dp * TOK + tc * 512 + 512], start=(dp == 0), stop=(dp == ND - 1)))
                        for dp in range(ND)])
                    cq_step()
                ctx.do("scalar", lambda e, ft=ft: e.activation(out=wf_sb[:, 0:1024], in_=PSA[:, 0:1024], func=AF.Identity, scale=1.0, bias=bcol[:, ft:ft + 1]), sem=sem, wait=w())
                ctx.do("vector", lambda e, ft=ft: e.tensor_add(out=x_fm[:, ft * TOK: ft * TOK + TOK], in0=x_fm[:, ft * TOK: ft * TOK + TOK], in1=wf_sb[:, 0:1024]), sem=sem, wait=w())

            emit_ln_fm2(ctx, sem, lambda d, tc: x_fm[:, d * TOK + tc * 512: d * TOK + tc * 512 + 512], A, hp, [0, 1], TOK, TOK)

            sstS = io["sst"]
            CH = []
            for half in range(2):
                for fb in range(16):
                    for dpg in range(4):
                        CH.append((w1q_d[dpg * 512:(dpg + 1) * 512, fb * 512:(fb + 1) * 512], 512))
                for ft in range(ND):
                    for sub in range(4):
                        CH.append((w2q_d[sub * 2048:(sub + 1) * 2048, ft * 128:(ft + 1) * 128], 128))
            ring = {"g": 0, "u": 0, "dn": [], "cons": []}

            def pump():
                g = ring["g"]
                if g >= len(CH):
                    return
                src, cdim = CH[g]
                wt = (w() if g < 2 else [(sem, ring["cons"][g - 2])]) + ([] if g == 0 else [(sstS, ring["dn"][g - 1])])
                ring["dn"].append(ctx.do("gpsimd", lambda e, src=src, cdim=cdim, g=g: e.dma_start(
                    out=rbuf[g % 2][:].rearrange("p (a c) -> p a c", c=cdim),
                    in_=src.rearrange("(a p) c -> p a c", p=128)), sem=sstS, inc=16, wait=wt))
                ring["g"] += 1

            def use():
                g = ring["u"]
                return rbuf[g % 2], [(sstS, ring["dn"][g])]

            def done_use():
                ring["cons"].append(ctx.val(sem))
                ring["u"] += 1
                pump()

            pump()
            pump()
            for half in range(2):
                hoff = half * 512
                for fb in range(16):
                    for dpg in range(4):
                        rb, ex = use()
                        for fs in range(4):
                            chain("tensor", [
                                (lambda e, i=i, fs=fs, rb=rb, dpg=dpg, hoff=hoff: e.matmul(
                                    PSA[:, fs * 512:(fs + 1) * 512],
                                    rb[:, i * 512 + fs * 128: i * 512 + fs * 128 + 128],
                                    A[:, (dpg * 4 + i) * TOK + hoff: (dpg * 4 + i) * TOK + hoff + 512],
                                    start=(dpg == 0 and i == 0), stop=(dpg == 3 and i == 3)))
                                for i in range(4)], extra=(ex if fs == 0 else ()))
                        done_use()
                        cq_step()
                        cq_step()
                    for fs in range(4):
                        ff = fb * 4 + fs
                        ctx.do("scalar", lambda e, ff=ff, fs=fs: e.activation(out=B[:, ff * 512:(ff + 1) * 512], in_=PSA[:, fs * 512:(fs + 1) * 512], func=AF.Gelu, bias=bcol[:, 32 + ff:33 + ff]), sem=sem, wait=w())
                for ft in range(ND):
                    for sub in range(4):
                        rb, ex = use()
                        chain("tensor", [
                            (lambda e, fi=fi, rb=rb, sub=sub, hoff=hoff: e.matmul(
                                psC,
                                rb[:, fi * 128:(fi + 1) * 128],
                                B[:, (sub * 16 + fi) * 512:(sub * 16 + fi + 1) * 512],
                                start=(sub == 0 and fi == 0), stop=(sub == 3 and fi == 15)))
                            for fi in range(16)], extra=ex)
                        done_use()
                        cq_step()
                        cq_step()
                    ctx.do("scalar", lambda e, ft=ft: e.activation(out=tf_sb[:], in_=psC, func=AF.Identity, scale=1.0, bias=bcol[:, 16 + ft:17 + ft]), sem=sem, wait=w())
                    ctx.do("vector", lambda e, ft=ft, hoff=hoff: e.tensor_add(out=x_fm[:, ft * TOK + hoff: ft * TOK + hoff + 512], in0=x_fm[:, ft * TOK + hoff: ft * TOK + hoff + 512], in1=tf_sb[:]), sem=sem, wait=w())
                    ctx.do("sync", lambda e, ft=ft, hoff=hoff: e.dma_start(out=x2_d[ft * 128:(ft + 1) * 128, hoff:hoff + 512], in_=x_fm[:, ft * TOK + hoff: ft * TOK + hoff + 512]), sem=sem, inc=16, wait=w())
            if cq_on:
                while cq_st["i"] < len(CQ):
                    cq_step()
                for en in ("gpsimd", "sync", "scalar"):
                    ctx.wait(en, io["pq_out"], cq_bases["out"] + 16 * len(CQ))
                ctx.do("gpsimd", lambda e: e.nop(), sem=sem, wait=w())
    return emit


def build_u3():
    nc = bass.Bass("TRN2", target_bir_lowering=False)
    io = {
        "x0": nc.dram_tensor("x", [D, TOK], F32, kind="ExternalInput").ap(),
        "attn": nc.dram_tensor("attn", [D, TOK], BF16, kind="ExternalInput").ap(),
        "woT": nc.dram_tensor("woT", [D, D], F32, kind="ExternalInput").ap(),
        "bo": nc.dram_tensor("bo", [D], F32, kind="ExternalInput").ap(),
        "w1q": nc.dram_tensor("w1q", [D, DFF0], BF16, kind="ExternalInput").ap(),
        "b1": nc.dram_tensor("b1", [DFF0], F32, kind="ExternalInput").ap(),
        "w2q": nc.dram_tensor("w2q", [DFF0, D], BF16, kind="ExternalInput").ap(),
        "b2": nc.dram_tensor("b2", [D], F32, kind="ExternalInput").ap(),
        "x2": nc.dram_tensor("x2", [D, TOK], F32, kind="ExternalOutput").ap(),
    }
    io["attn_tile"] = lambda d: io["attn"][d * 128:(d + 1) * 128, :]
    st = contextlib.ExitStack()
    io["PSA"] = st.enter_context(nc.psum_tensor("PSA", [128, 2048], F32))
    io["PSB"] = st.enter_context(nc.psum_tensor("PSB", [128, 2048], F32))
    io["sst"] = st.enter_context(nc.semaphore("sst"))
    ab = Arena(nc)
    run_phases(nc, [make_u3(nc, io, ab)])
    st.close()
    return nc


def emit_ln_fm2(ctx, sem, x_src, out_fm, hp, tc_list, in_stride, out_stride, nd=ND):
    d_model = nd * 128
    w = lambda: [(sem, ctx.val(sem))]
    tb0, tb1, abc, bbc, row = hp["tb0"], hp["tb1"], hp["abc"], hp["bbc"], hp["row"]
    psA, psB = hp["psA"], hp["psB"]
    oc, orow = hp["ones_col_b"], hp["ones_row"]
    for tc in tc_list:
        for d in range(nd):
            ctx.do("scalar", lambda e, d=d, tc=tc: e.activation(out=tb0[:], in_=x_src(d, tc), func=AF.Identity), sem=sem, wait=w())
            ctx.do("scalar", lambda e, d=d, tc=tc: e.activation(out=tb1[:], in_=x_src(d, tc), func=AF.Square), sem=sem, wait=w())
            ctx.do("tensor", lambda e, d=d: e.matmul(psA[0:1, 0:512], oc[:], tb0[:], start=(d == 0), stop=(d == nd - 1)), sem=sem, wait=w())
            ctx.do("tensor", lambda e, d=d: e.matmul(psA[0:1, 512:1024], oc[:], tb1[:], start=(d == 0), stop=(d == nd - 1)), sem=sem, wait=w())
        ctx.do("vector", lambda e: e.tensor_scalar(out=row[0:1, 0:512], in0=psA[0:1, 0:512], scalar1=1.0 / d_model, scalar2=None, op0=ALU.mult), sem=sem, wait=w())
        ctx.do("vector", lambda e: e.tensor_scalar(out=row[0:1, 512:1024], in0=psA[0:1, 512:1024], scalar1=1.0 / d_model, scalar2=None, op0=ALU.mult), sem=sem, wait=w())
        ctx.do("vector", lambda e: e.tensor_tensor(out=row[0:1, 1024:1536], in0=row[0:1, 0:512], in1=row[0:1, 0:512], op=ALU.mult), sem=sem, wait=w())
        ctx.do("vector", lambda e: e.tensor_tensor(out=row[0:1, 512:1024], in0=row[0:1, 512:1024], in1=row[0:1, 1024:1536], op=ALU.subtract), sem=sem, wait=w())
        ctx.do("vector", lambda e: e.tensor_scalar(out=row[0:1, 512:1024], in0=row[0:1, 512:1024], scalar1=1e-5, scalar2=None, op0=ALU.add), sem=sem, wait=w())
        ctx.do("scalar", lambda e: e.activation(out=row[0:1, 512:1024], in_=row[0:1, 512:1024], func=AF.Ln), sem=sem, wait=w())
        ctx.do("scalar", lambda e: e.activation(out=row[0:1, 512:1024], in_=row[0:1, 512:1024], func=AF.Exp, scale=-0.5), sem=sem, wait=w())
        ctx.do("vector", lambda e: e.tensor_tensor(out=row[0:1, 1024:1536], in0=row[0:1, 0:512], in1=row[0:1, 512:1024], op=ALU.mult), sem=sem, wait=w())
        ctx.do("vector", lambda e: e.tensor_scalar(out=row[0:1, 1024:1536], in0=row[0:1, 1024:1536], scalar1=-1.0, scalar2=None, op0=ALU.mult), sem=sem, wait=w())
        ctx.do("tensor", lambda e: e.matmul(psB[:, 0:512], orow[:], row[0:1, 512:1024], start=True, stop=True), sem=sem, wait=w())
        ctx.do("scalar", lambda e: e.activation(out=abc[:], in_=psB[:, 0:512], func=AF.Copy), sem=sem, wait=w())
        ctx.do("tensor", lambda e: e.matmul(psB[:, 0:512], orow[:], row[0:1, 1024:1536], start=True, stop=True), sem=sem, wait=w())
        ctx.do("scalar", lambda e: e.activation(out=bbc[:], in_=psB[:, 0:512], func=AF.Copy), sem=sem, wait=w())
        for d in range(nd):
            ctx.do("vector", lambda e, d=d, tc=tc: e.tensor_tensor(out=hp["tf"][:], in0=x_src(d, tc), in1=abc[:], op=ALU.mult), sem=sem, wait=w())
            ctx.do("vector", lambda e, d=d, tc=tc: e.tensor_tensor(out=out_fm[:, d * out_stride + tc * 512: d * out_stride + tc * 512 + 512], in0=hp["tf"][:], in1=bbc[:], op=ALU.add), sem=sem, wait=w())


def make_u5(nc, io, ab, mode, halo_mode="compute", skip_front=False):
    ab.reset("u5_")
    x_d = io["x_m"]
    ipT_d, cw_d, cb_d, alog_d, dtb_d = io["ipT"], io["cw"], io["cb"], io["alog"], io["dtb"]
    xh_d = io.get("xh")
    if mode == "b":
        dvec_d, nw_d, opT_d = io["dvec"], io["nw"], io["opT"]
        sf_d, phiT_d, mask_d = io.get("sf"), io.get("phiT"), io["maskr"]
        ws_scr = io["ws_scr"]
        x3_d = io["x3"]
    else:
        sfin_d = io["sfin"]
        phi_d = io["phi"]
    acs_scr = io["acs_scr"]
    zmask_d = io.get("zmask")
    xssm_scr = io["xssm_scr"]
    z_scr = io.get("z_scr")
    PSA, PSB = io["PSA"], io["PSB"]
    sb = ab.sb
    A32 = sb("A32", [128, 16384], BF16)
    XCH = sb("XCH", [128, 8192], F32)
    BIG1 = sb("BIG1", [128, 4096], F32)
    BIG2 = sb("BIG2", [128, 4096], BF16)
    BIG3 = sb("BIG3", [128, 4096], BF16)
    BIG4 = sb("BIG4", [128, 4096], BF16)
    y_tile = sb("y_tile", [128, 4096], BF16)
    hp = {
        "tb0": sb("tb0", [128, 512], BF16),
        "tb1": sb("tb1", [128, 512], BF16),
        "tf": sb("tf", [128, 512], F32),
        "abc": sb("abc", [128, 512], F32),
        "bbc": sb("bbc", [128, 512], F32),
        "row": sb("row", [1, 1536], F32),
        "ones_col_b": sb("ones_col_b", [128, 1], BF16),
        "ones_row": sb("ones_row", [1, 128], F32),
    }
    identb = sb("identb", [128, 128], BF16)
    identf = sb("identf", [128, 128], F32)
    xbcr = sb("xbcr", [128, 1028], BF16)
    xbs = sb("xbs", [128, 1024], BF16)
    BC = sb("BC", [128, 2 * TOK], BF16)
    dt_fm = sb("dt_fm", [64, TOK], F32)
    adisc = sb("adisc", [64, TOK], F32)
    acs = sb("acs", [64, TOK], F32)
    cw_sb = sb("cw_sb", [128, 136], F32)
    cb_sb = sb("cb_sb", [128, 34], F32)
    small = sb("small", [64, 16], F32)
    acsT = sb("acsT", [64, 64], F32)
    eacsT = sb("eacsT", [64, 64], F32)
    ad63 = sb("ad63", [64, 64], F32)
    w_sh = sb("w_sh", [64, 64], F32)
    dtT = sb("dtT", [64, 64], F32)
    btm = sb("btm", [64, 128], BF16)
    gt = sb("gt", [64, 64], BF16)
    er63 = sb("er63", [1, 64], F32)
    E63 = sb("E63", [128, 64], F32)
    D64 = sb("D64", [128, 64], F32)
    S_run = sb("S_run", [128, DIN], F32)
    st_tmp = sb("st_tmp", [128, 2048], F32)
    syb = sb("syb", [128, DIN], BF16)
    x2s = sb("x2s", [128, 512], F32)
    lnxh = sb("lnxh", [128, ND * 4], BF16)
    xht = sb("xht", [128, ND * 4], F32)
    rs_col = sb("rs_col", [128, 2], F32)
    ph8 = sb("ph8", [64, 32], F32)
    ws8 = sb("ws8", [8, 64], F32)
    wsrow = sb("wsrow", [1, 512], F32)
    zt = sb("zt", [128, 1024], BF16) if mode == "b" else None
    stag4 = sb("stag4", [128, 512], BF16)
    rawf3 = sb("rawf3", [128, 34 * 4], BF16)
    hal_all = sb("hal_all", [128, 34 * 4], BF16)
    zmask_sb = sb("zmask_sb", [128, 1], F32)
    hal3 = sb("hal3", [128, 8], BF16)

    psC = PSA[:, 0:512]
    psA_ln = PSB[0:1, 0:1024]
    psB_ln = PSB[:, 1024:1536]
    psT = PSB[:, 512:576].bitcast(BF16)
    psF = PSB[:, 576:704]
    hp["psA"] = psA_ln
    hp["psB"] = psB_ln

    def emit(ctx, sem):
        if True:
            w = lambda: [(sem, ctx.val(sem))]

            def chain(which, fns, inc_last=1):
                if ctx.en == which:
                    ctx.eng.wait_ge(sem, ctx.val(sem))
                    for fn in fns:
                        ins = fn(ctx.eng)
                    ins.then_inc(sem, inc_last)
                ctx._bump(sem, inc_last)

            def do(which, f2):
                ctx.do(which, f2, sem=sem, wait=w())

            def dma(f2, which="sync"):
                ctx.do(which, f2, sem=sem, inc=16, wait=w())

            do("gpsimd", lambda e: e.memset(hp["ones_col_b"][:], 1.0))
            do("gpsimd", lambda e: e.memset(hp["ones_row"][:], 1.0))
            do("gpsimd", lambda e: e.memset(identb[:], 1.0))
            do("gpsimd", lambda e: e.affine_select(out=identb[:], in_=identb[:], compare_op=ALU.is_equal, fill=0.0, base=0, pattern=[[-1, 128]], channel_multiplier=1))
            do("gpsimd", lambda e: e.memset(identf[:], 1.0))
            do("gpsimd", lambda e: e.affine_select(out=identf[:], in_=identf[:], compare_op=ALU.is_equal, fill=0.0, base=0, pattern=[[-1, 128]], channel_multiplier=1))
            for ct in range(34):
                dma(lambda e, ct=ct: e.dma_start(out=cw_sb[:, ct * 4:(ct + 1) * 4], in_=cw_d[ct * 128:(ct + 1) * 128, :]))
                dma(lambda e, ct=ct: e.dma_start(out=cb_sb[:, ct:ct + 1], in_=cb_d[ct * 128:(ct + 1) * 128].unsqueeze(1)))
            if halo_mode == "gather":
                dma(lambda e: e.dma_start(out=zmask_sb[:, :], in_=zmask_d[:, :]))
            dma(lambda e: e.dma_start(out=small[:, 0:1], in_=alog_d[:].unsqueeze(1)))
            dma(lambda e: e.dma_start(out=small[:, 2:3], in_=dtb_d[:].unsqueeze(1)))
            do("scalar", lambda e: e.activation(out=small[:, 1:2], in_=small[:, 0:1], func=AF.Exp))
            do("vector", lambda e: e.tensor_scalar(out=small[:, 1:2], in0=small[:, 1:2], scalar1=-1.0, scalar2=None, op0=ALU.mult))

            if mode == "b":
                if "phi_col" in io:
                    for j in range(NCORES):
                        dma(lambda e, j=j: e.dma_start(out=ph8[:, j:j + 1], in_=io["phi_col"](j)))
                else:
                    dma(lambda e: e.dma_start(out=ph8[:, 0:8], in_=phiT_d[:, :]))
                dma(lambda e: e.dma_start(out=ph8[:, 8:16], in_=mask_d[:, :]))
                do("vector", lambda e: e.tensor_tensor(out=ph8[:, 16:24], in0=ph8[:, 0:8], in1=ph8[:, 8:16], op=ALU.mult))
                do("gpsimd", lambda e: e.memset(ph8[:, 24:32], 0.0))
                for j in range(6, -1, -1):
                    do("vector", lambda e, j=j: e.tensor_tensor(out=ph8[:, 24 + j:25 + j], in0=ph8[:, 25 + j:26 + j], in1=ph8[:, 17 + j:18 + j], op=ALU.add))
                do("scalar", lambda e: e.activation(out=ph8[:, 24:32], in_=ph8[:, 24:32], func=AF.Exp))
                do("vector", lambda e: e.tensor_tensor(out=ph8[:, 24:32], in0=ph8[:, 24:32], in1=ph8[:, 8:16], op=ALU.mult))
                do("tensor", lambda e: e.transpose(psF[0:8, 0:64], ph8[0:64, 24:32], identf[0:64, 0:64]))
                do("scalar", lambda e: e.activation(out=ws8[:, :], in_=psF[0:8, 0:64], func=AF.Copy))
                dma(lambda e: e.dma_start(out=ws_scr[:, :], in_=ws8[:, :]))
                dma(lambda e: e.dma_start(out=er63[0:1, :].broadcast_to([1, 64]) if False else ph8[0:1, 0:0] if False else wsrow[0:1, :], in_=ws_scr.rearrange("a b -> (a b)").unsqueeze(0)))
                do("gpsimd", lambda e: e.memset(S_run[:], 0.0))
                sfb = [BIG1[:, 0:4096], XCH[:, 0:4096]]
                sf_src = lambda j: (io["sf_row"](j) if "sf_row" in io else sf_d[j * 128:(j + 1) * 128, :])
                sdn = [None] * NCORES
                cons = [None] * NCORES
                sdn[0] = ctx.do("sync", lambda e: e.dma_start(out=sfb[0], in_=sf_src(0)), sem=io["sst"], inc=16, wait=w())
                for j in range(NCORES):
                    if j + 1 < NCORES:
                        wt = (w() if j == 0 else [(sem, cons[j - 1])]) + [(io["sst"], sdn[j])]
                        sdn[j + 1] = ctx.do("sync", lambda e, j=j: e.dma_start(out=sfb[(j + 1) % 2], in_=sf_src(j + 1)), sem=io["sst"], inc=16, wait=wt)
                    do("tensor", lambda e, j=j: e.matmul(psC[:, 0:64], hp["ones_row"][:], wsrow[0:1, j * 64:(j + 1) * 64], start=True, stop=True))
                    do("scalar", lambda e: e.activation(out=E63[:], in_=psC[:, 0:64], func=AF.Copy))
                    for hh in range(2):
                        ctx.do("vector", lambda e, hh=hh, j=j: e.tensor_tensor(out=st_tmp[:], in0=sfb[j % 2][:, hh * 2048:(hh + 1) * 2048], in1=E63[:, hh * 32:(hh + 1) * 32].unsqueeze(2).broadcast_to([128, 32, 64]), op=ALU.mult), sem=sem, wait=w() + [(io["sst"], sdn[j])])
                        do("vector", lambda e, hh=hh: e.tensor_tensor(out=S_run[:, hh * 2048:(hh + 1) * 2048], in0=S_run[:, hh * 2048:(hh + 1) * 2048], in1=st_tmp[:], op=ALU.add))
                    cons[j] = ctx.val(sem)
            else:
                do("gpsimd", lambda e: e.memset(S_run[:], 0.0))

            for tc in range(2 if not skip_front else 0):
                for d in range(ND):
                    dma(lambda e, d=d, tc=tc: e.dma_start(out=XCH[:, d * 512:(d + 1) * 512], in_=x_d[d * 128:(d + 1) * 128, tc * 512:(tc + 1) * 512]))
                emit_ln_fm2(ctx, sem, lambda d, tcc: XCH[:, d * 512:(d + 1) * 512], A32, hp, [tc], 512, 1024)
            for d in range(ND if (halo_mode == "compute" and not skip_front) else 0):
                dma(lambda e, d=d: e.dma_start(out=xht[:, d * 4:d * 4 + 3], in_=xh_d[d * 128:(d + 1) * 128, :]))
            for d in range(ND if halo_mode == "compute" else 0):
                do("scalar", lambda e, d=d: e.activation(out=hp["tb0"][:, 0:3], in_=xht[:, d * 4:d * 4 + 3], func=AF.Identity))
                do("scalar", lambda e, d=d: e.activation(out=hp["tb1"][:, 0:3], in_=xht[:, d * 4:d * 4 + 3], func=AF.Square))
                do("tensor", lambda e, d=d: e.matmul(psA_ln[0:1, 0:3], hp["ones_col_b"][:], hp["tb0"][:, 0:3], start=(d == 0), stop=(d == ND - 1)))
                do("tensor", lambda e, d=d: e.matmul(psA_ln[0:1, 4:7], hp["ones_col_b"][:], hp["tb1"][:, 0:3], start=(d == 0), stop=(d == ND - 1)))
            R = hp["row"]
            if halo_mode == "compute":
              do("vector", lambda e: e.tensor_scalar(out=R[0:1, 0:3], in0=psA_ln[0:1, 0:3], scalar1=1.0 / D, scalar2=None, op0=ALU.mult))
              do("vector", lambda e: e.tensor_scalar(out=R[0:1, 4:7], in0=psA_ln[0:1, 4:7], scalar1=1.0 / D, scalar2=None, op0=ALU.mult))
              do("vector", lambda e: e.tensor_tensor(out=R[0:1, 8:11], in0=R[0:1, 0:3], in1=R[0:1, 0:3], op=ALU.mult))
              do("vector", lambda e: e.tensor_tensor(out=R[0:1, 4:7], in0=R[0:1, 4:7], in1=R[0:1, 8:11], op=ALU.subtract))
              do("vector", lambda e: e.tensor_scalar(out=R[0:1, 4:7], in0=R[0:1, 4:7], scalar1=1e-5, scalar2=None, op0=ALU.add))
              do("scalar", lambda e: e.activation(out=R[0:1, 4:7], in_=R[0:1, 4:7], func=AF.Ln))
              do("scalar", lambda e: e.activation(out=R[0:1, 4:7], in_=R[0:1, 4:7], func=AF.Exp, scale=-0.5))
              do("vector", lambda e: e.tensor_tensor(out=R[0:1, 8:11], in0=R[0:1, 0:3], in1=R[0:1, 4:7], op=ALU.mult))
              do("vector", lambda e: e.tensor_scalar(out=R[0:1, 8:11], in0=R[0:1, 8:11], scalar1=-1.0, scalar2=None, op0=ALU.mult))
              do("tensor", lambda e: e.matmul(psB_ln[:, 0:3], hp["ones_row"][:], R[0:1, 4:7], start=True, stop=True))
              do("scalar", lambda e: e.activation(out=hp["abc"][:, 0:3], in_=psB_ln[:, 0:3], func=AF.Copy))
              do("tensor", lambda e: e.matmul(psB_ln[:, 0:3], hp["ones_row"][:], R[0:1, 8:11], start=True, stop=True))
              do("scalar", lambda e: e.activation(out=hp["bbc"][:, 0:3], in_=psB_ln[:, 0:3], func=AF.Copy))
              for d in range(ND):
                  do("vector", lambda e, d=d: e.tensor_tensor(out=hp["tf"][:, 0:3], in0=xht[:, d * 4:d * 4 + 3], in1=hp["abc"][:, 0:3], op=ALU.mult))
                  do("vector", lambda e, d=d: e.tensor_tensor(out=lnxh[:, d * 4:d * 4 + 3], in0=hp["tf"][:, 0:3], in1=hp["bbc"][:, 0:3], op=ALU.add))

            sst = io["sst"]
            sring = {"n": 0, "base": ctx.val(sst)}
            ot_list = (list(range(32)) if mode == "b" else []) + ([] if skip_front else list(range(32, 67)))
            for oc in ot_list:
                ncols = 64 if oc == 66 else 128
                dma(lambda e, oc=oc, ncols=ncols: e.dma_start(
                    out=BIG2[:, 0:16 * ncols].rearrange("p (a c) -> p a c", c=ncols),
                    in_=ipT_d[:, oc * 128: oc * 128 + ncols].rearrange("(a p) c -> p a c", p=128)), which="gpsimd")
                for tc in range(2):
                    chain("tensor", [
                        (lambda e, d=d, tc=tc, ncols=ncols: e.matmul(PSA[0:ncols, tc * 512:tc * 512 + 512], BIG2[:, d * ncols:(d + 1) * ncols], A32[:, d * 1024 + tc * 512: d * 1024 + tc * 512 + 512], start=(d == 0), stop=(d == ND - 1)))
                        for d in range(ND)])
                if oc < 32:
                    do("scalar", lambda e: e.activation(out=zt[:, 0:512], in_=PSA[:, 0:512], func=AF.Silu))
                    do("scalar", lambda e: e.activation(out=zt[:, 512:1024], in_=PSA[:, 512:1024], func=AF.Silu))
                    for tt in range(8):
                        do("tensor", lambda e, tt=tt: e.transpose(psT[:], zt[:, tt * 128:(tt + 1) * 128], identb[:]))
                        slot = sring["n"] % 4
                        wv = [(sem, ctx.val(sem))]
                        if sring["n"] >= 4:
                            wv.append((sst, sring["base"] + (sring["n"] - 3) * 16))
                        ctx.do("vector", lambda e, slot=slot: e.tensor_copy(out=stag4[:, slot * 128:(slot + 1) * 128], in_=psT[:]), sem=sem, wait=wv)
                        ctx.do("gpsimd", lambda e, tt=tt, oc=oc, slot=slot: e.dma_start(out=z_scr[tt * 128:(tt + 1) * 128, oc * 128:(oc + 1) * 128], in_=stag4[:, slot * 128:(slot + 1) * 128]), sem=sst, inc=16, wait=[(sem, ctx.val(sem))])
                        sring["n"] += 1
                elif oc < 66:
                    ct = oc - 32
                    if halo_mode == "compute":
                        chain("tensor", [
                            (lambda e, d=d: e.matmul(PSB[0:128, 0:3], BIG2[:, d * 128:(d + 1) * 128], lnxh[:, d * 4:d * 4 + 3], start=(d == 0), stop=(d == ND - 1)))
                            for d in range(ND)])
                        do("scalar", lambda e: e.activation(out=xbcr[:, 0:3], in_=PSB[0:128, 0:3], func=AF.Copy))
                    else:
                        do("gpsimd", lambda e: e.memset(xbcr[:, 0:3], 0.0))
                    do("scalar", lambda e: e.activation(out=xbcr[:, 3:515], in_=PSA[:, 0:512], func=AF.Copy))
                    do("scalar", lambda e: e.activation(out=xbcr[:, 515:1027], in_=PSA[:, 512:1024], func=AF.Copy))
                    if halo_mode == "gather":
                        dma(lambda e, ct=ct: e.dma_start(out=io["xbce_c"][ct * 128:(ct + 1) * 128, 0:3], in_=xbcr[:, 1024:1027]))
                        do("vector", lambda e, ct=ct: e.tensor_copy(out=rawf3[:, ct * 4:ct * 4 + 3], in_=xbcr[:, 3:6]))
                    do("vector", lambda e, ct=ct: e.tensor_scalar(out=xbs[:], in0=xbcr[:, 0:1024], scalar1=cw_sb[:, ct * 4:ct * 4 + 1], scalar2=None, op0=ALU.mult))
                    for j in range(1, 4):
                        do("vector", lambda e, ct=ct, j=j: e.scalar_tensor_tensor(out=xbs[:], in0=xbcr[:, j:j + 1024], scalar=cw_sb[:, ct * 4 + j:ct * 4 + j + 1], in1=xbs[:], op0=ALU.mult, op1=ALU.add))
                    do("scalar", lambda e, ct=ct: e.activation(out=xbs[:], in_=xbs[:], func=AF.Silu, bias=cb_sb[:, ct:ct + 1]))
                    if ct < 32:
                        for tt in range(8):
                            do("tensor", lambda e, tt=tt: e.transpose(psT[:], xbs[:, tt * 128:(tt + 1) * 128], identb[:]))
                            slot = sring["n"] % 4
                            wv = [(sem, ctx.val(sem))]
                            if sring["n"] >= 4:
                                wv.append((sst, sring["base"] + (sring["n"] - 3) * 16))
                            ctx.do("vector", lambda e, slot=slot: e.tensor_copy(out=stag4[:, slot * 128:(slot + 1) * 128], in_=psT[:]), sem=sem, wait=wv)
                            ctx.do("gpsimd", lambda e, tt=tt, ct=ct, slot=slot: e.dma_start(out=xssm_scr[tt * 128:(tt + 1) * 128, ct * 128:(ct + 1) * 128], in_=stag4[:, slot * 128:(slot + 1) * 128]), sem=sst, inc=16, wait=[(sem, ctx.val(sem))])
                            sring["n"] += 1
                    else:
                        do("vector", lambda e, ct=ct: e.tensor_copy(out=BC[:, (ct - 32) * 1024:(ct - 31) * 1024], in_=xbs[:]))
                else:
                    do("scalar", lambda e: e.activation(out=dt_fm[:, 0:512], in_=PSA[0:64, 0:512], func=AF.Identity, bias=small[:, 2:3]))
                    do("scalar", lambda e: e.activation(out=dt_fm[:, 512:1024], in_=PSA[0:64, 512:1024], func=AF.Identity, bias=small[:, 2:3]))
                    do("scalar", lambda e: e.activation(out=adisc[:, 0:1024], in_=dt_fm[:], func=AF.Abs))
                    do("scalar", lambda e: e.activation(out=adisc[:, 0:1024], in_=adisc[:, 0:1024], func=AF.Exp, scale=-1.0))
                    do("vector", lambda e: e.tensor_scalar(out=adisc[:, 0:1024], in0=adisc[:, 0:1024], scalar1=1.0, scalar2=None, op0=ALU.add))
                    do("scalar", lambda e: e.activation(out=adisc[:, 0:1024], in_=adisc[:, 0:1024], func=AF.Ln))
                    do("vector", lambda e: e.tensor_scalar(out=dt_fm[:], in0=dt_fm[:], scalar1=0.0, scalar2=None, op0=ALU.max))
                    do("vector", lambda e: e.tensor_tensor(out=dt_fm[:], in0=dt_fm[:], in1=adisc[:, 0:1024], op=ALU.add))
            if halo_mode == "gather" and not skip_front:
                if io.get("mid_ag3") is not None and mode == "a":
                    io["mid_ag3"](ctx, sem)
                xbce_view = io["xbce_all"].rearrange("(j ct p) c -> j p ct c", p=128, ct=34)
                if ctx.en == "sync":
                    ctx.eng.wait_ge(sem, ctx.val(sem))
                    pprev = ctx.eng.snap((ctx.eng.partition_id() + 7) % 8)
                    ins = ctx.eng.dma_start(out=hal_all[:].rearrange("p (ct c) -> p ct c", c=4), in_=xbce_view[bass.ds(pprev, 1)].squeeze(0))
                    ins.then_inc(sem, 16)
                ctx._bump(sem, 16)
                do("vector", lambda e: e.tensor_scalar(out=hal_all[:], in0=hal_all[:], scalar1=zmask_sb[:, 0:1], scalar2=None, op0=ALU.mult))
                for ct in range(34):
                    do("vector", lambda e, ct=ct: e.tensor_copy(out=hal3[:, 0:3], in_=hal_all[:, ct * 4:ct * 4 + 3]))
                    do("vector", lambda e, ct=ct: e.tensor_copy(out=hal3[:, 3:6], in_=rawf3[:, ct * 4:ct * 4 + 3]))
                    do("vector", lambda e, ct=ct: e.tensor_scalar(out=xbs[:, 0:3], in0=hal3[:, 0:3], scalar1=cw_sb[:, ct * 4:ct * 4 + 1], scalar2=None, op0=ALU.mult))
                    for j in range(1, 4):
                        do("vector", lambda e, ct=ct, j=j: e.scalar_tensor_tensor(out=xbs[:, 0:3], in0=hal3[:, j:j + 3], scalar=cw_sb[:, ct * 4 + j:ct * 4 + j + 1], in1=xbs[:, 0:3], op0=ALU.mult, op1=ALU.add))
                    do("scalar", lambda e, ct=ct: e.activation(out=xbs[:, 0:3], in_=xbs[:, 0:3], func=AF.Silu, bias=cb_sb[:, ct:ct + 1]))
                    if ct < 32:
                        do("tensor", lambda e: e.transpose(psT[0:3, 0:128], xbs[:, 0:3], identb[:]))
                        do("scalar", lambda e: e.activation(out=xbcr[0:3, 0:128], in_=psT[0:3, 0:128], func=AF.Copy))
                        dma(lambda e, ct=ct: e.dma_start(out=xssm_scr[0:3, ct * 128:(ct + 1) * 128], in_=xbcr[0:3, 0:128]))
                    else:
                        do("vector", lambda e, ct=ct: e.tensor_copy(out=BC[:, (ct - 32) * 1024:(ct - 32) * 1024 + 3], in_=xbs[:, 0:3]))
            if not skip_front:
                do("vector", lambda e: e.tensor_scalar(out=adisc[:], in0=dt_fm[:], scalar1=small[:, 1:2], scalar2=None, op0=ALU.mult))
                for c in range(NCHUNK):
                    do("vector", lambda e, c=c: e.tensor_tensor_scan(out=acs[:, c * 64:(c + 1) * 64], data0=adisc[:, c * 64:(c + 1) * 64], data1=adisc[:, c * 64:(c + 1) * 64], initial=0.0, op0=ALU.add, op1=ALU.bypass))
            if mode == "a":
                do("vector", lambda e: e.reduce_sum(out=small[:, 3:4], in_=adisc[:], axis=AX.X))
                if io.get("phi_is_row"):
                    dma(lambda e: e.dma_start(out=phi_d[0:1, 0:64], in_=small[:, 3:4].transpose([1, 0]) if False else small[:, 3:4]))
                else:
                    dma(lambda e: e.dma_start(out=phi_d[:, :], in_=small[:, 3:4]))
            if mode == "b":
                dma(lambda e: e.dma_start(out=er63[0:1, :], in_=dvec_d[:].unsqueeze(0)))
                do("tensor", lambda e: e.matmul(psC[:, 0:64], hp["ones_row"][:], er63[0:1, :], start=True, stop=True))
                do("scalar", lambda e: e.activation(out=D64[:], in_=psC[:, 0:64], func=AF.Copy))

            ctx.wait("sync", sst, ctx.val(sst))
            ctx.wait("gpsimd", sst, ctx.val(sst))
            for c in range(NCHUNK):
                ac = acs[:, c * 64:(c + 1) * 64]
                dtc = dt_fm[:, c * 64:(c + 1) * 64]
                dma(lambda e, c=c: e.dma_start(out=acs_scr[:, :], in_=ac))
                do("tensor", lambda e: e.transpose(psF[0:64, 0:64], ac, identf[0:64, 0:64]))
                do("scalar", lambda e: e.activation(out=acsT[:], in_=psF[0:64, 0:64], func=AF.Copy))
                do("tensor", lambda e: e.transpose(psF[0:64, 0:64], dtc, identf[0:64, 0:64]))
                do("scalar", lambda e: e.activation(out=dtT[:], in_=psF[0:64, 0:64], func=AF.Copy))
                if mode == "b":
                    do("scalar", lambda e: e.activation(out=eacsT[:], in_=acsT[:], func=AF.Exp))
                do("vector", lambda e: e.tensor_tensor(out=ad63[:], in0=ac[:, 63:64].broadcast_to([64, 64]), in1=ac, op=ALU.subtract))
                do("scalar", lambda e: e.activation(out=ad63[:], in_=ad63[:], func=AF.Exp))
                do("vector", lambda e: e.tensor_tensor(out=ad63[:], in0=ad63[:], in1=dtc, op=ALU.mult))
                do("tensor", lambda e: e.transpose(psF[0:64, 0:64], ad63[:], identf[0:64, 0:64]))
                do("scalar", lambda e: e.activation(out=w_sh[:], in_=psF[0:64, 0:64], func=AF.Copy))
                do("scalar", lambda e: e.activation(out=small[:, 4:5], in_=ac[:, 63:64], func=AF.Exp))
                do("tensor", lambda e: e.transpose(psF[0:1, 0:64], small[:, 4:5], identf[0:64, 0:64]))
                do("scalar", lambda e: e.activation(out=er63[0:1, :], in_=psF[0:1, 0:64], func=AF.Copy))
                do("tensor", lambda e: e.matmul(psC[:, 0:64], hp["ones_row"][:], er63[0:1, :], start=True, stop=True))
                do("scalar", lambda e: e.activation(out=E63[:], in_=psC[:, 0:64], func=AF.Copy))
                do("tensor", lambda e, c=c: e.transpose(psT[0:64, 0:128], BC[:, c * 64:(c + 1) * 64], identb[:]))
                do("vector", lambda e: e.tensor_copy(out=btm[:], in_=psT[0:64, :]))
                if mode == "b":
                    do("tensor", lambda e, c=c: e.matmul(psF[0:64, 0:64], BC[:, c * 64:(c + 1) * 64], BC[:, 1024 + c * 64: 1024 + (c + 1) * 64], start=True, stop=True))
                    do("scalar", lambda e: e.activation(out=gt[:], in_=psF[0:64, 0:64], func=AF.Copy))
                    do("gpsimd", lambda e: e.affine_select(out=gt[:], in_=gt[:], compare_op=ALU.is_ge, fill=0.0, base=0, pattern=[[1, 64]], channel_multiplier=-1))
                    dma(lambda e: e.dma_start(out=BIG1[0:64, :].rearrange("s (h l) -> s h l", h=64), in_=AP(tensor=acs_scr.tensor, offset=0, ap=[[0, 64], [64, 64], [1, 64]])))
                    do("vector", lambda e: e.tensor_tensor(out=BIG1[0:64, :], in0=BIG1[0:64, :], in1=acsT[:].unsqueeze(2).broadcast_to([64, 64, 64]), op=ALU.subtract))
                    do("vector", lambda e: e.tensor_scalar(out=BIG1[0:64, :], in0=BIG1[0:64, :], scalar1=0.0, scalar2=None, op0=ALU.min))
                    do("scalar", lambda e: e.activation(out=BIG4[0:64, :], in_=BIG1[0:64, :], func=AF.Exp))
                    do("gpsimd", lambda e: e.tensor_tensor(out=BIG4[0:64, :], in0=BIG4[0:64, :], in1=gt[:].unsqueeze(1).broadcast_to([64, 64, 64]), op=ALU.mult))
                dma(lambda e, c=c: e.dma_start(out=BIG2[0:64, :], in_=xssm_scr[c * 64:(c + 1) * 64, :]))
                do("vector", lambda e: e.tensor_tensor(out=BIG2[0:64, :], in0=BIG2[0:64, :], in1=dtT[:].unsqueeze(2).broadcast_to([64, 64, 64]), op=ALU.mult))
                do("vector", lambda e: e.tensor_tensor(out=BIG3[0:64, :], in0=BIG2[0:64, :], in1=w_sh[:].unsqueeze(2).broadcast_to([64, 64, 64]), op=ALU.mult))
                if mode == "b":
                    do("scalar", lambda e: e.activation(out=syb[:], in_=S_run[:], func=AF.Copy))
                    for hh in range(2):
                        hc = hh * 2048
                        mms = []
                        for hl in range(32):
                            h = hh * 32 + hl
                            mms.append(lambda e, h=h, hl=hl: e.matmul(PSA[0:64, hl * 64:(hl + 1) * 64], BIG4[0:64, h * 64:(h + 1) * 64], BIG2[0:64, h * 64:(h + 1) * 64], start=True, stop=True))
                        chain("tensor", mms)
                        chain("tensor", [
                            (lambda e, i=i, c=c, hc=hc: e.matmul(PSB[0:64, i * 512:(i + 1) * 512], BC[:, 1024 + c * 64:1024 + (c + 1) * 64], syb[:, hc + i * 512: hc + (i + 1) * 512], start=True, stop=True))
                            for i in range(4)])
                        do("vector", lambda e, hh=hh: e.tensor_tensor(out=st_tmp[0:64, :], in0=PSB[0:64, 0:2048], in1=eacsT[:, hh * 32:(hh + 1) * 32].unsqueeze(2).broadcast_to([64, 32, 64]), op=ALU.mult))
                        do("vector", lambda e, c=c, hc=hc: e.tensor_tensor(out=y_tile[(c % 2) * 64:(c % 2) * 64 + 64, hc:hc + 2048], in0=PSA[0:64, 0:2048], in1=st_tmp[0:64, :], op=ALU.add))
                for hh in range(2):
                    hc = hh * 2048
                    chain("tensor", [
                        (lambda e, i=i, hc=hc: e.matmul(PSA[:, i * 512:(i + 1) * 512], btm[:], BIG3[0:64, hc + i * 512: hc + (i + 1) * 512], start=True, stop=True))
                        for i in range(4)])
                    do("vector", lambda e, hh=hh: e.tensor_tensor(out=st_tmp[:], in0=S_run[:, hc:hc + 2048], in1=E63[:, hh * 32:(hh + 1) * 32].unsqueeze(2).broadcast_to([128, 32, 64]), op=ALU.mult))
                    do("vector", lambda e, hc=hc: e.tensor_tensor(out=S_run[:, hc:hc + 2048], in0=st_tmp[:], in1=PSA[:, 0:2048], op=ALU.add))
                if mode == "b" and c % 2 == 1:
                    tt = c // 2
                    xsb = BIG1[:, 0:2048].bitcast(BF16)
                    dma(lambda e, tt=tt: e.dma_start(out=xsb, in_=xssm_scr[tt * 128:(tt + 1) * 128, :]))
                    do("vector", lambda e: e.tensor_tensor(out=xsb, in0=xsb, in1=D64[:].unsqueeze(2).broadcast_to([128, 64, 64]), op=ALU.mult))
                    do("vector", lambda e: e.tensor_tensor(out=y_tile[:], in0=y_tile[:], in1=xsb, op=ALU.add))
                    zl = BIG3[:, 0:4096]
                    dma(lambda e, tt=tt: e.dma_start(out=zl, in_=z_scr[tt * 128:(tt + 1) * 128, :]))
                    do("vector", lambda e: e.tensor_tensor(out=y_tile[:], in0=y_tile[:], in1=zl, op=ALU.mult))
                    sqb = BIG2[:, 0:4096]
                    do("scalar", lambda e: e.activation(out=sqb, in_=y_tile[:], func=AF.Square))
                    do("vector", lambda e: e.reduce_sum(out=rs_col[:, 0:1], in_=sqb, axis=AX.X))
                    do("vector", lambda e: e.tensor_scalar(out=rs_col[:, 0:1], in0=rs_col[:, 0:1], scalar1=1.0 / DIN, scalar2=1e-5, op0=ALU.mult, op1=ALU.add))
                    do("scalar", lambda e: e.activation(out=rs_col[:, 1:2], in_=rs_col[:, 0:1], func=AF.Ln))
                    do("scalar", lambda e: e.activation(out=rs_col[:, 1:2], in_=rs_col[:, 1:2], func=AF.Exp, scale=-0.5))
                    do("scalar", lambda e: e.activation(out=y_tile[:], in_=y_tile[:], func=AF.Identity, scale=rs_col[:, 1:2]))
                    nwb = BIG3[:, 0:4096]
                    dma(lambda e: e.dma_start(out=nwb.rearrange("p (a c) -> p a c", a=1), in_=AP(tensor=nw_d.tensor, offset=0, ap=[[0, 128], [0, 1], [1, 4096]])), which="gpsimd")
                    do("vector", lambda e: e.tensor_tensor(out=y_tile[:], in0=y_tile[:], in1=nwb, op=ALU.mult))
                    for k in range(32):
                        do("tensor", lambda e, k=k: e.transpose(psT[:], y_tile[:, k * 128:(k + 1) * 128], identb[:]))
                        do("vector", lambda e, k=k, tt=tt: e.tensor_copy(out=A32[:, k * 512 + (tt % 4) * 128: k * 512 + (tt % 4) * 128 + 128], in_=psT[:]))
                if mode == "b" and c in (7, 15):
                    tchunk = 0 if c == 7 else 1
                    for ft in range(ND):
                        dma(lambda e, ft=ft: e.dma_start(
                            out=BIG2[:].rearrange("p (a c2) -> p a c2", c2=128),
                            in_=opT_d[:, ft * 128:(ft + 1) * 128].rearrange("(a p) c2 -> p a c2", p=128)), which="gpsimd")
                        chain("tensor", [
                            (lambda e, k=k: e.matmul(psC[:], BIG2[:, k * 128:(k + 1) * 128], A32[:, k * 512:(k + 1) * 512], start=(k == 0), stop=(k == 31)))
                            for k in range(32)])
                        dma(lambda e, ft=ft, tchunk=tchunk: e.dma_start(out=x2s[:], in_=x_d[ft * 128:(ft + 1) * 128, tchunk * 512:(tchunk + 1) * 512]))
                        do("vector", lambda e: e.tensor_tensor(out=x2s[:], in0=x2s[:], in1=psC[:], op=ALU.add))
                        dma(lambda e, ft=ft, tchunk=tchunk: e.dma_start(out=x3_d[ft * 128:(ft + 1) * 128, tchunk * 512:(tchunk + 1) * 512], in_=x2s[:]))
            if mode == "a":
                dma(lambda e: e.dma_start(out=sfin_d[:, :], in_=S_run[:]))
    return emit


def build_u5(mode):
    nc = bass.Bass("TRN2", target_bir_lowering=False)
    io = {
        "x_m": nc.dram_tensor("x", [D, TOK], F32, kind="ExternalInput").ap(),
        "xh": nc.dram_tensor("xh", [D, 3], F32, kind="ExternalInput").ap(),
        "ipT": nc.dram_tensor("ipT", [D, DIP], F32, kind="ExternalInput").ap(),
        "cw": nc.dram_tensor("cw", [DXBC, 4], F32, kind="ExternalInput").ap(),
        "cb": nc.dram_tensor("cb", [DXBC], F32, kind="ExternalInput").ap(),
        "alog": nc.dram_tensor("alog", [NH_M], F32, kind="ExternalInput").ap(),
        "dtb": nc.dram_tensor("dtb", [NH_M], F32, kind="ExternalInput").ap(),
        "acs_scr": nc.dram_tensor("acs_scr", [NH_M, CHUNK], F32, kind="Internal").ap(),
        "xssm_scr": nc.dram_tensor("xssm_scr", [TOK, DIN], BF16, kind="Internal").ap(),
    }
    if mode == "b":
        io["dvec"] = nc.dram_tensor("dvec", [NH_M], F32, kind="ExternalInput").ap()
        io["nw"] = nc.dram_tensor("nw", [DIN], F32, kind="ExternalInput").ap()
        io["opT"] = nc.dram_tensor("opT", [DIN, D], F32, kind="ExternalInput").ap()
        io["sf"] = nc.dram_tensor("sf", [NCORES * 128, DIN], F32, kind="ExternalInput").ap()
        io["phiT"] = nc.dram_tensor("phiT", [NH_M, NCORES], F32, kind="ExternalInput").ap()
        io["maskr"] = nc.dram_tensor("maskr", [NH_M, NCORES], F32, kind="ExternalInput").ap()
        io["ws_scr"] = nc.dram_tensor("ws_scr", [8, 64], F32, kind="Internal").ap()
        io["x3"] = nc.dram_tensor("x3", [D, TOK], F32, kind="ExternalOutput").ap()
        io["z_scr"] = nc.dram_tensor("z_scr", [TOK, DIN], BF16, kind="Internal").ap()
    else:
        io["sfin"] = nc.dram_tensor("sfin", [128, DIN], F32, kind="ExternalOutput").ap()
        io["phi"] = nc.dram_tensor("phi", [NH_M, 1], F32, kind="ExternalOutput").ap()
    st = contextlib.ExitStack()
    io["PSA"] = st.enter_context(nc.psum_tensor("PSA", [128, 2048], F32))
    io["PSB"] = st.enter_context(nc.psum_tensor("PSB", [128, 2048], F32))
    io["sst"] = st.enter_context(nc.semaphore("sst"))
    ab = Arena(nc)
    run_phases(nc, [make_u5(nc, io, ab, mode)])
    st.close()
    return nc


NF1 = HID1 // 128


def make_u6(nc, io, ab):
    ab.reset("u6_")
    x_d = io["x_f"]
    wgq_d, wuq_d, wdq_d = io["wgq"], io["wuq"], io["wdq"]
    sg_d, su_d, sd_d = io["sg"], io["su"], io["sd"]
    x4_d = io["x4"]
    PSA, PSB = io["PSA"], io["PSB"]
    sb = ab.sb
    A = sb("A", [128, ND * TOK], BF16)
    GU = sb("GU", [128, NF1 * TOK], BF16)
    XCH = GU[:, 0:16384].bitcast(F32)
    wcol_b = [sb(f"wcol_b{i}", [128, 2048], BF16) for i in range(2)]
    wdcol_b = [sb(f"wdcol_b{i}", [128, NF1 * 128], BF16) for i in range(2)]
    gt_sb = sb("gt_sb", [128, 512], BF16)
    tf2 = sb("tf2", [128, 512], F32)
    scol = sb("scol", [128, 2 * NF1 + ND], F32)
    x2s = sb("x2s", [128, 512], F32)
    hp = ln_helpers(ab, PSB)
    psC = PSA[:, 0:512]
    psC2 = PSA[:, 512:1024]

    def emit(ctx, sem):
        if True:
            w = lambda: [(sem, ctx.val(sem))]

            def chain(which, fns, inc_last=1, extra=()):
                if ctx.en == which:
                    ctx.eng.wait_ge(sem, ctx.val(sem))
                    for (s2, v2) in extra:
                        ctx.eng.wait_ge(s2, v2)
                    for fn in fns:
                        ins = fn(ctx.eng)
                    ins.then_inc(sem, inc_last)
                ctx._bump(sem, inc_last)

            def do(which, f2):
                ctx.do(which, f2, sem=sem, wait=w())

            def dma(f2, which="sync"):
                ctx.do(which, f2, sem=sem, inc=16, wait=w())

            do("gpsimd", lambda e: e.memset(hp["ones_col_b"][:], 1.0))
            do("gpsimd", lambda e: e.memset(hp["ones_row"][:], 1.0))
            ctx.burst("sync", [
                (lambda e, ff=ff: e.dma_start(out=scol[:, ff:ff + 1], in_=sg_d[ff * 128:(ff + 1) * 128].unsqueeze(1)))
                for ff in range(NF1)] + [
                (lambda e, ff=ff: e.dma_start(out=scol[:, NF1 + ff:NF1 + ff + 1], in_=su_d[ff * 128:(ff + 1) * 128].unsqueeze(1)))
                for ff in range(NF1)] + [
                (lambda e, ft=ft: e.dma_start(out=scol[:, 2 * NF1 + ft:2 * NF1 + ft + 1], in_=sd_d[ft * 128:(ft + 1) * 128].unsqueeze(1)))
                for ft in range(ND)], sem)
            for tc in range(2):
                for d in range(ND):
                    dma(lambda e, d=d, tc=tc: e.dma_start(out=XCH[:, d * 512:(d + 1) * 512], in_=x_d[d * 128:(d + 1) * 128, tc * 512:(tc + 1) * 512]))
                emit_ln_fm2(ctx, sem, lambda d, tcc: XCH[:, d * 512:(d + 1) * 512], A, hp, [tc], 512, 1024)

            sstS = io["sst"]
            CH = []
            for ff in range(NF1):
                CH.append(("gu", wgq_d[:, ff * 128:(ff + 1) * 128]))
                CH.append(("gu", wuq_d[:, ff * 128:(ff + 1) * 128]))
            for ft in range(ND):
                CH.append(("d", wdq_d[:, ft * 128:(ft + 1) * 128]))
            ring = {"g": 0, "u": 0, "dn": [], "cons": []}

            def pump():
                g = ring["g"]
                if g >= len(CH):
                    return
                kind, src = CH[g]
                buf = wcol_b[g % 2] if kind == "gu" else wdcol_b[g % 2]
                wt = (w() if g < 2 else [(sem, ring["cons"][g - 2])]) + ([] if g == 0 else [(sstS, ring["dn"][g - 1])])
                ring["dn"].append(ctx.do("gpsimd", lambda e, src=src, buf=buf, g=g: e.dma_start(
                    out=buf[:].rearrange("p (a c) -> p a c", c=128),
                    in_=src.rearrange("(a p) c -> p a c", p=128)), sem=sstS, inc=16, wait=wt))
                ring["g"] += 1

            def use():
                g = ring["u"]
                kind = CH[g][0]
                buf = wcol_b[g % 2] if kind == "gu" else wdcol_b[g % 2]
                return buf, [(sstS, ring["dn"][g])]

            def done_use():
                ring["cons"].append(ctx.val(sem))
                ring["u"] += 1
                pump()

            pump()
            pump()
            for ff in range(NF1):
                for (sc_off, _) in ((0, 0), (NF1, 0)):
                    rb, ex = use()
                    for tc in range(2):
                        chain("tensor", [
                            (lambda e, d=d, tc=tc, rb=rb: e.matmul(psC if tc == 0 else psC2, rb[:, d * 128:(d + 1) * 128], A[:, d * 1024 + tc * 512: d * 1024 + tc * 512 + 512], start=(d == 0), stop=(d == ND - 1)))
                            for d in range(ND)], extra=(ex if tc == 0 else ()))
                    done_use()
                    if sc_off == 0:
                        for tc in range(2):
                            do("scalar", lambda e, ff=ff, tc=tc: e.activation(out=GU[:, ff * 1024 + tc * 512: ff * 1024 + tc * 512 + 512], in_=(psC if tc == 0 else psC2), func=AF.Silu, scale=scol[:, ff:ff + 1]))
                    else:
                        for tc in range(2):
                            do("scalar", lambda e, ff=ff, tc=tc: e.activation(out=gt_sb[:], in_=(psC if tc == 0 else psC2), func=AF.Identity, scale=scol[:, NF1 + ff:NF1 + ff + 1]))
                            do("vector", lambda e, ff=ff, tc=tc: e.tensor_tensor(out=GU[:, ff * 1024 + tc * 512: ff * 1024 + tc * 512 + 512], in0=GU[:, ff * 1024 + tc * 512: ff * 1024 + tc * 512 + 512], in1=gt_sb[:], op=ALU.mult))
            for ft in range(ND):
                rb, ex = use()
                for tc in range(2):
                    chain("tensor", [
                        (lambda e, f=f2, tc=tc, rb=rb: e.matmul(psC if tc == 0 else psC2, rb[:, f * 128:(f + 1) * 128], GU[:, f * 1024 + tc * 512: f * 1024 + tc * 512 + 512], start=(f == 0), stop=(f == NF1 - 1)))
                        for f2 in range(NF1)], extra=(ex if tc == 0 else ()))
                done_use()
                for tc in range(2):
                    do("scalar", lambda e, ft=ft, tc=tc: e.activation(out=tf2[:], in_=(psC if tc == 0 else psC2), func=AF.Identity, scale=scol[:, 2 * NF1 + ft:2 * NF1 + ft + 1]))
                    dma(lambda e, ft=ft, tc=tc: e.dma_start(out=x2s[:], in_=x_d[ft * 128:(ft + 1) * 128, tc * 512:(tc + 1) * 512]))
                    do("vector", lambda e: e.tensor_tensor(out=x2s[:], in0=x2s[:], in1=tf2[:], op=ALU.add))
                    dma(lambda e, ft=ft, tc=tc: e.dma_start(out=x4_d[ft * 128:(ft + 1) * 128, tc * 512:(tc + 1) * 512], in_=x2s[:]))
    return emit


def build_u6():
    nc = bass.Bass("TRN2", target_bir_lowering=False)
    io = {
        "x_f": nc.dram_tensor("x", [D, TOK], F32, kind="ExternalInput").ap(),
        "wgq": nc.dram_tensor("wgq", [D, HID1], BF16, kind="ExternalInput").ap(),
        "wuq": nc.dram_tensor("wuq", [D, HID1], BF16, kind="ExternalInput").ap(),
        "wdq": nc.dram_tensor("wdq", [HID1, D], BF16, kind="ExternalInput").ap(),
        "sg": nc.dram_tensor("sg", [HID1], F32, kind="ExternalInput").ap(),
        "su": nc.dram_tensor("su", [HID1], F32, kind="ExternalInput").ap(),
        "sd": nc.dram_tensor("sd", [D], F32, kind="ExternalInput").ap(),
        "x4": nc.dram_tensor("x4", [D, TOK], F32, kind="ExternalOutput").ap(),
    }
    st = contextlib.ExitStack()
    io["PSA"] = st.enter_context(nc.psum_tensor("PSA", [128, 2048], F32))
    io["PSB"] = st.enter_context(nc.psum_tensor("PSB", [128, 2048], F32))
    io["sst"] = st.enter_context(nc.semaphore("sst"))
    ab = Arena(nc)
    run_phases(nc, [make_u6(nc, io, ab)])
    st.close()
    return nc


def build_full():
    nc = bass.Bass("TRN2", target_bir_lowering=False, detect_race_conditions=False)
    io = {}
    T = nc.dram_tensor
    io["x0"] = T("x0", [D, TOK], F32, kind="ExternalInput").ap()
    io["wqT"] = T("wqT", [D, HPC * HD], F32, kind="ExternalInput").ap()
    io["wkT"] = T("wkT", [D, HPC * HD], F32, kind="ExternalInput").ap()
    io["wvT"] = T("wvT", [D, HPC * HD], F32, kind="ExternalInput").ap()
    io["bq"] = T("bq", [HPC * HD], F32, kind="ExternalInput").ap()
    io["bk"] = T("bk", [HPC * HD], F32, kind="ExternalInput").ap()
    io["bv"] = T("bv", [HPC * HD], F32, kind="ExternalInput").ap()
    io["woT"] = T("woT", [D, D], F32, kind="ExternalInput").ap()
    io["bo"] = T("bo", [D], F32, kind="ExternalInput").ap()
    io["w1T"] = T("w1T", [D, DFF0], F32, kind="ExternalInput").ap()
    io["b1"] = T("b1", [DFF0], F32, kind="ExternalInput").ap()
    io["w2T"] = T("w2T", [DFF0, D], F32, kind="ExternalInput").ap()
    io["b2"] = T("b2", [D], F32, kind="ExternalInput").ap()
    io["ipT"] = T("ipT", [D, DIP], F32, kind="ExternalInput").ap()
    io["cw"] = T("cw", [DXBC, 4], F32, kind="ExternalInput").ap()
    io["cb"] = T("cb", [DXBC], F32, kind="ExternalInput").ap()
    io["alog"] = T("alog", [NH_M], F32, kind="ExternalInput").ap()
    io["dtb"] = T("dtb", [NH_M], F32, kind="ExternalInput").ap()
    io["dvec"] = T("dvec", [NH_M], F32, kind="ExternalInput").ap()
    io["nw"] = T("nw", [DIN], F32, kind="ExternalInput").ap()
    io["opT"] = T("opT", [DIN, D], F32, kind="ExternalInput").ap()
    io["maskr"] = T("maskr", [NH_M, NCORES], F32, kind="ExternalInput").ap()
    io["zmask"] = T("zmask", [128, 1], F32, kind="ExternalInput").ap()
    io["w1s"] = T("w1s", [D, S // NCORES], F32, kind="ExternalInput").ap()
    io["w2s"] = T("w2s", [DFF0, D // NCORES], F32, kind="ExternalInput").ap()
    io["wgT"] = T("wgT", [D, HID1], I32, kind="ExternalInput").ap()
    io["wuT"] = T("wuT", [D, HID1], I32, kind="ExternalInput").ap()
    io["wdT"] = T("wdT", [HID1, D], I32, kind="ExternalInput").ap()
    io["sg"] = T("sg", [HID1], F32, kind="ExternalInput").ap()
    io["su"] = T("su", [HID1], F32, kind="ExternalInput").ap()
    io["sd"] = T("sd", [D], F32, kind="ExternalInput").ap()
    io["x4"] = T("x4", [D, TOK], F32, kind="ExternalOutput").ap()
    if _FDBG:
        d_lnx = T("d_lnx", [D, TOK], BF16, kind="ExternalOutput").ap()
        d_attn = T("d_attn", [HPC * HD, S], BF16, kind="ExternalOutput").ap()
        d_x2 = T("d_x2", [D, TOK], F32, kind="ExternalOutput").ap()
        d_x3 = T("d_x3", [D, TOK], F32, kind="ExternalOutput").ap()
        d_sphi = T("d_sphi", [132, DIN], F32, kind="ExternalOutput").ap()
        io["d_A"] = T("d_A", [D, TOK], BF16, kind="ExternalOutput").ap()
        io["d_x1"] = T("d_x1", [D, TOK], F32, kind="ExternalOutput").ap()
        io["d_thr"] = T("d_thr", [128, 4], F32, kind="ExternalOutput").ap()
        d_ws = T("d_ws", [16, 2], F32, kind="ExternalOutput").ap()
    io["lnx_c"] = T("lnx_c", [D, TOK], BF16, kind="Internal").ap()
    lnx_all = T("lnx_all", [NCORES * D, TOK], BF16, kind="Internal").ap()
    io["wsum_c"] = T("wsum_c", [1, 2], F32, kind="Internal").ap()
    io["wsum_all"] = T("wsum_all", [NCORES, 2], F32, kind="Internal").ap()
    io["attn_c"] = T("attn_c", [HPC * HD, S], BF16, kind="Internal").ap()
    attn_all = T("attn_all", [D, S], BF16, kind="Internal").ap()
    io["xbce_c"] = T("xbce_c", [DXBC, 4], BF16, kind="Internal").ap()
    io["xbce_all"] = T("xbce_all", [NCORES * DXBC, 4], BF16, kind="Internal").ap()
    sphi_c = T("sphi_c", [132, DIN], F32, kind="Internal").ap()
    sphi_all = T("sphi_all", [NCORES * 132, DIN], F32, kind="Internal").ap()
    io["x2"] = T("x2_scr", [D, TOK], F32, kind="Internal").ap()
    io["x3"] = T("x3_scr", [D, TOK], F32, kind="Internal").ap()
    io["w1q"] = T("w1q_scr", [D, DFF0], BF16, kind="Internal").ap()
    io["w2q"] = T("w2q_scr", [DFF0, D], BF16, kind="Internal").ap()
    io["wgq"] = T("wgq_scr", [D, HID1], BF16, kind="Internal").ap()
    io["wuq"] = T("wuq_scr", [D, HID1], BF16, kind="Internal").ap()
    io["wdq"] = T("wdq_scr", [HID1, D], BF16, kind="Internal").ap()
    io["acs_scr"] = T("acs_scr", [NH_M, CHUNK], F32, kind="Internal").ap()
    io["xssm_scr"] = T("xssm_scr", [TOK, DIN], BF16, kind="Internal").ap()
    io["z_scr"] = T("z_scr", [TOK, DIN], BF16, kind="Internal").ap()
    io["ws_scr"] = T("ws_scr", [8, 64], F32, kind="Internal").ap()

    st = contextlib.ExitStack()
    io["PSA"] = st.enter_context(nc.psum_tensor("PSA", [128, 2048], F32))
    io["PSB"] = st.enter_context(nc.psum_tensor("PSB", [128, 2048], F32))
    io["sst"] = st.enter_context(nc.semaphore("sst"))
    io["pq_in"] = st.enter_context(nc.semaphore("pq_in"))
    io["pq_v"] = st.enter_context(nc.semaphore("pq_v"))
    io["pq_out"] = st.enter_context(nc.semaphore("pq_out"))

    io["lnx_tile"] = lambda d, qc: lnx_all[(qc // 2) * D + d * 128: (qc // 2) * D + (d + 1) * 128, (qc % 2) * 512:(qc % 2) * 512 + 512]

    def attn_dyn(eng, A):
        off = eng.snap(eng.partition_id() * TOK)
        src_ap = attn_all[:, bass.ds(off, TOK)].rearrange("(dp p) t -> p dp t", p=128)
        return eng.dma_start(out=A[:, 0:ND * TOK].rearrange("p (dp t) -> p dp t", t=TOK), in_=src_ap)
    io["attn_dyn"] = attn_dyn
    io["wsum_mode"] = "fused"
    io["x_m"] = io["x2"]
    io["x_f"] = io["x3"]
    io["sfin"] = sphi_c[0:128, :]
    io["phi"] = sphi_c[128:129, 0:64]
    io["phi_is_row"] = True
    io["sf_row"] = lambda j: sphi_all[j * 132:j * 132 + 128, :]
    io["phi_col"] = lambda j: sphi_all[j * 132 + 128:j * 132 + 129, 0:64]

    def prevoff_load(ctx):
        holder = {}

        def fn(ct):
            if "base" not in holder:
                cid = ctx.eng.partition_id()
                holder["base"] = ctx.eng.snap(((cid + 7) % 8) * DXBC)
            return holder["base"] + ct * 128
        return (None, fn)
    io["prevoff_load"] = prevoff_load

    def cc(ctx, sem, kind, ins_ap, outs_ap):
        ctx.do("gpsimd", lambda e: e.collective_compute(
            kind, ALU.bypass, replica_groups=[list(range(NCORES))],
            ins=[ins_ap.opt()], outs=[outs_ap.opt()]), sem=sem, inc=1,
            wait=[(sem, ctx.val(sem))])

    def mid_ag3(ctx, sem):
        cc(ctx, sem, "AllGather", io["xbce_c"], io["xbce_all"])
    io["mid_ag3"] = mid_ag3

    ab = Arena(nc)
    em_u1 = make_u1(nc, io, ab)
    em_u2 = make_u2(nc, io, ab)
    em_u3 = make_u3(nc, io, ab)
    em_u5a = make_u5(nc, io, ab, "a", halo_mode="gather")
    em_u5b = make_u5(nc, io, ab, "b", halo_mode="gather", skip_front=True)
    em_u6 = make_u6(nc, io, ab)

    def em_wsum(ctx, sem):
        w = lambda: [(sem, ctx.val(sem))]
        wf = wsum_buf
        acc = wsum_acc
        ctx.do("gpsimd", lambda e: e.memset(acc[:], 0.0), sem=sem, wait=w())
        for (wT, colacc, rows, width, grp) in ((io["w1s"], 0, D, 1024, 1), (io["w2s"], 1, DFF0, 256, 4)):
            for rg in range(rows // 128 // grp):
                ctx.burst("sync", [
                    (lambda e, r=rg * grp + i, wT=wT, width=width, i=i: e.dma_start(out=wf[:, i * width:(i + 1) * width], in_=wT[(rg * grp + i) * 128:(rg * grp + i + 1) * 128, :]))
                    for i in range(grp)], sem)
                for i in range(grp):
                    ctx.do("vector", lambda e, width=width, i=i: e.reduce_sum(out=acc[:, 2:3], in_=wf[:, i * width:(i + 1) * width], axis=AX.X, apply_absolute_value=True), sem=sem, wait=w())
                    ctx.do("vector", lambda e, colacc=colacc: e.tensor_add(out=acc[:, colacc:colacc + 1], in0=acc[:, colacc:colacc + 1], in1=acc[:, 2:3]), sem=sem, wait=w())
        ctx.do("gpsimd", lambda e: e.memset(wsum_onesf[:], 1.0), sem=sem, wait=w())
        for i in range(2):
            ctx.do("tensor", lambda e, i=i: e.matmul(io["PSB"][0:1, i:i + 1], wsum_onesf[:], acc[:, i:i + 1], start=True, stop=True), sem=sem, wait=w())
        ctx.do("scalar", lambda e: e.activation(out=acc[0:1, 0:2], in_=io["PSB"][0:1, 0:2], func=AF.Copy), sem=sem, wait=w())
        ctx.do("sync", lambda e: e.dma_start(out=io["wsum_c"][:, :], in_=acc[0:1, 0:2]), sem=sem, inc=16, wait=w())

    ab.reset("ws_")
    wsum_buf = ab.sb("wsum_buf", [128, 1024], F32)
    wsum_acc = ab.sb("wsum_acc", [128, 4], F32)
    wsum_onesf = ab.sb("wsum_onesf", [128, 1], F32)

    def em_ag1(ctx, sem):
        cc(ctx, sem, "AllGather", io["lnx_c"], lnx_all)
        cc(ctx, sem, "AllGather", io["wsum_c"], io["wsum_all"])

    def em_ag2(ctx, sem):
        cc(ctx, sem, "AllGather", io["attn_c"], attn_all)

    def em_ag4(ctx, sem):
        cc(ctx, sem, "AllGather", sphi_c, sphi_all)

    def em_dbg(ctx, sem):
        ctx.do("sync", lambda e: e.dma_start(out=d_ws[0:8, :], in_=io["wsum_all"][:, :]), sem=sem, inc=16, wait=[(sem, ctx.val(sem))])
        ctx.do("sync", lambda e: e.dma_start(out=d_ws[8:9, :], in_=io["wsum_c"][:, :]), sem=sem, inc=16, wait=[(sem, ctx.val(sem))])
        for (dst, srcb) in ((d_lnx, io["lnx_c"]), (d_attn, io["attn_c"]), (d_x2, io["x2"]), (d_x3, io["x3"]), (d_sphi, sphi_c)):
            ctx.do("sync", lambda e, dst=dst, srcb=srcb: e.dma_start(out=dst[:, :], in_=srcb[:, :]), sem=sem, inc=16, wait=[(sem, ctx.val(sem))])

    emits = [em_u1, em_wsum, em_ag1, em_u2, em_ag2, em_u3, em_u5a, em_ag4, em_u5b, em_u6] + ([em_dbg] if _FDBG else [])
    run_phases_multi(nc, emits)
    st.close()
    return nc


def run_phases_multi(nc, emits):
    st2 = contextlib.ExitStack()
    sems = [st2.enter_context(nc.semaphore(f"s{i}")) for i in range(len(emits))]

    def prog(eng_name):
        def f(eng):
            ctx = Ctx(eng_name, eng)
            prev_sem, prev_val = None, 0
            for i, em in enumerate(emits):
                if prev_sem is not None:
                    for en in ENGS:
                        ctx.wait(en, prev_sem, prev_val)
                em(ctx, sems[i])
                prev_sem, prev_val = sems[i], ctx.val(sems[i])
            ctx.wait("sync", prev_sem, prev_val)
        return f

    with nc.Block() as block:
        for en in ENGS:
            getattr(block, en)(prog(en))
    st2.close()


def kernel_fused(inputs):
    x = np.ascontiguousarray(np.asarray(inputs["L0_attn_hidden_states"]).reshape(S, D))
    nc = build_full()
    wqT = np.ascontiguousarray(np.asarray(inputs["L0_attn_Wq"]).T)
    wkT = np.ascontiguousarray(np.asarray(inputs["L0_attn_Wk"]).T)
    wvT = np.ascontiguousarray(np.asarray(inputs["L0_attn_Wv"]).T)
    in_maps = []
    for c in range(NCORES):
        sl = slice(c * HPC * HD, (c + 1) * HPC * HD)
        m = {
            "x0": np.ascontiguousarray(x[c * TOK:(c + 1) * TOK].T),
            "wqT": np.ascontiguousarray(wqT[:, sl]),
            "wkT": np.ascontiguousarray(wkT[:, sl]),
            "wvT": np.ascontiguousarray(wvT[:, sl]),
            "bq": np.ascontiguousarray(np.asarray(inputs["L0_attn_bq"])[sl]),
            "bk": np.ascontiguousarray(np.asarray(inputs["L0_attn_bk"])[sl]),
            "bv": np.ascontiguousarray(np.asarray(inputs["L0_attn_bv"])[sl]),
            "woT": np.ascontiguousarray(np.asarray(inputs["L0_attn_Wo"]).T),
            "bo": np.asarray(inputs["L0_attn_bo"]),
            "w1T": np.ascontiguousarray(np.asarray(inputs["L0_ffn_w1"]).T),
            "w1s": np.ascontiguousarray(np.asarray(inputs["L0_ffn_w1"]).T[:, c * 1024:(c + 1) * 1024]),
            "w2s": np.ascontiguousarray(np.asarray(inputs["L0_ffn_w2"]).T[:, c * 256:(c + 1) * 256]),
            "b1": np.asarray(inputs["L0_ffn_b1"]),
            "w2T": np.ascontiguousarray(np.asarray(inputs["L0_ffn_w2"]).T),
            "b2": np.asarray(inputs["L0_ffn_b2"]),
            "ipT": np.ascontiguousarray(np.asarray(inputs["L1_attn_in_proj_w"]).T),
            "cw": np.ascontiguousarray(np.asarray(inputs["L1_attn_conv_w"]).reshape(DXBC, 4)),
            "cb": np.asarray(inputs["L1_attn_conv_b"]),
            "alog": np.asarray(inputs["L1_attn_A_log"]),
            "dtb": np.asarray(inputs["L1_attn_dt_bias"]),
            "dvec": np.asarray(inputs["L1_attn_D"]),
            "nw": np.asarray(inputs["L1_attn_norm_w"]),
            "opT": np.ascontiguousarray(np.asarray(inputs["L1_attn_out_proj_w"]).T),
            "maskr": np.ascontiguousarray(np.tile((np.arange(NCORES) < c).astype(np.float32), (NH_M, 1))),
            "zmask": np.full((128, 1), 0.0 if c == 0 else 1.0, np.float32),
            "wgT": np.ascontiguousarray(np.asarray(inputs["L1_ffn_wq_gate"]).T),
            "wuT": np.ascontiguousarray(np.asarray(inputs["L1_ffn_wq_up"]).T),
            "wdT": np.ascontiguousarray(np.asarray(inputs["L1_ffn_wq_down"]).T),
            "sg": np.asarray(inputs["L1_ffn_s_gate"]),
            "su": np.asarray(inputs["L1_ffn_s_up"]),
            "sd": np.asarray(inputs["L1_ffn_s_down"]),
        }
        in_maps.append(m)
    res = run_bass_kernel_spmd(nc, in_maps, core_ids=list(range(NCORES)))
    if _FDBG:
        global _DBG
        _DBG = {k: [res.results[c][k] for c in range(NCORES)] for k in ("d_lnx", "d_attn", "d_x2", "d_x3", "d_sphi", "d_A", "d_x1", "d_thr", "d_ws")}
    x4_full = np.concatenate([res.results[c]["x4"] for c in range(NCORES)], axis=1)
    return np.ascontiguousarray(x4_full.T).reshape(1, S, D)


_BF = ml_dtypes.bfloat16


def kernel(**inputs):
    global _DBG
    if _STAGE == "fused":
        return kernel_fused(inputs)
    x = np.ascontiguousarray(np.asarray(inputs["L0_attn_hidden_states"]).reshape(S, D))
    if _X2_OVERRIDE is not None:
        x2_full = _X2_OVERRIDE
        return _run_u5(inputs, x2_full)
    nc1 = build_u1()
    in_maps = [{"x": np.ascontiguousarray(x[c * TOK:(c + 1) * TOK].T)} for c in range(NCORES)]
    res = run_bass_kernel_spmd(nc1, in_maps, core_ids=list(range(NCORES)))
    lnx_full = np.concatenate([res.results[c]["lnx"] for c in range(NCORES)], axis=1)
    if _STAGE == "u1":
        return lnx_full

    wqT = np.ascontiguousarray(np.asarray(inputs["L0_attn_Wq"]).T)
    wkT = np.ascontiguousarray(np.asarray(inputs["L0_attn_Wk"]).T)
    wvT = np.ascontiguousarray(np.asarray(inputs["L0_attn_Wv"]).T)
    bq = np.asarray(inputs["L0_attn_bq"]); bk = np.asarray(inputs["L0_attn_bk"]); bv = np.asarray(inputs["L0_attn_bv"])
    nc2 = build_u2()
    in_maps = []
    for c in range(NCORES):
        sl = slice(c * HPC * HD, (c + 1) * HPC * HD)
        in_maps.append({
            "lnx": lnx_full,
            "wqT": np.ascontiguousarray(wqT[:, sl]),
            "wkT": np.ascontiguousarray(wkT[:, sl]),
            "wvT": np.ascontiguousarray(wvT[:, sl]),
            "bq": np.ascontiguousarray(bq[sl]),
            "bk": np.ascontiguousarray(bk[sl]),
            "bv": np.ascontiguousarray(bv[sl]),
        })
    res = run_bass_kernel_spmd(nc2, in_maps, core_ids=list(range(NCORES)))
    attn_full = np.concatenate([res.results[c]["attn"] for c in range(NCORES)], axis=0)
    if _STAGE == "u2":
        return attn_full

    woT = np.ascontiguousarray(np.asarray(inputs["L0_attn_Wo"]).T)
    bo = np.asarray(inputs["L0_attn_bo"])
    w1T = np.ascontiguousarray(np.asarray(inputs["L0_ffn_w1"]).T)
    b1 = np.asarray(inputs["L0_ffn_b1"])
    w2T = np.ascontiguousarray(np.asarray(inputs["L0_ffn_w2"]).T)
    b2 = np.asarray(inputs["L0_ffn_b2"])
    nc3 = build_u3()
    in_maps = []
    for c in range(NCORES):
        in_maps.append({
            "x": np.ascontiguousarray(x[c * TOK:(c + 1) * TOK].T),
            "attn": np.ascontiguousarray(attn_full[:, c * TOK:(c + 1) * TOK]),
            "woT": woT, "bo": bo, "w1T": w1T, "b1": b1, "w2T": w2T, "b2": b2,
        })
    res = run_bass_kernel_spmd(nc3, in_maps, core_ids=list(range(NCORES)))
    x2_full = np.concatenate([res.results[c]["x2"] for c in range(NCORES)], axis=1)
    if _STAGE == "u3":
        return x2_full
    return _run_u5(inputs, x2_full)


def _run_u5(inputs, x2_full):
    global _DBG
    ipT = np.ascontiguousarray(np.asarray(inputs["L1_attn_in_proj_w"]).T)
    cw = np.ascontiguousarray(np.asarray(inputs["L1_attn_conv_w"]).reshape(DXBC, 4))
    cb = np.asarray(inputs["L1_attn_conv_b"])
    alog = np.asarray(inputs["L1_attn_A_log"])
    dtb = np.asarray(inputs["L1_attn_dt_bias"])
    dvec = np.asarray(inputs["L1_attn_D"])
    nw = np.asarray(inputs["L1_attn_norm_w"])
    opT = np.ascontiguousarray(np.asarray(inputs["L1_attn_out_proj_w"]).T)
    zero_h = np.zeros((D, 3), np.float32)
    nc5a = build_u5("a")
    in_maps = []
    for c in range(NCORES):
        xs = np.ascontiguousarray(x2_full[:, c * TOK:(c + 1) * TOK])
        xh = zero_h if c == 0 else np.ascontiguousarray(x2_full[:, c * TOK - 3:c * TOK])
        in_maps.append({"x": xs, "xh": xh, "ipT": ipT, "cw": cw, "cb": cb, "alog": alog, "dtb": dtb})
    res = run_bass_kernel_spmd(nc5a, in_maps, core_ids=list(range(NCORES)))
    sf = np.concatenate([res.results[c]["sfin"] for c in range(NCORES)], axis=0)
    phiT = np.concatenate([res.results[c]["phi"] for c in range(NCORES)], axis=1)
    if _STAGE == "u5a":
        return sf, phiT

    nc5b = build_u5("b")
    in_maps = []
    for c in range(NCORES):
        xs = np.ascontiguousarray(x2_full[:, c * TOK:(c + 1) * TOK])
        xh = zero_h if c == 0 else np.ascontiguousarray(x2_full[:, c * TOK - 3:c * TOK])
        maskr = np.ascontiguousarray(np.tile((np.arange(NCORES) < c).astype(np.float32), (NH_M, 1)))
        in_maps.append({"x": xs, "xh": xh, "ipT": ipT, "cw": cw, "cb": cb, "alog": alog, "dtb": dtb,
                        "dvec": dvec, "nw": nw, "opT": opT, "sf": sf, "phiT": np.ascontiguousarray(phiT), "maskr": maskr})
    res = run_bass_kernel_spmd(nc5b, in_maps, core_ids=list(range(NCORES)))
    x3_full = np.concatenate([res.results[c]["x3"] for c in range(NCORES)], axis=1)
    _DBG = {"x2": x2_full}
    if _STAGE == "u5":
        return x3_full

    wgT = np.ascontiguousarray(np.asarray(inputs["L1_ffn_wq_gate"]).T)
    wuT = np.ascontiguousarray(np.asarray(inputs["L1_ffn_wq_up"]).T)
    wdT = np.ascontiguousarray(np.asarray(inputs["L1_ffn_wq_down"]).T)
    sg = np.asarray(inputs["L1_ffn_s_gate"]); su = np.asarray(inputs["L1_ffn_s_up"]); sd = np.asarray(inputs["L1_ffn_s_down"])
    nc6 = build_u6()
    in_maps = []
    for c in range(NCORES):
        in_maps.append({"x": np.ascontiguousarray(x3_full[:, c * TOK:(c + 1) * TOK]),
                        "wgT": wgT, "wuT": wuT, "wdT": wdT, "sg": sg, "su": su, "sd": sd})
    res = run_bass_kernel_spmd(nc6, in_maps, core_ids=list(range(NCORES)))
    x4_full = np.concatenate([res.results[c]["x4"] for c in range(NCORES)], axis=1)
    if _STAGE == "u6":
        return x4_full
    return np.ascontiguousarray(x4_full.T).reshape(1, S, D)


_STAGE = "fused"
_FDBG = False
_DBG = None
_X2_OVERRIDE = None


# version 12
# speedup vs baseline: 1.0491x; 1.0491x over previous
import contextlib
import numpy as np
import ml_dtypes
import concourse.bass as bass
import concourse.mybir as mybir
from concourse.bass_utils import run_bass_kernel_spmd
from concourse.ap import AP

F32 = mybir.dt.float32
BF16 = mybir.dt.bfloat16
I32 = mybir.dt.int32
AF = mybir.ActivationFunctionType
ALU = mybir.AluOpType
AX = mybir.AxisListType

NCORES = 8
S = 8192
D = 2048
TOK = S // NCORES
NT = TOK // 128
ND = D // 128
HEADS = 16
HD = 128
HPC = HEADS // NCORES
DFF0 = 8192
DIN = 4096
DSTATE = 128
NH_M = 64
PH = 64
DXBC = 4352
DIP = 8512
CHUNK = 64
NCHUNK = TOK // CHUNK
HID1 = 5632

ENGS = ("sync", "tensor", "scalar", "vector", "gpsimd")


class Ctx:

    def __init__(self, eng_name, eng):
        self.en = eng_name
        self.eng = eng
        self.c = {}

    def val(self, sem):
        return self.c.get(sem.name, 0)

    def _bump(self, sem, inc):
        self.c[sem.name] = self.c.get(sem.name, 0) + inc
        return self.c[sem.name]

    def do(self, which, f, sem=None, inc=1, wait=()):
        if self.en == which:
            for (s, v) in wait:
                self.eng.wait_ge(s, v)
            ins = f(self.eng)
            if sem is not None:
                ins.then_inc(sem, inc)
        if sem is not None:
            return self._bump(sem, inc)
        return None

    def burst(self, which, fns, sem, wait=None):
        tgt = self.val(sem) if wait is None else wait
        if self.en == which:
            self.eng.wait_ge(sem, tgt)
            for fn in fns:
                fn(self.eng).then_inc(sem, 16)
        for _ in fns:
            self._bump(sem, 16)

    def wait(self, which, sem, v):
        if self.en == which:
            self.eng.wait_ge(sem, v)

    def barrier(self, bar_sem, engines=ENGS):
        n = len(engines)
        base = self.val(bar_sem)
        for e in engines:
            self.do(e, lambda eng: eng.nop(), sem=bar_sem, inc=1)
        for e in engines:
            self.wait(e, bar_sem, base + n)



class Arena:

    def __init__(self, nc, base=None):
        self.nc = nc
        self.base = ((nc.sbuf_base + 31) // 32 * 32) if base is None else base
        self.off = 0
        self.pfx = ""

    def reset(self, pfx):
        self.off = 0
        self.pfx = pfx

    def sb(self, name, shape, dtype):
        import math
        elem = {F32: 4, BF16: 2, I32: 4}[dtype]
        per_part = shape[1] * elem if len(shape) == 2 else int(np.prod(shape[1:])) * elem
        per_part = (per_part + 31) // 32 * 32
        t = self.nc.alloc_sbuf_tensor_at(self.pfx + name, list(shape), dtype, offset=self.base + self.off)
        self.off += per_part
        return t


def build_identity(ctx, g_ident):
    ctx.do("gpsimd", lambda e: e.memset(g_ident[:], 1.0))
    ctx.do("gpsimd", lambda e: e.affine_select(
        out=g_ident[:], in_=g_ident[:], compare_op=ALU.is_equal, fill=0.0,
        base=0, pattern=[[-1, 128]], channel_multiplier=1))


def emit_ln_fm(ctx, sem, x_fm, out_fm, hp, nd=ND, ntok=TOK):
    d_model = nd * 128
    w = lambda: [(sem, ctx.val(sem))]
    tb0, tb1 = hp["tb0"], hp["tb1"]
    abc, bbc = hp["abc"], hp["bbc"]
    row = hp["row"]
    psA, psB = hp["psA"], hp["psB"]
    oc, orow = hp["ones_col_b"], hp["ones_row"]
    for tc in range(ntok // 512):
        def xs(d):
            return x_fm[:, d * ntok + tc * 512: d * ntok + tc * 512 + 512]
        for d in range(nd):
            ctx.do("scalar", lambda e, d=d: e.activation(out=tb0[:], in_=xs(d), func=AF.Identity), sem=sem, wait=w())
            ctx.do("scalar", lambda e, d=d: e.activation(out=tb1[:], in_=xs(d), func=AF.Square), sem=sem, wait=w())
            ctx.do("tensor", lambda e, d=d: e.matmul(psA[0:1, 0:512], oc[:], tb0[:], start=(d == 0), stop=(d == nd - 1)), sem=sem, wait=w())
            ctx.do("tensor", lambda e, d=d: e.matmul(psA[0:1, 512:1024], oc[:], tb1[:], start=(d == 0), stop=(d == nd - 1)), sem=sem, wait=w())
        ctx.do("vector", lambda e: e.tensor_scalar(out=row[0:1, 0:512], in0=psA[0:1, 0:512], scalar1=1.0 / d_model, scalar2=None, op0=ALU.mult), sem=sem, wait=w())
        ctx.do("vector", lambda e: e.tensor_scalar(out=row[0:1, 512:1024], in0=psA[0:1, 512:1024], scalar1=1.0 / d_model, scalar2=None, op0=ALU.mult), sem=sem, wait=w())
        ctx.do("vector", lambda e: e.scalar_tensor_tensor(out=row[0:1, 512:1024], in0=row[0:1, 0:512], scalar=-1.0, in1=row[0:1, 512:1024], op0=ALU.mult if False else ALU.bypass, op1=ALU.bypass) if False else None, sem=None)
        ctx.do("vector", lambda e: e.tensor_tensor(out=row[0:1, 1024:1536], in0=row[0:1, 0:512], in1=row[0:1, 0:512], op=ALU.mult), sem=sem, wait=w())
        ctx.do("vector", lambda e: e.tensor_tensor(out=row[0:1, 512:1024], in0=row[0:1, 512:1024], in1=row[0:1, 1024:1536], op=ALU.subtract), sem=sem, wait=w())
        ctx.do("vector", lambda e: e.tensor_scalar(out=row[0:1, 512:1024], in0=row[0:1, 512:1024], scalar1=1e-5, scalar2=None, op0=ALU.add), sem=sem, wait=w())
        ctx.do("scalar", lambda e: e.activation(out=row[0:1, 512:1024], in_=row[0:1, 512:1024], func=AF.Ln), sem=sem, wait=w())
        ctx.do("scalar", lambda e: e.activation(out=row[0:1, 512:1024], in_=row[0:1, 512:1024], func=AF.Exp, scale=-0.5), sem=sem, wait=w())
        ctx.do("vector", lambda e: e.tensor_tensor(out=row[0:1, 1024:1536], in0=row[0:1, 0:512], in1=row[0:1, 512:1024], op=ALU.mult), sem=sem, wait=w())
        ctx.do("vector", lambda e: e.tensor_scalar(out=row[0:1, 1024:1536], in0=row[0:1, 1024:1536], scalar1=-1.0, scalar2=None, op0=ALU.mult), sem=sem, wait=w())
        ctx.do("tensor", lambda e: e.matmul(psB[:], orow[:], row[0:1, 512:1024], start=True, stop=True), sem=sem, wait=w())
        ctx.do("scalar", lambda e: e.activation(out=abc[:], in_=psB[:], func=AF.Copy), sem=sem, wait=w())
        ctx.do("tensor", lambda e: e.matmul(psB[:], orow[:], row[0:1, 1024:1536], start=True, stop=True), sem=sem, wait=w())
        ctx.do("scalar", lambda e: e.activation(out=bbc[:], in_=psB[:], func=AF.Copy), sem=sem, wait=w())
        for d in range(nd):
            ctx.do("vector", lambda e, d=d: e.tensor_tensor(out=tb0[:] if False else hp["tf"][:], in0=xs(d), in1=abc[:], op=ALU.mult), sem=sem, wait=w())
            ctx.do("vector", lambda e, d=d, tc=tc: e.tensor_tensor(out=out_fm[:, d * ntok + tc * 512: d * ntok + tc * 512 + 512], in0=hp["tf"][:], in1=bbc[:], op=ALU.add), sem=sem, wait=w())


def ln_helpers(ab, PSB):
    return {
        "tb0": ab.sb("tb0", [128, 512], BF16),
        "tb1": ab.sb("tb1", [128, 512], BF16),
        "tf": ab.sb("tf", [128, 512], F32),
        "abc": ab.sb("abc", [128, 512], F32),
        "bbc": ab.sb("bbc", [128, 512], F32),
        "row": ab.sb("row", [1, 1536], F32),
        "ones_col_b": ab.sb("ones_col_b", [128, 1], BF16),
        "ones_row": ab.sb("ones_row", [1, 128], F32),
        "psA": PSB[0:1, 0:1024],
        "psB": PSB[:, 1024:1536],
    }


def make_u1(nc, io, ab):
    ab.reset("u1_")
    x_d, lnx_d, PSB = io["x0"], io["lnx_c"], io["PSB"]
    x_fm = ab.sb("x_fm", [128, ND * TOK], F32)
    lnxfm = ab.sb("lnxfm", [128, ND * TOK], BF16)
    hp = ln_helpers(ab, PSB)

    def emit(ctx, sem):
        w = lambda: [(sem, ctx.val(sem))]
        ctx.do("gpsimd", lambda e: e.memset(hp["ones_col_b"][:], 1.0), sem=sem, wait=w())
        ctx.do("gpsimd", lambda e: e.memset(hp["ones_row"][:], 1.0), sem=sem, wait=w())
        ctx.burst("sync", [
            (lambda e, d=d: e.dma_start(out=x_fm[:, d * TOK:(d + 1) * TOK], in_=x_d[d * 128:(d + 1) * 128, :]))
            for d in range(ND)], sem)
        emit_ln_fm2(ctx, sem, lambda d, tc: x_fm[:, d * TOK + tc * 512: d * TOK + tc * 512 + 512], lnxfm, hp, [0, 1], TOK, TOK)
        ctx.burst("sync", [
            (lambda e, d=d: e.dma_start(out=lnx_d[d * 128:(d + 1) * 128, :], in_=lnxfm[:, d * TOK:(d + 1) * TOK]))
            for d in range(ND)], sem)
    return emit


def run_phases(nc, emits):
    st2 = contextlib.ExitStack()
    sem = st2.enter_context(nc.semaphore("s"))

    def prog(eng_name):
        def f(eng):
            ctx = Ctx(eng_name, eng)
            for em in emits:
                em(ctx, sem)
            ctx.wait("sync", sem, ctx.val(sem))
        return f

    with nc.Block() as block:
        for en in ENGS:
            getattr(block, en)(prog(en))
    st2.close()


def build_u1():
    nc = bass.Bass("TRN2", target_bir_lowering=False)
    io = {
        "x0": nc.dram_tensor("x", [D, TOK], F32, kind="ExternalInput").ap(),
        "lnx_c": nc.dram_tensor("lnx", [D, TOK], BF16, kind="ExternalOutput").ap(),
    }
    st = contextlib.ExitStack()
    io["PSA"] = st.enter_context(nc.psum_tensor("PSA", [128, 2048], F32))
    io["PSB"] = st.enter_context(nc.psum_tensor("PSB", [128, 2048], F32))
    ab = Arena(nc)
    run_phases(nc, [make_u1(nc, io, ab)])
    st.close()
    return nc


ATT_SCALE = float(HD) ** -0.5


def make_u2(nc, io, ab):
    ab.reset("u2_")
    lnx_tile = io["lnx_tile"]
    wqT_d, wkT_d, wvT_d = io["wqT"], io["wkT"], io["wvT"]
    bq_d, bk_d, bv_d = io["bq"], io["bk"], io["bv"]
    attn_d = io["attn_c"]
    PSA, PSB = io["PSA"], io["PSB"]
    sb = ab.sb
    wld_f = sb("wld_f", [128, 256], F32)
    wq_b = sb("wq_b", [128, ND * 256], BF16)
    wk_b = sb("wk_b", [128, ND * 256], BF16)
    wv_b = sb("wv_b", [128, ND * 256], BF16)
    bias_sb = sb("bias_sb", [128, 8], F32)
    bvbc = sb("bvbc", [128, 256], F32)
    lnx_sb = [sb(f"lnx_sb{i}", [128, ND * 512], BF16) for i in range(2)]
    q_fm = sb("q_fm", [128, HPC * S], BF16)
    k_fm = sb("k_fm", [128, HPC * S], BF16)
    v_sb = sb("v_sb", [128, (S // 128) * 260], BF16)
    pt_sb = sb("pt_sb", [128, 1024], BF16)
    atile2 = [sb(f"atile2_{i}", [128, 512], BF16) for i in range(2)]
    sq_sb = sb("sq_sb", [128, 2048], BF16)
    scal = sb("scal", [1, 16], F32)
    scal2 = sb("scal2", [1, 512], F32)
    negb = sb("negb", [128, 2], F32)
    ones_col = sb("ones_col", [128, 1], BF16)
    ones_row = sb("ones_row", [1, 128], F32)
    ident = sb("ident", [128, 128], BF16)
    pq_on = "w1q" in io
    if pq_on:
        pq_win = [sb(f"pq_win{i}", [128, 1024], F32) for i in range(2)]
        pq_ma = sb("pq_ma", [128, 1024], F32)
        pq_q = [sb(f"pq_q{i}", [128, 1024], BF16) for i in range(2)]
        pq_acc8 = sb("pq_acc8", [8, 2], F32)
        pq_ones8 = sb("pq_ones8", [8, 1], F32)
        pq_s4 = sb("pq_s4", [1, 4], F32)
        pq_thrc = sb("pq_thrc", [128, 4], F32)
    psS = PSA[:, 0:1024]
    psK2 = PSA[:, 1024:2048]
    psV = PSB[:, 0:1024]
    psT = PSB[:, 1024:1088].bitcast(BF16)
    psM = PSB[:, 1536:2048]
    NQC = S // 512

    def emit(ctx, sem):
        w = lambda: [(sem, ctx.val(sem))]

        def chain(which, fns, inc_last=1):
            if ctx.en == which:
                ctx.eng.wait_ge(sem, ctx.val(sem))
                for fn in fns:
                    ins = fn(ctx.eng)
                ins.then_inc(sem, inc_last)
            ctx._bump(sem, inc_last)

        build_identity(ctx, ident)
        ctx.do("gpsimd", lambda e: e.memset(ones_col[:], 1.0), sem=sem, wait=w())
        ctx.do("gpsimd", lambda e: e.memset(ones_row[:], 1.0), sem=sem, wait=w())
        ctx.do("gpsimd", lambda e: e.memset(v_sb[:], 0.0), sem=sem, wait=w())

        if pq_on:
            pq_in, pq_v, pq_out = io["pq_in"], io["pq_v"], io["pq_out"]
            ctx.do("gpsimd", lambda e: e.memset(pq_ones8[:], 1.0), sem=sem, wait=w())
            ctx.do("sync", lambda e: e.dma_start(out=pq_acc8[:, :], in_=io["wsum_all"][:, :]), sem=sem, inc=16, wait=w())
            ctx.do("tensor", lambda e: e.matmul(psM[0:1, 0:2], pq_ones8[:], pq_acc8[0:8, 0:2], start=True, stop=True), sem=sem, wait=w())
            ctx.do("vector", lambda e: e.tensor_scalar(out=pq_s4[0:1, 0:2], in0=psM[0:1, 0:2], scalar1=0.7 / (D * DFF0), scalar2=None, op0=ALU.mult), sem=sem, wait=w())
            ctx.do("vector", lambda e: e.tensor_scalar(out=pq_s4[0:1, 2:4], in0=pq_s4[0:1, 0:2], scalar1=-1.0, scalar2=None, op0=ALU.mult), sem=sem, wait=w())
            ctx.do("tensor", lambda e: e.matmul(psM[:, 0:4], ones_row[:], pq_s4[0:1, 0:4], start=True, stop=True), sem=sem, wait=w())
            ctx.do("scalar", lambda e: e.activation(out=pq_thrc[:], in_=psM[:, 0:4], func=AF.Copy), sem=sem, wait=w())
            pq_rounds = []
            for r in range(D // 128):
                for cc in range(DFF0 // 1024):
                    pq_rounds.append((io["w1T"][r * 128:(r + 1) * 128, cc * 1024:(cc + 1) * 1024],
                                      io["w1q"][r * 128:(r + 1) * 128, cc * 1024:(cc + 1) * 1024], 0))
            for r in range(DFF0 // 128):
                for cc in range(D // 1024):
                    pq_rounds.append((io["w2T"][r * 128:(r + 1) * 128, cc * 1024:(cc + 1) * 1024],
                                      io["w2q"][r * 128:(r + 1) * 128, cc * 1024:(cc + 1) * 1024], 1))
            pq_st = {"i": 0}

            def pq_step():
                i = pq_st["i"]
                if i >= len(pq_rounds):
                    return
                src, dst, tc_ = pq_rounds[i]
                wt_in = ([] if i < 2 else [(pq_v, 2 * (i - 1))]) + ([] if i == 0 else [(pq_in, 16 * i)])
                ctx.do("gpsimd", lambda e, src=src, i=i: e.dma_start(out=pq_win[i % 2][:], in_=src), sem=pq_in, inc=16, wait=wt_in)
                ctx.do("vector", lambda e, i=i, tc_=tc_: e.tensor_scalar(out=pq_ma[:], in0=pq_win[i % 2][:], scalar1=pq_thrc[:, tc_:tc_ + 1], scalar2=-1.0, op0=ALU.is_ge, op1=ALU.add), sem=pq_v, inc=1, wait=[(pq_in, 16 * (i + 1))])
                wt_q = [] if i < 2 else [(pq_out, 16 * (i - 1))]
                ctx.do("vector", lambda e, i=i, tc_=tc_: e.scalar_tensor_tensor(out=pq_q[i % 2][:], in0=pq_win[i % 2][:], scalar=pq_thrc[:, 2 + tc_:3 + tc_], in1=pq_ma[:], op0=ALU.is_ge, op1=ALU.add), sem=pq_v, inc=1, wait=wt_q)
                wt_o = [(pq_v, 2 * (i + 1))] + ([] if i == 0 else [(pq_out, 16 * i)])
                ctx.do("sync", lambda e, dst=dst, i=i: e.dma_start(out=dst, in_=pq_q[i % 2][:]), sem=pq_out, inc=16, wait=wt_o)
                pq_st["i"] += 1
        else:
            def pq_step():
                return
        for h in range(2):
            ctx.do("gpsimd", lambda e, h=h: e.memset(v_sb[:].rearrange("p (t c) -> p t c", c=260)[:, :, h * 129 + 128: h * 129 + 129], 1.0), sem=sem, wait=w())
        ctx.burst("gpsimd", [
            (lambda e, d=d, wT_d=wT_d, w_b=w_b: e.dma_start(out=w_b[:, d * 256:(d + 1) * 256], in_=wT_d[d * 128:(d + 1) * 128, :]))
            for d in range(ND) for (wT_d, w_b) in ((wqT_d, wq_b), (wkT_d, wk_b), (wvT_d, wv_b))], sem)
        ctx.burst("sync", [
            (lambda e, h=h: e.dma_start(out=bias_sb[:, h:h + 1], in_=bq_d[h * 128:(h + 1) * 128].unsqueeze(1)))
            for h in range(2)] + [
            (lambda e, h=h: e.dma_start(out=bias_sb[:, 2 + h:3 + h], in_=bk_d[h * 128:(h + 1) * 128].unsqueeze(1)))
            for h in range(2)], sem)
        ctx.do("vector", lambda e: e.tensor_scalar(out=bias_sb[:, 0:2], in0=bias_sb[:, 0:2], scalar1=ATT_SCALE, scalar2=None, op0=ALU.mult), sem=sem, wait=w())
        ctx.do("sync", lambda e: e.dma_start(out=bvbc[0:1, 0:256], in_=bv_d.unsqueeze(0)), sem=sem, inc=16, wait=w())
        ctx.do("tensor", lambda e: e.matmul(psM[:, 0:256], ones_row[:], bvbc[0:1, 0:256], start=True, stop=True), sem=sem, wait=w())
        ctx.do("scalar", lambda e: e.activation(out=bvbc[:], in_=psM[:, 0:256], func=AF.Copy), sem=sem, wait=w())

        for qc in range(NQC):
            buf = lnx_sb[qc % 2]
            ctx.burst("sync", [
                (lambda e, d=d, qc=qc, buf=buf: e.dma_start(out=buf[:, d * 512:(d + 1) * 512], in_=lnx_tile(d, qc)))
                for d in range(ND)], sem)
            for h in range(2):
                chain("tensor", [
                    (lambda e, h=h, d=d, buf=buf: e.matmul(psS[:, h * 512:(h + 1) * 512], wq_b[:, d * 256 + h * 128: d * 256 + h * 128 + 128], buf[:, d * 512:(d + 1) * 512], start=(d == 0), stop=(d == ND - 1)))
                    for d in range(ND)])
                chain("tensor", [
                    (lambda e, h=h, d=d, buf=buf: e.matmul(psK2[:, h * 512:(h + 1) * 512], wk_b[:, d * 256 + h * 128: d * 256 + h * 128 + 128], buf[:, d * 512:(d + 1) * 512], start=(d == 0), stop=(d == ND - 1)))
                    for d in range(ND)])
                ctx.do("scalar", lambda e, h=h, qc=qc: e.activation(out=q_fm[:, h * S + qc * 512: h * S + qc * 512 + 512], in_=psS[:, h * 512:(h + 1) * 512], func=AF.Identity, scale=ATT_SCALE, bias=bias_sb[:, h:h + 1]), sem=sem, wait=w())
                ctx.do("scalar", lambda e, h=h, qc=qc: e.activation(out=k_fm[:, h * S + qc * 512: h * S + qc * 512 + 512], in_=psK2[:, h * 512:(h + 1) * 512], func=AF.Identity, scale=1.0, bias=bias_sb[:, 2 + h:3 + h]), sem=sem, wait=w())
            for j in range(4):
                chain("tensor", [
                    (lambda e, j=j, d=d, buf=buf: e.matmul(psV[:, j * 256:(j + 1) * 256], buf[:, d * 512 + j * 128: d * 512 + j * 128 + 128], wv_b[:, d * 256:(d + 1) * 256], start=(d == 0), stop=(d == ND - 1)))
                    for d in range(ND)])
                ctx.do("vector", lambda e, j=j: e.tensor_add(out=psV[:, j * 256:(j + 1) * 256], in0=psV[:, j * 256:(j + 1) * 256], in1=bvbc[:]), sem=sem, wait=w())
                tt = qc * 4 + j
                for h in range(2):
                    ctx.do("vector", lambda e, j=j, h=h, tt=tt: e.tensor_copy(out=v_sb[:, tt * 260 + h * 129: tt * 260 + h * 129 + 128], in_=psV[:, j * 256 + h * 128: j * 256 + (h + 1) * 128]), sem=sem, wait=w())

        for h in range(2):
            for (srcb, outc) in ((q_fm, 0), (k_fm, 1)):
                for ch in range(S // 2048):
                    ctx.do("scalar", lambda e, h=h, ch=ch, srcb=srcb: e.activation(out=sq_sb[:], in_=srcb[:, h * S + ch * 2048: h * S + ch * 2048 + 2048], func=AF.Square), sem=sem, wait=w())
                    for sub in range(4):
                        ctx.do("tensor", lambda e, sub=sub: e.matmul(psM[0:1, 0:512], ones_col[:], sq_sb[:, sub * 512:(sub + 1) * 512], start=True, stop=True), sem=sem, wait=w())
                        first = (ch == 0 and sub == 0)
                        if first:
                            ctx.do("vector", lambda e, outc=outc: e.reduce_max(out=scal[0:1, 8 + outc:9 + outc], in_=psM[0:1, 0:512], axis=AX.X), sem=sem, wait=w())
                            ctx.do("vector", lambda e, outc=outc: e.tensor_copy(out=scal[0:1, outc:outc + 1], in_=scal[0:1, 8 + outc:9 + outc]), sem=sem, wait=w())
                        else:
                            ctx.do("vector", lambda e, outc=outc: e.reduce_max(out=scal[0:1, 8 + outc:9 + outc], in_=psM[0:1, 0:512], axis=AX.X), sem=sem, wait=w())
                            ctx.do("vector", lambda e, outc=outc: e.tensor_tensor(out=scal[0:1, outc:outc + 1], in0=scal[0:1, outc:outc + 1], in1=scal[0:1, 8 + outc:9 + outc], op=ALU.max), sem=sem, wait=w())
            ctx.do("scalar", lambda e: e.activation(out=scal[0:1, 2:3], in_=scal[0:1, 0:1], func=AF.Ln), sem=sem, wait=w())
            ctx.do("scalar", lambda e: e.activation(out=scal[0:1, 3:4], in_=scal[0:1, 1:2], func=AF.Ln), sem=sem, wait=w())
            ctx.do("vector", lambda e: e.tensor_add(out=scal[0:1, 4:5], in0=scal[0:1, 2:3], in1=scal[0:1, 3:4]), sem=sem, wait=w())
            ctx.do("scalar", lambda e: e.activation(out=scal[0:1, 5:6], in_=scal[0:1, 4:5], func=AF.Exp, scale=0.5), sem=sem, wait=w())
            ctx.do("vector", lambda e: e.tensor_scalar(out=scal[0:1, 6:7], in0=scal[0:1, 5:6], scalar1=-1.0, scalar2=None, op0=ALU.mult), sem=sem, wait=w())
            ctx.do("tensor", lambda e: e.matmul(psM[:, 0:1], ones_row[:], scal[0:1, 6:7], start=True, stop=True), sem=sem, wait=w())
            ctx.do("scalar", lambda e, h=h: e.activation(out=negb[:, h:h + 1], in_=psM[:, 0:1], func=AF.Copy), sem=sem, wait=w())

        bcast_f = sq_sb[:, 0:1024].bitcast(F32)
        sstS = io["sst"]
        outdn = []
        for h in range(2):
            for qc in range(NQC):
                kvt = 4 * (qc + 1)
                for pair in range(kvt // 2):
                    k0 = 2 * pair
                    chain("tensor", [
                        (lambda e, h=h, qc=qc, half=half: e.matmul(psS[:, half * 512:(half + 1) * 512], k_fm[:, h * S + (k0 + half) * 128: h * S + (k0 + half) * 128 + 128], q_fm[:, h * S + qc * 512: h * S + qc * 512 + 512], start=True, stop=True))
                        for half in range(2)])
                    ctx.do("scalar", lambda e, h=h: e.activation(out=pt_sb[:], in_=psS[:], func=AF.Exp, bias=negb[:, h:h + 1]), sem=sem, wait=w())
                    for half in range(2):
                        kt = k0 + half
                        if kt * 128 + 127 >= qc * 512:
                            ctx.do("gpsimd", lambda e, kt=kt, qc=qc, half=half: e.affine_select(
                                out=pt_sb[:, half * 512:(half + 1) * 512], in_=pt_sb[:, half * 512:(half + 1) * 512],
                                compare_op=ALU.is_ge, fill=0.0,
                                base=qc * 512 - kt * 128, pattern=[[1, 512]], channel_multiplier=-1), sem=sem, wait=w())
                    av = []
                    for half in range(2):
                        kt = k0 + half
                        av.append((lambda e, h=h, kt=kt, half=half, kvt=kvt: e.matmul(
                            psV[:, 0:512],
                            v_sb[:, kt * 260 + h * 129: kt * 260 + h * 129 + 128],
                            pt_sb[:, half * 512:(half + 1) * 512],
                            start=(kt == 0), stop=(kt == kvt - 1))))
                        av.append((lambda e, kt=kt, half=half, kvt=kvt: e.matmul(
                            psM[0:1, 0:512],
                            ones_col[:],
                            pt_sb[:, half * 512:(half + 1) * 512],
                            start=(kt == 0), stop=(kt == kvt - 1))))
                    chain("tensor", av)
                    pq_step()
                ctx.do("vector", lambda e: e.reciprocal(out=scal2[0:1, :], in_=psM[0:1, 0:512]), sem=sem, wait=w())
                ctx.do("tensor", lambda e: e.matmul(psK2[:, 0:512], ones_row[:], scal2[0:1, :], start=True, stop=True), sem=sem, wait=w())
                ctx.do("scalar", lambda e: e.activation(out=bcast_f[:], in_=psK2[:, 0:512], func=AF.Copy), sem=sem, wait=w())
                t = h * NQC + qc
                wt_a = w() if t < 2 else (w() + [(sstS, outdn[t - 2])])
                ctx.do("vector", lambda e, t=t: e.tensor_tensor(out=atile2[t % 2][:], in0=psV[:, 0:512], in1=bcast_f[:], op=ALU.mult), sem=sem, wait=wt_a)
                wt_d = w() + ([] if t == 0 else [(sstS, outdn[t - 1])])
                outdn.append(ctx.do("sync", lambda e, h=h, qc=qc, t=t: e.dma_start(out=attn_d[h * 128:(h + 1) * 128, qc * 512:(qc + 1) * 512], in_=atile2[t % 2][:]), sem=sstS, inc=16, wait=wt_d))

        if pq_on:
            while pq_st["i"] < len(pq_rounds):
                pq_step()
            for en in ("gpsimd", "sync", "vector"):
                ctx.wait(en, io["pq_out"], 16 * len(pq_rounds))
        for en in ("gpsimd", "sync"):
            ctx.wait(en, sstS, ctx.val(sstS))
        ctx.do("gpsimd", lambda e: e.nop(), sem=sem, wait=w())
    return emit


def build_u2():
    nc = bass.Bass("TRN2", target_bir_lowering=False)
    io = {
        "lnx": nc.dram_tensor("lnx", [D, S], BF16, kind="ExternalInput").ap(),
        "wqT": nc.dram_tensor("wqT", [D, HPC * HD], F32, kind="ExternalInput").ap(),
        "wkT": nc.dram_tensor("wkT", [D, HPC * HD], F32, kind="ExternalInput").ap(),
        "wvT": nc.dram_tensor("wvT", [D, HPC * HD], F32, kind="ExternalInput").ap(),
        "bq": nc.dram_tensor("bq", [HPC * HD], F32, kind="ExternalInput").ap(),
        "bk": nc.dram_tensor("bk", [HPC * HD], F32, kind="ExternalInput").ap(),
        "bv": nc.dram_tensor("bv", [HPC * HD], F32, kind="ExternalInput").ap(),
        "attn_c": nc.dram_tensor("attn", [HPC * HD, S], BF16, kind="ExternalOutput").ap(),
    }
    io["lnx_tile"] = lambda d, qc: io["lnx"][d * 128:(d + 1) * 128, qc * 512:(qc + 1) * 512]
    st = contextlib.ExitStack()
    io["PSA"] = st.enter_context(nc.psum_tensor("PSA", [128, 2048], F32))
    io["PSB"] = st.enter_context(nc.psum_tensor("PSB", [128, 2048], F32))
    io["sst"] = st.enter_context(nc.semaphore("sst"))
    ab = Arena(nc)
    run_phases(nc, [make_u2(nc, io, ab)])
    st.close()
    return nc


def make_u3(nc, io, ab):
    ab.reset("u3_")
    x_d = io["x0"]
    attn_tile = io.get("attn_tile")
    woT_d, bo_d = io["woT"], io["bo"]
    b1_d, b2_d = io["b1"], io["b2"]
    w1q_d, w2q_d = io["w1q"], io["w2q"]
    x2_d = io["x2"]
    PSA, PSB = io["PSA"], io["PSB"]
    sb = ab.sb
    x_fm = sb("x_fm", [128, ND * TOK], F32)
    A = sb("A", [128, ND * TOK], BF16)
    B = sb("B", [128, 32768], BF16)
    rbuf = [sb(f"rbuf{i}", [128, 2048], BF16) for i in range(2)]
    wf_sb = sb("wf_sb", [128, 1024], F32)
    bcol = sb("bcol", [128, 96], F32)
    if "wgq" in io:
        cin = [sb(f"cin{i}", [128, 512], I32) for i in range(2)]
        cout = [sb(f"cout{i}", [128, 512], BF16) for i in range(2)]
    hp = ln_helpers(ab, PSB)
    tf_sb = hp["tf"]
    psC = PSA[:, 0:512]

    def emit(ctx, sem):
        if True:
            w = lambda: [(sem, ctx.val(sem))]

            def chain(which, fns, inc_last=1, extra=()):
                if ctx.en == which:
                    ctx.eng.wait_ge(sem, ctx.val(sem))
                    for (s2, v2) in extra:
                        ctx.eng.wait_ge(s2, v2)
                    for fn in fns:
                        ins = fn(ctx.eng)
                    ins.then_inc(sem, inc_last)
                ctx._bump(sem, inc_last)

            cq_on = "wgq" in io
            if cq_on:
                pq_in, pq_v, pq_out = io["pq_in"], io["pq_v"], io["pq_out"]
                CQ = []
                for (srcT, dstT, rows, cols) in ((io["wgT"], io["wgq"], D, HID1), (io["wuT"], io["wuq"], D, HID1), (io["wdT"], io["wdq"], HID1, D)):
                    for r in range(rows // 128):
                        for c in range(cols // 512):
                            CQ.append((srcT[r * 128:(r + 1) * 128, c * 512:(c + 1) * 512], dstT[r * 128:(r + 1) * 128, c * 512:(c + 1) * 512]))
                cq_bases = {}
                cq_st = {"i": 0}

                def cq_step():
                    i = cq_st["i"]
                    if i >= len(CQ):
                        return
                    if i == 0:
                        cq_bases["in"] = ctx.val(pq_in)
                        cq_bases["v"] = ctx.val(pq_v)
                        cq_bases["out"] = ctx.val(pq_out)
                    bi, bv, bo = cq_bases["in"], cq_bases["v"], cq_bases["out"]
                    src, dst = CQ[i]
                    wt_in = ([] if i < 2 else [(pq_v, bv + (i - 1))]) + ([] if i == 0 else [(pq_in, bi + 16 * i)])
                    ctx.do("scalar", lambda e, src=src, i=i: e.dma_start(out=cin[i % 2][:], in_=src), sem=pq_in, inc=16, wait=wt_in)
                    wt_c = [(pq_in, bi + 16 * (i + 1))] + ([] if i < 2 else [(pq_out, bo + 16 * (i - 1))])
                    ctx.do("scalar", lambda e, i=i: e.activation(out=cout[i % 2][:], in_=cin[i % 2][:], func=AF.Copy, bias=-128.0), sem=pq_v, inc=1, wait=wt_c)
                    wt_o = [(pq_v, bv + i + 1)] + ([] if i == 0 else [(pq_out, bo + 16 * i)])
                    ctx.do("sync", lambda e, dst=dst, i=i: e.dma_start(out=dst, in_=cout[i % 2][:]), sem=pq_out, inc=16, wait=wt_o)
                    cq_st["i"] += 1
            else:
                def cq_step():
                    return

            ctx.do("gpsimd", lambda e: e.memset(hp["ones_col_b"][:], 1.0), sem=sem, wait=w())
            ctx.do("gpsimd", lambda e: e.memset(hp["ones_row"][:], 1.0), sem=sem, wait=w())
            ctx.burst("sync", [
                (lambda e, d=d: e.dma_start(out=x_fm[:, d * TOK:(d + 1) * TOK], in_=x_d[d * 128:(d + 1) * 128, :]))
                for d in range(ND)], sem)
            if io.get("attn_dyn") is not None:
                if ctx.en == "sync":
                    ctx.eng.wait_ge(sem, ctx.val(sem))
                    ins = io["attn_dyn"](ctx.eng, A)
                    ins.then_inc(sem, 16)
                ctx._bump(sem, 16)
            else:
                for d in range(ND):
                    ctx.do("sync", lambda e, d=d: e.dma_start(out=A[:, d * TOK:(d + 1) * TOK], in_=attn_tile(d)), sem=sem, inc=16, wait=w())
            ctx.burst("sync", [
                (lambda e, d=d: e.dma_start(out=bcol[:, d:d + 1], in_=bo_d[d * 128:(d + 1) * 128].unsqueeze(1)))
                for d in range(ND)] + [
                (lambda e, d=d: e.dma_start(out=bcol[:, 16 + d:17 + d], in_=b2_d[d * 128:(d + 1) * 128].unsqueeze(1)))
                for d in range(ND)] + [
                (lambda e, ff=ff: e.dma_start(out=bcol[:, 32 + ff:33 + ff], in_=b1_d[ff * 128:(ff + 1) * 128].unsqueeze(1)))
                for ff in range(64)], sem)
            ctx.burst("gpsimd", [
                (lambda e, dp=dp: e.dma_start(out=B[:, dp * 2048:(dp + 1) * 2048], in_=woT_d[dp * 128:(dp + 1) * 128, :]))
                for dp in range(ND)], sem)

            for ft in range(ND):
                for tc in range(2):
                    chain("tensor", [
                        (lambda e, dp=dp, ft=ft, tc=tc: e.matmul(PSA[:, tc * 512:(tc + 1) * 512], B[:, dp * 2048 + ft * 128: dp * 2048 + ft * 128 + 128], A[:, dp * TOK + tc * 512: dp * TOK + tc * 512 + 512], start=(dp == 0), stop=(dp == ND - 1)))
                        for dp in range(ND)])
                    cq_step()
                ctx.do("scalar", lambda e, ft=ft: e.activation(out=wf_sb[:, 0:1024], in_=PSA[:, 0:1024], func=AF.Identity, scale=1.0, bias=bcol[:, ft:ft + 1]), sem=sem, wait=w())
                ctx.do("vector", lambda e, ft=ft: e.tensor_add(out=x_fm[:, ft * TOK: ft * TOK + TOK], in0=x_fm[:, ft * TOK: ft * TOK + TOK], in1=wf_sb[:, 0:1024]), sem=sem, wait=w())

            emit_ln_fm2(ctx, sem, lambda d, tc: x_fm[:, d * TOK + tc * 512: d * TOK + tc * 512 + 512], A, hp, [0, 1], TOK, TOK)

            sstS = io["sst"]
            CH = []
            for half in range(2):
                for fb in range(16):
                    for dpg in range(4):
                        CH.append((w1q_d[dpg * 512:(dpg + 1) * 512, fb * 512:(fb + 1) * 512], 512))
                for ft in range(ND):
                    for sub in range(4):
                        CH.append((w2q_d[sub * 2048:(sub + 1) * 2048, ft * 128:(ft + 1) * 128], 128))
            ring = {"g": 0, "u": 0, "dn": [], "cons": []}

            def pump():
                g = ring["g"]
                if g >= len(CH):
                    return
                src, cdim = CH[g]
                wt = (w() if g < 2 else [(sem, ring["cons"][g - 2])]) + ([] if g == 0 else [(sstS, ring["dn"][g - 1])])
                ring["dn"].append(ctx.do("gpsimd", lambda e, src=src, cdim=cdim, g=g: e.dma_start(
                    out=rbuf[g % 2][:].rearrange("p (a c) -> p a c", c=cdim),
                    in_=src.rearrange("(a p) c -> p a c", p=128)), sem=sstS, inc=16, wait=wt))
                ring["g"] += 1

            def use():
                g = ring["u"]
                return rbuf[g % 2], [(sstS, ring["dn"][g])]

            def done_use():
                ring["cons"].append(ctx.val(sem))
                ring["u"] += 1
                pump()

            pump()
            pump()
            for half in range(2):
                hoff = half * 512
                for fb in range(16):
                    for dpg in range(4):
                        rb, ex = use()
                        for fs in range(4):
                            chain("tensor", [
                                (lambda e, i=i, fs=fs, rb=rb, dpg=dpg, hoff=hoff: e.matmul(
                                    PSA[:, fs * 512:(fs + 1) * 512],
                                    rb[:, i * 512 + fs * 128: i * 512 + fs * 128 + 128],
                                    A[:, (dpg * 4 + i) * TOK + hoff: (dpg * 4 + i) * TOK + hoff + 512],
                                    start=(dpg == 0 and i == 0), stop=(dpg == 3 and i == 3)))
                                for i in range(4)], extra=(ex if fs == 0 else ()))
                        done_use()
                        cq_step()
                        cq_step()
                    for fs in range(4):
                        ff = fb * 4 + fs
                        ctx.do("scalar", lambda e, ff=ff, fs=fs: e.activation(out=B[:, ff * 512:(ff + 1) * 512], in_=PSA[:, fs * 512:(fs + 1) * 512], func=AF.Gelu, bias=bcol[:, 32 + ff:33 + ff]), sem=sem, wait=w())
                for ft in range(ND):
                    for sub in range(4):
                        rb, ex = use()
                        chain("tensor", [
                            (lambda e, fi=fi, rb=rb, sub=sub, hoff=hoff: e.matmul(
                                psC,
                                rb[:, fi * 128:(fi + 1) * 128],
                                B[:, (sub * 16 + fi) * 512:(sub * 16 + fi + 1) * 512],
                                start=(sub == 0 and fi == 0), stop=(sub == 3 and fi == 15)))
                            for fi in range(16)], extra=ex)
                        done_use()
                        cq_step()
                        cq_step()
                    ctx.do("scalar", lambda e, ft=ft: e.activation(out=tf_sb[:], in_=psC, func=AF.Identity, scale=1.0, bias=bcol[:, 16 + ft:17 + ft]), sem=sem, wait=w())
                    ctx.do("vector", lambda e, ft=ft, hoff=hoff: e.tensor_add(out=x_fm[:, ft * TOK + hoff: ft * TOK + hoff + 512], in0=x_fm[:, ft * TOK + hoff: ft * TOK + hoff + 512], in1=tf_sb[:]), sem=sem, wait=w())
                    ctx.do("sync", lambda e, ft=ft, hoff=hoff: e.dma_start(out=x2_d[ft * 128:(ft + 1) * 128, hoff:hoff + 512], in_=x_fm[:, ft * TOK + hoff: ft * TOK + hoff + 512]), sem=sem, inc=16, wait=w())
            if cq_on:
                while cq_st["i"] < len(CQ):
                    cq_step()
                for en in ("gpsimd", "sync", "scalar"):
                    ctx.wait(en, io["pq_out"], cq_bases["out"] + 16 * len(CQ))
                ctx.do("gpsimd", lambda e: e.nop(), sem=sem, wait=w())
    return emit


def build_u3():
    nc = bass.Bass("TRN2", target_bir_lowering=False)
    io = {
        "x0": nc.dram_tensor("x", [D, TOK], F32, kind="ExternalInput").ap(),
        "attn": nc.dram_tensor("attn", [D, TOK], BF16, kind="ExternalInput").ap(),
        "woT": nc.dram_tensor("woT", [D, D], F32, kind="ExternalInput").ap(),
        "bo": nc.dram_tensor("bo", [D], F32, kind="ExternalInput").ap(),
        "w1q": nc.dram_tensor("w1q", [D, DFF0], BF16, kind="ExternalInput").ap(),
        "b1": nc.dram_tensor("b1", [DFF0], F32, kind="ExternalInput").ap(),
        "w2q": nc.dram_tensor("w2q", [DFF0, D], BF16, kind="ExternalInput").ap(),
        "b2": nc.dram_tensor("b2", [D], F32, kind="ExternalInput").ap(),
        "x2": nc.dram_tensor("x2", [D, TOK], F32, kind="ExternalOutput").ap(),
    }
    io["attn_tile"] = lambda d: io["attn"][d * 128:(d + 1) * 128, :]
    st = contextlib.ExitStack()
    io["PSA"] = st.enter_context(nc.psum_tensor("PSA", [128, 2048], F32))
    io["PSB"] = st.enter_context(nc.psum_tensor("PSB", [128, 2048], F32))
    io["sst"] = st.enter_context(nc.semaphore("sst"))
    ab = Arena(nc)
    run_phases(nc, [make_u3(nc, io, ab)])
    st.close()
    return nc


def emit_ln_fm2(ctx, sem, x_src, out_fm, hp, tc_list, in_stride, out_stride, nd=ND):
    d_model = nd * 128
    w = lambda: [(sem, ctx.val(sem))]
    tb0, tb1, abc, bbc, row = hp["tb0"], hp["tb1"], hp["abc"], hp["bbc"], hp["row"]
    psA, psB = hp["psA"], hp["psB"]
    oc, orow = hp["ones_col_b"], hp["ones_row"]
    for tc in tc_list:
        for d in range(nd):
            ctx.do("scalar", lambda e, d=d, tc=tc: e.activation(out=tb0[:], in_=x_src(d, tc), func=AF.Identity), sem=sem, wait=w())
            ctx.do("scalar", lambda e, d=d, tc=tc: e.activation(out=tb1[:], in_=x_src(d, tc), func=AF.Square), sem=sem, wait=w())
            ctx.do("tensor", lambda e, d=d: e.matmul(psA[0:1, 0:512], oc[:], tb0[:], start=(d == 0), stop=(d == nd - 1)), sem=sem, wait=w())
            ctx.do("tensor", lambda e, d=d: e.matmul(psA[0:1, 512:1024], oc[:], tb1[:], start=(d == 0), stop=(d == nd - 1)), sem=sem, wait=w())
        ctx.do("vector", lambda e: e.tensor_scalar(out=row[0:1, 0:512], in0=psA[0:1, 0:512], scalar1=1.0 / d_model, scalar2=None, op0=ALU.mult), sem=sem, wait=w())
        ctx.do("vector", lambda e: e.tensor_scalar(out=row[0:1, 512:1024], in0=psA[0:1, 512:1024], scalar1=1.0 / d_model, scalar2=None, op0=ALU.mult), sem=sem, wait=w())
        ctx.do("vector", lambda e: e.tensor_tensor(out=row[0:1, 1024:1536], in0=row[0:1, 0:512], in1=row[0:1, 0:512], op=ALU.mult), sem=sem, wait=w())
        ctx.do("vector", lambda e: e.tensor_tensor(out=row[0:1, 512:1024], in0=row[0:1, 512:1024], in1=row[0:1, 1024:1536], op=ALU.subtract), sem=sem, wait=w())
        ctx.do("vector", lambda e: e.tensor_scalar(out=row[0:1, 512:1024], in0=row[0:1, 512:1024], scalar1=1e-5, scalar2=None, op0=ALU.add), sem=sem, wait=w())
        ctx.do("scalar", lambda e: e.activation(out=row[0:1, 512:1024], in_=row[0:1, 512:1024], func=AF.Ln), sem=sem, wait=w())
        ctx.do("scalar", lambda e: e.activation(out=row[0:1, 512:1024], in_=row[0:1, 512:1024], func=AF.Exp, scale=-0.5), sem=sem, wait=w())
        ctx.do("vector", lambda e: e.tensor_tensor(out=row[0:1, 1024:1536], in0=row[0:1, 0:512], in1=row[0:1, 512:1024], op=ALU.mult), sem=sem, wait=w())
        ctx.do("vector", lambda e: e.tensor_scalar(out=row[0:1, 1024:1536], in0=row[0:1, 1024:1536], scalar1=-1.0, scalar2=None, op0=ALU.mult), sem=sem, wait=w())
        ctx.do("tensor", lambda e: e.matmul(psB[:, 0:512], orow[:], row[0:1, 512:1024], start=True, stop=True), sem=sem, wait=w())
        ctx.do("scalar", lambda e: e.activation(out=abc[:], in_=psB[:, 0:512], func=AF.Copy), sem=sem, wait=w())
        ctx.do("tensor", lambda e: e.matmul(psB[:, 0:512], orow[:], row[0:1, 1024:1536], start=True, stop=True), sem=sem, wait=w())
        ctx.do("scalar", lambda e: e.activation(out=bbc[:], in_=psB[:, 0:512], func=AF.Copy), sem=sem, wait=w())
        for d in range(nd):
            ctx.do("vector", lambda e, d=d, tc=tc: e.tensor_tensor(out=hp["tf"][:], in0=x_src(d, tc), in1=abc[:], op=ALU.mult), sem=sem, wait=w())
            ctx.do("vector", lambda e, d=d, tc=tc: e.tensor_tensor(out=out_fm[:, d * out_stride + tc * 512: d * out_stride + tc * 512 + 512], in0=hp["tf"][:], in1=bbc[:], op=ALU.add), sem=sem, wait=w())


def make_u5(nc, io, ab, mode, halo_mode="compute", skip_front=False):
    ab.reset("u5_")
    x_d = io["x_m"]
    ipT_d, cw_d, cb_d, alog_d, dtb_d = io["ipT"], io["cw"], io["cb"], io["alog"], io["dtb"]
    xh_d = io.get("xh")
    if mode == "b":
        dvec_d, nw_d, opT_d = io["dvec"], io["nw"], io["opT"]
        sf_d, phiT_d, mask_d = io.get("sf"), io.get("phiT"), io["maskr"]
        ws_scr = io["ws_scr"]
        x3_d = io["x3"]
    else:
        sfin_d = io["sfin"]
        phi_d = io["phi"]
    acs_scr = io["acs_scr"]
    zmask_d = io.get("zmask")
    xssm_scr = io["xssm_scr"]
    z_scr = io.get("z_scr")
    PSA, PSB = io["PSA"], io["PSB"]
    sb = ab.sb
    A32 = sb("A32", [128, 16384], BF16)
    XCH = sb("XCH", [128, 8192], F32)
    BIG1 = sb("BIG1", [128, 4096], F32)
    BIG2 = sb("BIG2", [128, 4096], BF16)
    BIG3 = sb("BIG3", [128, 4096], BF16)
    BIG4 = sb("BIG4", [128, 4096], BF16)
    y_tile = sb("y_tile", [128, 4096], BF16)
    hp = {
        "tb0": sb("tb0", [128, 512], BF16),
        "tb1": sb("tb1", [128, 512], BF16),
        "tf": sb("tf", [128, 512], F32),
        "abc": sb("abc", [128, 512], F32),
        "bbc": sb("bbc", [128, 512], F32),
        "row": sb("row", [1, 1536], F32),
        "ones_col_b": sb("ones_col_b", [128, 1], BF16),
        "ones_row": sb("ones_row", [1, 128], F32),
    }
    identb = sb("identb", [128, 128], BF16)
    identf = sb("identf", [128, 128], F32)
    xbcr = sb("xbcr", [128, 1028], BF16)
    xbs = sb("xbs", [128, 1024], BF16)
    BC = sb("BC", [128, 2 * TOK], BF16)
    dt_fm = sb("dt_fm", [64, TOK], F32)
    adisc = sb("adisc", [64, TOK], F32)
    acs = sb("acs", [64, TOK], F32)
    cw_sb = sb("cw_sb", [128, 136], F32)
    cb_sb = sb("cb_sb", [128, 34], F32)
    small = sb("small", [64, 16], F32)
    acsT = sb("acsT", [64, 64], F32)
    eacsT = sb("eacsT", [64, 64], F32)
    ad63 = sb("ad63", [64, 64], F32)
    w_sh = sb("w_sh", [64, 64], F32)
    dtT = sb("dtT", [64, 64], F32)
    btm = sb("btm", [64, 128], BF16)
    gt = sb("gt", [64, 64], BF16)
    er63 = sb("er63", [1, 64], F32)
    E63 = sb("E63", [128, 64], F32)
    D64 = sb("D64", [128, 64], F32)
    S_run = sb("S_run", [128, DIN], F32)
    st_tmp = sb("st_tmp", [128, 2048], F32)
    syb = sb("syb", [128, DIN], BF16)
    x2s = sb("x2s", [128, 512], F32)
    lnxh = sb("lnxh", [128, ND * 4], BF16)
    xht = sb("xht", [128, ND * 4], F32)
    rs_col = sb("rs_col", [128, 2], F32)
    ph8 = sb("ph8", [64, 32], F32)
    ws8 = sb("ws8", [8, 64], F32)
    wsrow = sb("wsrow", [1, 512], F32)
    zt = sb("zt", [128, 1024], BF16) if mode == "b" else None
    stag4 = sb("stag4", [128, 512], BF16)
    rawf3 = sb("rawf3", [128, 34 * 4], BF16)
    hal_all = sb("hal_all", [128, 34 * 4], BF16)
    zmask_sb = sb("zmask_sb", [128, 1], F32)
    hal3 = sb("hal3", [128, 8], BF16)

    psC = PSA[:, 0:512]
    psA_ln = PSB[0:1, 0:1024]
    psB_ln = PSB[:, 1024:1536]
    psT = PSB[:, 512:576].bitcast(BF16)
    psF = PSB[:, 576:704]
    hp["psA"] = psA_ln
    hp["psB"] = psB_ln

    def emit(ctx, sem):
        if True:
            w = lambda: [(sem, ctx.val(sem))]

            def chain(which, fns, inc_last=1):
                if ctx.en == which:
                    ctx.eng.wait_ge(sem, ctx.val(sem))
                    for fn in fns:
                        ins = fn(ctx.eng)
                    ins.then_inc(sem, inc_last)
                ctx._bump(sem, inc_last)

            def do(which, f2):
                ctx.do(which, f2, sem=sem, wait=w())

            def dma(f2, which="sync"):
                ctx.do(which, f2, sem=sem, inc=16, wait=w())

            do("gpsimd", lambda e: e.memset(hp["ones_col_b"][:], 1.0))
            do("gpsimd", lambda e: e.memset(hp["ones_row"][:], 1.0))
            do("gpsimd", lambda e: e.memset(identb[:], 1.0))
            do("gpsimd", lambda e: e.affine_select(out=identb[:], in_=identb[:], compare_op=ALU.is_equal, fill=0.0, base=0, pattern=[[-1, 128]], channel_multiplier=1))
            do("gpsimd", lambda e: e.memset(identf[:], 1.0))
            do("gpsimd", lambda e: e.affine_select(out=identf[:], in_=identf[:], compare_op=ALU.is_equal, fill=0.0, base=0, pattern=[[-1, 128]], channel_multiplier=1))
            for ct in range(34):
                dma(lambda e, ct=ct: e.dma_start(out=cw_sb[:, ct * 4:(ct + 1) * 4], in_=cw_d[ct * 128:(ct + 1) * 128, :]))
                dma(lambda e, ct=ct: e.dma_start(out=cb_sb[:, ct:ct + 1], in_=cb_d[ct * 128:(ct + 1) * 128].unsqueeze(1)))
            if halo_mode == "gather":
                dma(lambda e: e.dma_start(out=zmask_sb[:, :], in_=zmask_d[:, :]))
            dma(lambda e: e.dma_start(out=small[:, 0:1], in_=alog_d[:].unsqueeze(1)))
            dma(lambda e: e.dma_start(out=small[:, 2:3], in_=dtb_d[:].unsqueeze(1)))
            do("scalar", lambda e: e.activation(out=small[:, 1:2], in_=small[:, 0:1], func=AF.Exp))
            do("vector", lambda e: e.tensor_scalar(out=small[:, 1:2], in0=small[:, 1:2], scalar1=-1.0, scalar2=None, op0=ALU.mult))

            if mode == "b":
                if "phi_col" in io:
                    for j in range(NCORES):
                        dma(lambda e, j=j: e.dma_start(out=ph8[:, j:j + 1], in_=io["phi_col"](j)))
                else:
                    dma(lambda e: e.dma_start(out=ph8[:, 0:8], in_=phiT_d[:, :]))
                dma(lambda e: e.dma_start(out=ph8[:, 8:16], in_=mask_d[:, :]))
                do("vector", lambda e: e.tensor_tensor(out=ph8[:, 16:24], in0=ph8[:, 0:8], in1=ph8[:, 8:16], op=ALU.mult))
                do("gpsimd", lambda e: e.memset(ph8[:, 24:32], 0.0))
                for j in range(6, -1, -1):
                    do("vector", lambda e, j=j: e.tensor_tensor(out=ph8[:, 24 + j:25 + j], in0=ph8[:, 25 + j:26 + j], in1=ph8[:, 17 + j:18 + j], op=ALU.add))
                do("scalar", lambda e: e.activation(out=ph8[:, 24:32], in_=ph8[:, 24:32], func=AF.Exp))
                do("vector", lambda e: e.tensor_tensor(out=ph8[:, 24:32], in0=ph8[:, 24:32], in1=ph8[:, 8:16], op=ALU.mult))
                do("tensor", lambda e: e.transpose(psF[0:8, 0:64], ph8[0:64, 24:32], identf[0:64, 0:64]))
                do("scalar", lambda e: e.activation(out=ws8[:, :], in_=psF[0:8, 0:64], func=AF.Copy))
                dma(lambda e: e.dma_start(out=ws_scr[:, :], in_=ws8[:, :]))
                dma(lambda e: e.dma_start(out=er63[0:1, :].broadcast_to([1, 64]) if False else ph8[0:1, 0:0] if False else wsrow[0:1, :], in_=ws_scr.rearrange("a b -> (a b)").unsqueeze(0)))
                do("gpsimd", lambda e: e.memset(S_run[:], 0.0))
                sfb = [BIG1[:, 0:4096], XCH[:, 0:4096]]
                sf_src = lambda j: (io["sf_row"](j) if "sf_row" in io else sf_d[j * 128:(j + 1) * 128, :])
                sdn = [None] * NCORES
                cons = [None] * NCORES
                sdn[0] = ctx.do("sync", lambda e: e.dma_start(out=sfb[0], in_=sf_src(0)), sem=io["sst"], inc=16, wait=w())
                for j in range(NCORES):
                    if j + 1 < NCORES:
                        wt = (w() if j == 0 else [(sem, cons[j - 1])]) + [(io["sst"], sdn[j])]
                        sdn[j + 1] = ctx.do("sync", lambda e, j=j: e.dma_start(out=sfb[(j + 1) % 2], in_=sf_src(j + 1)), sem=io["sst"], inc=16, wait=wt)
                    do("tensor", lambda e, j=j: e.matmul(psC[:, 0:64], hp["ones_row"][:], wsrow[0:1, j * 64:(j + 1) * 64], start=True, stop=True))
                    do("scalar", lambda e: e.activation(out=E63[:], in_=psC[:, 0:64], func=AF.Copy))
                    for hh in range(2):
                        ctx.do("vector", lambda e, hh=hh, j=j: e.tensor_tensor(out=st_tmp[:], in0=sfb[j % 2][:, hh * 2048:(hh + 1) * 2048], in1=E63[:, hh * 32:(hh + 1) * 32].unsqueeze(2).broadcast_to([128, 32, 64]), op=ALU.mult), sem=sem, wait=w() + [(io["sst"], sdn[j])])
                        do("vector", lambda e, hh=hh: e.tensor_tensor(out=S_run[:, hh * 2048:(hh + 1) * 2048], in0=S_run[:, hh * 2048:(hh + 1) * 2048], in1=st_tmp[:], op=ALU.add))
                    cons[j] = ctx.val(sem)
            else:
                do("gpsimd", lambda e: e.memset(S_run[:], 0.0))

            for tc in range(2 if not skip_front else 0):
                for d in range(ND):
                    dma(lambda e, d=d, tc=tc: e.dma_start(out=XCH[:, d * 512:(d + 1) * 512], in_=x_d[d * 128:(d + 1) * 128, tc * 512:(tc + 1) * 512]))
                emit_ln_fm2(ctx, sem, lambda d, tcc: XCH[:, d * 512:(d + 1) * 512], A32, hp, [tc], 512, 1024)
            for d in range(ND if (halo_mode == "compute" and not skip_front) else 0):
                dma(lambda e, d=d: e.dma_start(out=xht[:, d * 4:d * 4 + 3], in_=xh_d[d * 128:(d + 1) * 128, :]))
            for d in range(ND if halo_mode == "compute" else 0):
                do("scalar", lambda e, d=d: e.activation(out=hp["tb0"][:, 0:3], in_=xht[:, d * 4:d * 4 + 3], func=AF.Identity))
                do("scalar", lambda e, d=d: e.activation(out=hp["tb1"][:, 0:3], in_=xht[:, d * 4:d * 4 + 3], func=AF.Square))
                do("tensor", lambda e, d=d: e.matmul(psA_ln[0:1, 0:3], hp["ones_col_b"][:], hp["tb0"][:, 0:3], start=(d == 0), stop=(d == ND - 1)))
                do("tensor", lambda e, d=d: e.matmul(psA_ln[0:1, 4:7], hp["ones_col_b"][:], hp["tb1"][:, 0:3], start=(d == 0), stop=(d == ND - 1)))
            R = hp["row"]
            if halo_mode == "compute":
              do("vector", lambda e: e.tensor_scalar(out=R[0:1, 0:3], in0=psA_ln[0:1, 0:3], scalar1=1.0 / D, scalar2=None, op0=ALU.mult))
              do("vector", lambda e: e.tensor_scalar(out=R[0:1, 4:7], in0=psA_ln[0:1, 4:7], scalar1=1.0 / D, scalar2=None, op0=ALU.mult))
              do("vector", lambda e: e.tensor_tensor(out=R[0:1, 8:11], in0=R[0:1, 0:3], in1=R[0:1, 0:3], op=ALU.mult))
              do("vector", lambda e: e.tensor_tensor(out=R[0:1, 4:7], in0=R[0:1, 4:7], in1=R[0:1, 8:11], op=ALU.subtract))
              do("vector", lambda e: e.tensor_scalar(out=R[0:1, 4:7], in0=R[0:1, 4:7], scalar1=1e-5, scalar2=None, op0=ALU.add))
              do("scalar", lambda e: e.activation(out=R[0:1, 4:7], in_=R[0:1, 4:7], func=AF.Ln))
              do("scalar", lambda e: e.activation(out=R[0:1, 4:7], in_=R[0:1, 4:7], func=AF.Exp, scale=-0.5))
              do("vector", lambda e: e.tensor_tensor(out=R[0:1, 8:11], in0=R[0:1, 0:3], in1=R[0:1, 4:7], op=ALU.mult))
              do("vector", lambda e: e.tensor_scalar(out=R[0:1, 8:11], in0=R[0:1, 8:11], scalar1=-1.0, scalar2=None, op0=ALU.mult))
              do("tensor", lambda e: e.matmul(psB_ln[:, 0:3], hp["ones_row"][:], R[0:1, 4:7], start=True, stop=True))
              do("scalar", lambda e: e.activation(out=hp["abc"][:, 0:3], in_=psB_ln[:, 0:3], func=AF.Copy))
              do("tensor", lambda e: e.matmul(psB_ln[:, 0:3], hp["ones_row"][:], R[0:1, 8:11], start=True, stop=True))
              do("scalar", lambda e: e.activation(out=hp["bbc"][:, 0:3], in_=psB_ln[:, 0:3], func=AF.Copy))
              for d in range(ND):
                  do("vector", lambda e, d=d: e.tensor_tensor(out=hp["tf"][:, 0:3], in0=xht[:, d * 4:d * 4 + 3], in1=hp["abc"][:, 0:3], op=ALU.mult))
                  do("vector", lambda e, d=d: e.tensor_tensor(out=lnxh[:, d * 4:d * 4 + 3], in0=hp["tf"][:, 0:3], in1=hp["bbc"][:, 0:3], op=ALU.add))

            sst = io["sst"]
            sring = {"n": 0, "base": ctx.val(sst)}
            ot_list = (list(range(32)) if mode == "b" else []) + ([] if skip_front else list(range(32, 67)))
            for oc in ot_list:
                ncols = 64 if oc == 66 else 128
                dma(lambda e, oc=oc, ncols=ncols: e.dma_start(
                    out=BIG2[:, 0:16 * ncols].rearrange("p (a c) -> p a c", c=ncols),
                    in_=ipT_d[:, oc * 128: oc * 128 + ncols].rearrange("(a p) c -> p a c", p=128)), which="gpsimd")
                for tc in range(2):
                    chain("tensor", [
                        (lambda e, d=d, tc=tc, ncols=ncols: e.matmul(PSA[0:ncols, tc * 512:tc * 512 + 512], BIG2[:, d * ncols:(d + 1) * ncols], A32[:, d * 1024 + tc * 512: d * 1024 + tc * 512 + 512], start=(d == 0), stop=(d == ND - 1)))
                        for d in range(ND)])
                if oc < 32:
                    do("scalar", lambda e: e.activation(out=zt[:, 0:512], in_=PSA[:, 0:512], func=AF.Silu))
                    do("scalar", lambda e: e.activation(out=zt[:, 512:1024], in_=PSA[:, 512:1024], func=AF.Silu))
                    for tt in range(8):
                        do("tensor", lambda e, tt=tt: e.transpose(psT[:], zt[:, tt * 128:(tt + 1) * 128], identb[:]))
                        slot = sring["n"] % 4
                        wv = [(sem, ctx.val(sem))]
                        if sring["n"] >= 4:
                            wv.append((sst, sring["base"] + (sring["n"] - 3) * 16))
                        ctx.do("vector", lambda e, slot=slot: e.tensor_copy(out=stag4[:, slot * 128:(slot + 1) * 128], in_=psT[:]), sem=sem, wait=wv)
                        ctx.do("gpsimd", lambda e, tt=tt, oc=oc, slot=slot: e.dma_start(out=z_scr[tt * 128:(tt + 1) * 128, oc * 128:(oc + 1) * 128], in_=stag4[:, slot * 128:(slot + 1) * 128]), sem=sst, inc=16, wait=[(sem, ctx.val(sem))])
                        sring["n"] += 1
                elif oc < 66:
                    ct = oc - 32
                    if halo_mode == "compute":
                        chain("tensor", [
                            (lambda e, d=d: e.matmul(PSB[0:128, 0:3], BIG2[:, d * 128:(d + 1) * 128], lnxh[:, d * 4:d * 4 + 3], start=(d == 0), stop=(d == ND - 1)))
                            for d in range(ND)])
                        do("scalar", lambda e: e.activation(out=xbcr[:, 0:3], in_=PSB[0:128, 0:3], func=AF.Copy))
                    else:
                        do("gpsimd", lambda e: e.memset(xbcr[:, 0:3], 0.0))
                    do("scalar", lambda e: e.activation(out=xbcr[:, 3:515], in_=PSA[:, 0:512], func=AF.Copy))
                    do("scalar", lambda e: e.activation(out=xbcr[:, 515:1027], in_=PSA[:, 512:1024], func=AF.Copy))
                    if halo_mode == "gather":
                        dma(lambda e, ct=ct: e.dma_start(out=io["xbce_c"][ct * 128:(ct + 1) * 128, 0:3], in_=xbcr[:, 1024:1027]))
                        do("vector", lambda e, ct=ct: e.tensor_copy(out=rawf3[:, ct * 4:ct * 4 + 3], in_=xbcr[:, 3:6]))
                    do("vector", lambda e, ct=ct: e.tensor_scalar(out=xbs[:], in0=xbcr[:, 0:1024], scalar1=cw_sb[:, ct * 4:ct * 4 + 1], scalar2=None, op0=ALU.mult))
                    for j in range(1, 4):
                        do("vector", lambda e, ct=ct, j=j: e.scalar_tensor_tensor(out=xbs[:], in0=xbcr[:, j:j + 1024], scalar=cw_sb[:, ct * 4 + j:ct * 4 + j + 1], in1=xbs[:], op0=ALU.mult, op1=ALU.add))
                    do("scalar", lambda e, ct=ct: e.activation(out=xbs[:], in_=xbs[:], func=AF.Silu, bias=cb_sb[:, ct:ct + 1]))
                    if ct < 32:
                        for tt in range(8):
                            do("tensor", lambda e, tt=tt: e.transpose(psT[:], xbs[:, tt * 128:(tt + 1) * 128], identb[:]))
                            slot = sring["n"] % 4
                            wv = [(sem, ctx.val(sem))]
                            if sring["n"] >= 4:
                                wv.append((sst, sring["base"] + (sring["n"] - 3) * 16))
                            ctx.do("vector", lambda e, slot=slot: e.tensor_copy(out=stag4[:, slot * 128:(slot + 1) * 128], in_=psT[:]), sem=sem, wait=wv)
                            ctx.do("gpsimd", lambda e, tt=tt, ct=ct, slot=slot: e.dma_start(out=xssm_scr[tt * 128:(tt + 1) * 128, ct * 128:(ct + 1) * 128], in_=stag4[:, slot * 128:(slot + 1) * 128]), sem=sst, inc=16, wait=[(sem, ctx.val(sem))])
                            sring["n"] += 1
                    else:
                        do("vector", lambda e, ct=ct: e.tensor_copy(out=BC[:, (ct - 32) * 1024:(ct - 31) * 1024], in_=xbs[:]))
                else:
                    do("scalar", lambda e: e.activation(out=dt_fm[:, 0:512], in_=PSA[0:64, 0:512], func=AF.Identity, bias=small[:, 2:3]))
                    do("scalar", lambda e: e.activation(out=dt_fm[:, 512:1024], in_=PSA[0:64, 512:1024], func=AF.Identity, bias=small[:, 2:3]))
                    do("scalar", lambda e: e.activation(out=adisc[:, 0:1024], in_=dt_fm[:], func=AF.Abs))
                    do("scalar", lambda e: e.activation(out=adisc[:, 0:1024], in_=adisc[:, 0:1024], func=AF.Exp, scale=-1.0))
                    do("vector", lambda e: e.tensor_scalar(out=adisc[:, 0:1024], in0=adisc[:, 0:1024], scalar1=1.0, scalar2=None, op0=ALU.add))
                    do("scalar", lambda e: e.activation(out=adisc[:, 0:1024], in_=adisc[:, 0:1024], func=AF.Ln))
                    do("vector", lambda e: e.tensor_scalar(out=dt_fm[:], in0=dt_fm[:], scalar1=0.0, scalar2=None, op0=ALU.max))
                    do("vector", lambda e: e.tensor_tensor(out=dt_fm[:], in0=dt_fm[:], in1=adisc[:, 0:1024], op=ALU.add))
            if halo_mode == "gather" and not skip_front:
                if io.get("mid_ag3") is not None and mode == "a":
                    io["mid_ag3"](ctx, sem)
                xbce_view = io["xbce_all"].rearrange("(j ct p) c -> j p ct c", p=128, ct=34)
                if ctx.en == "sync":
                    ctx.eng.wait_ge(sem, ctx.val(sem))
                    pprev = ctx.eng.snap((ctx.eng.partition_id() + 7) % 8)
                    ins = ctx.eng.dma_start(out=hal_all[:].rearrange("p (ct c) -> p ct c", c=4), in_=xbce_view[bass.ds(pprev, 1)].squeeze(0))
                    ins.then_inc(sem, 16)
                ctx._bump(sem, 16)
                do("vector", lambda e: e.tensor_scalar(out=hal_all[:], in0=hal_all[:], scalar1=zmask_sb[:, 0:1], scalar2=None, op0=ALU.mult))
                for ct in range(34):
                    do("vector", lambda e, ct=ct: e.tensor_copy(out=hal3[:, 0:3], in_=hal_all[:, ct * 4:ct * 4 + 3]))
                    do("vector", lambda e, ct=ct: e.tensor_copy(out=hal3[:, 3:6], in_=rawf3[:, ct * 4:ct * 4 + 3]))
                    do("vector", lambda e, ct=ct: e.tensor_scalar(out=xbs[:, 0:3], in0=hal3[:, 0:3], scalar1=cw_sb[:, ct * 4:ct * 4 + 1], scalar2=None, op0=ALU.mult))
                    for j in range(1, 4):
                        do("vector", lambda e, ct=ct, j=j: e.scalar_tensor_tensor(out=xbs[:, 0:3], in0=hal3[:, j:j + 3], scalar=cw_sb[:, ct * 4 + j:ct * 4 + j + 1], in1=xbs[:, 0:3], op0=ALU.mult, op1=ALU.add))
                    do("scalar", lambda e, ct=ct: e.activation(out=xbs[:, 0:3], in_=xbs[:, 0:3], func=AF.Silu, bias=cb_sb[:, ct:ct + 1]))
                    if ct < 32:
                        do("tensor", lambda e: e.transpose(psT[0:3, 0:128], xbs[:, 0:3], identb[:]))
                        do("scalar", lambda e: e.activation(out=xbcr[0:3, 0:128], in_=psT[0:3, 0:128], func=AF.Copy))
                        dma(lambda e, ct=ct: e.dma_start(out=xssm_scr[0:3, ct * 128:(ct + 1) * 128], in_=xbcr[0:3, 0:128]))
                    else:
                        do("vector", lambda e, ct=ct: e.tensor_copy(out=BC[:, (ct - 32) * 1024:(ct - 32) * 1024 + 3], in_=xbs[:, 0:3]))
            if not skip_front:
                do("vector", lambda e: e.tensor_scalar(out=adisc[:], in0=dt_fm[:], scalar1=small[:, 1:2], scalar2=None, op0=ALU.mult))
                for c in range(NCHUNK):
                    do("vector", lambda e, c=c: e.tensor_tensor_scan(out=acs[:, c * 64:(c + 1) * 64], data0=adisc[:, c * 64:(c + 1) * 64], data1=adisc[:, c * 64:(c + 1) * 64], initial=0.0, op0=ALU.add, op1=ALU.bypass))
            if mode == "a":
                do("vector", lambda e: e.reduce_sum(out=small[:, 3:4], in_=adisc[:], axis=AX.X))
                if io.get("phi_is_row"):
                    dma(lambda e: e.dma_start(out=phi_d[0:1, 0:64], in_=small[:, 3:4].transpose([1, 0]) if False else small[:, 3:4]))
                else:
                    dma(lambda e: e.dma_start(out=phi_d[:, :], in_=small[:, 3:4]))
            if mode == "b":
                dma(lambda e: e.dma_start(out=er63[0:1, :], in_=dvec_d[:].unsqueeze(0)))
                do("tensor", lambda e: e.matmul(psC[:, 0:64], hp["ones_row"][:], er63[0:1, :], start=True, stop=True))
                do("scalar", lambda e: e.activation(out=D64[:], in_=psC[:, 0:64], func=AF.Copy))

            ctx.wait("sync", sst, ctx.val(sst))
            ctx.wait("gpsimd", sst, ctx.val(sst))
            for c in range(NCHUNK):
                ac = acs[:, c * 64:(c + 1) * 64]
                dtc = dt_fm[:, c * 64:(c + 1) * 64]
                dma(lambda e, c=c: e.dma_start(out=acs_scr[:, :], in_=ac))
                do("tensor", lambda e: e.transpose(psF[0:64, 0:64], ac, identf[0:64, 0:64]))
                do("scalar", lambda e: e.activation(out=acsT[:], in_=psF[0:64, 0:64], func=AF.Copy))
                do("tensor", lambda e: e.transpose(psF[0:64, 0:64], dtc, identf[0:64, 0:64]))
                do("scalar", lambda e: e.activation(out=dtT[:], in_=psF[0:64, 0:64], func=AF.Copy))
                if mode == "b":
                    do("scalar", lambda e: e.activation(out=eacsT[:], in_=acsT[:], func=AF.Exp))
                do("vector", lambda e: e.tensor_tensor(out=ad63[:], in0=ac[:, 63:64].broadcast_to([64, 64]), in1=ac, op=ALU.subtract))
                do("scalar", lambda e: e.activation(out=ad63[:], in_=ad63[:], func=AF.Exp))
                do("vector", lambda e: e.tensor_tensor(out=ad63[:], in0=ad63[:], in1=dtc, op=ALU.mult))
                do("tensor", lambda e: e.transpose(psF[0:64, 0:64], ad63[:], identf[0:64, 0:64]))
                do("scalar", lambda e: e.activation(out=w_sh[:], in_=psF[0:64, 0:64], func=AF.Copy))
                do("scalar", lambda e: e.activation(out=small[:, 4:5], in_=ac[:, 63:64], func=AF.Exp))
                do("tensor", lambda e: e.transpose(psF[0:1, 0:64], small[:, 4:5], identf[0:64, 0:64]))
                do("scalar", lambda e: e.activation(out=er63[0:1, :], in_=psF[0:1, 0:64], func=AF.Copy))
                do("tensor", lambda e: e.matmul(psC[:, 0:64], hp["ones_row"][:], er63[0:1, :], start=True, stop=True))
                do("scalar", lambda e: e.activation(out=E63[:], in_=psC[:, 0:64], func=AF.Copy))
                do("tensor", lambda e, c=c: e.transpose(psT[0:64, 0:128], BC[:, c * 64:(c + 1) * 64], identb[:]))
                do("vector", lambda e: e.tensor_copy(out=btm[:], in_=psT[0:64, :]))
                if mode == "b":
                    do("tensor", lambda e, c=c: e.matmul(psF[0:64, 0:64], BC[:, c * 64:(c + 1) * 64], BC[:, 1024 + c * 64: 1024 + (c + 1) * 64], start=True, stop=True))
                    do("scalar", lambda e: e.activation(out=gt[:], in_=psF[0:64, 0:64], func=AF.Copy))
                    do("gpsimd", lambda e: e.affine_select(out=gt[:], in_=gt[:], compare_op=ALU.is_ge, fill=0.0, base=0, pattern=[[1, 64]], channel_multiplier=-1))
                    dma(lambda e: e.dma_start(out=BIG1[0:64, :].rearrange("s (h l) -> s h l", h=64), in_=AP(tensor=acs_scr.tensor, offset=0, ap=[[0, 64], [64, 64], [1, 64]])))
                    do("vector", lambda e: e.tensor_tensor(out=BIG1[0:64, :], in0=BIG1[0:64, :], in1=acsT[:].unsqueeze(2).broadcast_to([64, 64, 64]), op=ALU.subtract))
                    do("vector", lambda e: e.tensor_scalar(out=BIG1[0:64, :], in0=BIG1[0:64, :], scalar1=0.0, scalar2=None, op0=ALU.min))
                    do("scalar", lambda e: e.activation(out=BIG4[0:64, :], in_=BIG1[0:64, :], func=AF.Exp))
                    do("gpsimd", lambda e: e.tensor_tensor(out=BIG4[0:64, :], in0=BIG4[0:64, :], in1=gt[:].unsqueeze(1).broadcast_to([64, 64, 64]), op=ALU.mult))
                dma(lambda e, c=c: e.dma_start(out=BIG2[0:64, :], in_=xssm_scr[c * 64:(c + 1) * 64, :]))
                do("vector", lambda e: e.tensor_tensor(out=BIG2[0:64, :], in0=BIG2[0:64, :], in1=dtT[:].unsqueeze(2).broadcast_to([64, 64, 64]), op=ALU.mult))
                do("vector", lambda e: e.tensor_tensor(out=BIG3[0:64, :], in0=BIG2[0:64, :], in1=w_sh[:].unsqueeze(2).broadcast_to([64, 64, 64]), op=ALU.mult))
                if mode == "b":
                    do("scalar", lambda e: e.activation(out=syb[:], in_=S_run[:], func=AF.Copy))
                    for hh in range(2):
                        hc = hh * 2048
                        mms = []
                        for hl in range(32):
                            h = hh * 32 + hl
                            mms.append(lambda e, h=h, hl=hl: e.matmul(PSA[0:64, hl * 64:(hl + 1) * 64], BIG4[0:64, h * 64:(h + 1) * 64], BIG2[0:64, h * 64:(h + 1) * 64], start=True, stop=True))
                        chain("tensor", mms)
                        chain("tensor", [
                            (lambda e, i=i, c=c, hc=hc: e.matmul(PSB[0:64, i * 512:(i + 1) * 512], BC[:, 1024 + c * 64:1024 + (c + 1) * 64], syb[:, hc + i * 512: hc + (i + 1) * 512], start=True, stop=True))
                            for i in range(4)])
                        do("vector", lambda e, hh=hh: e.tensor_tensor(out=st_tmp[0:64, :], in0=PSB[0:64, 0:2048], in1=eacsT[:, hh * 32:(hh + 1) * 32].unsqueeze(2).broadcast_to([64, 32, 64]), op=ALU.mult))
                        do("vector", lambda e, c=c, hc=hc: e.tensor_tensor(out=y_tile[(c % 2) * 64:(c % 2) * 64 + 64, hc:hc + 2048], in0=PSA[0:64, 0:2048], in1=st_tmp[0:64, :], op=ALU.add))
                for hh in range(2):
                    hc = hh * 2048
                    chain("tensor", [
                        (lambda e, i=i, hc=hc: e.matmul(PSA[:, i * 512:(i + 1) * 512], btm[:], BIG3[0:64, hc + i * 512: hc + (i + 1) * 512], start=True, stop=True))
                        for i in range(4)])
                    do("vector", lambda e, hh=hh: e.tensor_tensor(out=st_tmp[:], in0=S_run[:, hc:hc + 2048], in1=E63[:, hh * 32:(hh + 1) * 32].unsqueeze(2).broadcast_to([128, 32, 64]), op=ALU.mult))
                    do("vector", lambda e, hc=hc: e.tensor_tensor(out=S_run[:, hc:hc + 2048], in0=st_tmp[:], in1=PSA[:, 0:2048], op=ALU.add))
                if mode == "b" and c % 2 == 1:
                    tt = c // 2
                    xsb = BIG1[:, 0:2048].bitcast(BF16)
                    dma(lambda e, tt=tt: e.dma_start(out=xsb, in_=xssm_scr[tt * 128:(tt + 1) * 128, :]))
                    do("vector", lambda e: e.tensor_tensor(out=xsb, in0=xsb, in1=D64[:].unsqueeze(2).broadcast_to([128, 64, 64]), op=ALU.mult))
                    do("vector", lambda e: e.tensor_tensor(out=y_tile[:], in0=y_tile[:], in1=xsb, op=ALU.add))
                    zl = BIG3[:, 0:4096]
                    dma(lambda e, tt=tt: e.dma_start(out=zl, in_=z_scr[tt * 128:(tt + 1) * 128, :]))
                    do("vector", lambda e: e.tensor_tensor(out=y_tile[:], in0=y_tile[:], in1=zl, op=ALU.mult))
                    sqb = BIG2[:, 0:4096]
                    do("scalar", lambda e: e.activation(out=sqb, in_=y_tile[:], func=AF.Square))
                    do("vector", lambda e: e.reduce_sum(out=rs_col[:, 0:1], in_=sqb, axis=AX.X))
                    do("vector", lambda e: e.tensor_scalar(out=rs_col[:, 0:1], in0=rs_col[:, 0:1], scalar1=1.0 / DIN, scalar2=1e-5, op0=ALU.mult, op1=ALU.add))
                    do("scalar", lambda e: e.activation(out=rs_col[:, 1:2], in_=rs_col[:, 0:1], func=AF.Ln))
                    do("scalar", lambda e: e.activation(out=rs_col[:, 1:2], in_=rs_col[:, 1:2], func=AF.Exp, scale=-0.5))
                    do("scalar", lambda e: e.activation(out=y_tile[:], in_=y_tile[:], func=AF.Identity, scale=rs_col[:, 1:2]))
                    nwb = BIG3[:, 0:4096]
                    dma(lambda e: e.dma_start(out=nwb.rearrange("p (a c) -> p a c", a=1), in_=AP(tensor=nw_d.tensor, offset=0, ap=[[0, 128], [0, 1], [1, 4096]])), which="gpsimd")
                    do("vector", lambda e: e.tensor_tensor(out=y_tile[:], in0=y_tile[:], in1=nwb, op=ALU.mult))
                    for k in range(32):
                        do("tensor", lambda e, k=k: e.transpose(psT[:], y_tile[:, k * 128:(k + 1) * 128], identb[:]))
                        do("vector", lambda e, k=k, tt=tt: e.tensor_copy(out=A32[:, k * 512 + (tt % 4) * 128: k * 512 + (tt % 4) * 128 + 128], in_=psT[:]))
                if mode == "b" and c in (7, 15):
                    tchunk = 0 if c == 7 else 1
                    for ft in range(ND):
                        dma(lambda e, ft=ft: e.dma_start(
                            out=BIG2[:].rearrange("p (a c2) -> p a c2", c2=128),
                            in_=opT_d[:, ft * 128:(ft + 1) * 128].rearrange("(a p) c2 -> p a c2", p=128)), which="gpsimd")
                        chain("tensor", [
                            (lambda e, k=k: e.matmul(psC[:], BIG2[:, k * 128:(k + 1) * 128], A32[:, k * 512:(k + 1) * 512], start=(k == 0), stop=(k == 31)))
                            for k in range(32)])
                        dma(lambda e, ft=ft, tchunk=tchunk: e.dma_start(out=x2s[:], in_=x_d[ft * 128:(ft + 1) * 128, tchunk * 512:(tchunk + 1) * 512]))
                        do("vector", lambda e: e.tensor_tensor(out=x2s[:], in0=x2s[:], in1=psC[:], op=ALU.add))
                        dma(lambda e, ft=ft, tchunk=tchunk: e.dma_start(out=x3_d[ft * 128:(ft + 1) * 128, tchunk * 512:(tchunk + 1) * 512], in_=x2s[:]))
            if mode == "a":
                dma(lambda e: e.dma_start(out=sfin_d[:, :], in_=S_run[:]))
    return emit


def build_u5(mode):
    nc = bass.Bass("TRN2", target_bir_lowering=False)
    io = {
        "x_m": nc.dram_tensor("x", [D, TOK], F32, kind="ExternalInput").ap(),
        "xh": nc.dram_tensor("xh", [D, 3], F32, kind="ExternalInput").ap(),
        "ipT": nc.dram_tensor("ipT", [D, DIP], F32, kind="ExternalInput").ap(),
        "cw": nc.dram_tensor("cw", [DXBC, 4], F32, kind="ExternalInput").ap(),
        "cb": nc.dram_tensor("cb", [DXBC], F32, kind="ExternalInput").ap(),
        "alog": nc.dram_tensor("alog", [NH_M], F32, kind="ExternalInput").ap(),
        "dtb": nc.dram_tensor("dtb", [NH_M], F32, kind="ExternalInput").ap(),
        "acs_scr": nc.dram_tensor("acs_scr", [NH_M, CHUNK], F32, kind="Internal").ap(),
        "xssm_scr": nc.dram_tensor("xssm_scr", [TOK, DIN], BF16, kind="Internal").ap(),
    }
    if mode == "b":
        io["dvec"] = nc.dram_tensor("dvec", [NH_M], F32, kind="ExternalInput").ap()
        io["nw"] = nc.dram_tensor("nw", [DIN], F32, kind="ExternalInput").ap()
        io["opT"] = nc.dram_tensor("opT", [DIN, D], F32, kind="ExternalInput").ap()
        io["sf"] = nc.dram_tensor("sf", [NCORES * 128, DIN], F32, kind="ExternalInput").ap()
        io["phiT"] = nc.dram_tensor("phiT", [NH_M, NCORES], F32, kind="ExternalInput").ap()
        io["maskr"] = nc.dram_tensor("maskr", [NH_M, NCORES], F32, kind="ExternalInput").ap()
        io["ws_scr"] = nc.dram_tensor("ws_scr", [8, 64], F32, kind="Internal").ap()
        io["x3"] = nc.dram_tensor("x3", [D, TOK], F32, kind="ExternalOutput").ap()
        io["z_scr"] = nc.dram_tensor("z_scr", [TOK, DIN], BF16, kind="Internal").ap()
    else:
        io["sfin"] = nc.dram_tensor("sfin", [128, DIN], F32, kind="ExternalOutput").ap()
        io["phi"] = nc.dram_tensor("phi", [NH_M, 1], F32, kind="ExternalOutput").ap()
    st = contextlib.ExitStack()
    io["PSA"] = st.enter_context(nc.psum_tensor("PSA", [128, 2048], F32))
    io["PSB"] = st.enter_context(nc.psum_tensor("PSB", [128, 2048], F32))
    io["sst"] = st.enter_context(nc.semaphore("sst"))
    ab = Arena(nc)
    run_phases(nc, [make_u5(nc, io, ab, mode)])
    st.close()
    return nc


NF1 = HID1 // 128


def make_u6(nc, io, ab):
    ab.reset("u6_")
    x_d = io["x_f"]
    wgq_d, wuq_d, wdq_d = io["wgq"], io["wuq"], io["wdq"]
    sg_d, su_d, sd_d = io["sg"], io["su"], io["sd"]
    x4_d = io["x4"]
    PSA, PSB = io["PSA"], io["PSB"]
    sb = ab.sb
    A = sb("A", [128, ND * TOK], BF16)
    GU = sb("GU", [128, NF1 * TOK], BF16)
    XCH = GU[:, 0:16384].bitcast(F32)
    wcol_b = [sb(f"wcol_b{i}", [128, 2048], BF16) for i in range(2)]
    wdcol_b = [sb(f"wdcol_b{i}", [128, NF1 * 128], BF16) for i in range(2)]
    gt_sb = sb("gt_sb", [128, 512], BF16)
    tf2 = sb("tf2", [128, 512], F32)
    scol = sb("scol", [128, 2 * NF1 + ND], F32)
    x2s = sb("x2s", [128, 512], F32)
    hp = ln_helpers(ab, PSB)
    psC = PSA[:, 0:512]
    psC2 = PSA[:, 512:1024]

    def emit(ctx, sem):
        if True:
            w = lambda: [(sem, ctx.val(sem))]

            def chain(which, fns, inc_last=1, extra=()):
                if ctx.en == which:
                    ctx.eng.wait_ge(sem, ctx.val(sem))
                    for (s2, v2) in extra:
                        ctx.eng.wait_ge(s2, v2)
                    for fn in fns:
                        ins = fn(ctx.eng)
                    ins.then_inc(sem, inc_last)
                ctx._bump(sem, inc_last)

            def do(which, f2):
                ctx.do(which, f2, sem=sem, wait=w())

            def dma(f2, which="sync"):
                ctx.do(which, f2, sem=sem, inc=16, wait=w())

            do("gpsimd", lambda e: e.memset(hp["ones_col_b"][:], 1.0))
            do("gpsimd", lambda e: e.memset(hp["ones_row"][:], 1.0))
            ctx.burst("sync", [
                (lambda e, ff=ff: e.dma_start(out=scol[:, ff:ff + 1], in_=sg_d[ff * 128:(ff + 1) * 128].unsqueeze(1)))
                for ff in range(NF1)] + [
                (lambda e, ff=ff: e.dma_start(out=scol[:, NF1 + ff:NF1 + ff + 1], in_=su_d[ff * 128:(ff + 1) * 128].unsqueeze(1)))
                for ff in range(NF1)] + [
                (lambda e, ft=ft: e.dma_start(out=scol[:, 2 * NF1 + ft:2 * NF1 + ft + 1], in_=sd_d[ft * 128:(ft + 1) * 128].unsqueeze(1)))
                for ft in range(ND)], sem)
            for tc in range(2):
                for d in range(ND):
                    dma(lambda e, d=d, tc=tc: e.dma_start(out=XCH[:, d * 512:(d + 1) * 512], in_=x_d[d * 128:(d + 1) * 128, tc * 512:(tc + 1) * 512]))
                emit_ln_fm2(ctx, sem, lambda d, tcc: XCH[:, d * 512:(d + 1) * 512], A, hp, [tc], 512, 1024)

            sstS = io["sst"]
            CH = []
            for ff in range(NF1):
                CH.append(("gu", wgq_d[:, ff * 128:(ff + 1) * 128]))
                CH.append(("gu", wuq_d[:, ff * 128:(ff + 1) * 128]))
            for ft in range(ND):
                CH.append(("d", wdq_d[:, ft * 128:(ft + 1) * 128]))
            ring = {"g": 0, "u": 0, "dn": [], "cons": []}

            def pump():
                g = ring["g"]
                if g >= len(CH):
                    return
                kind, src = CH[g]
                buf = wcol_b[g % 2] if kind == "gu" else wdcol_b[g % 2]
                wt = (w() if g < 2 else [(sem, ring["cons"][g - 2])]) + ([] if g == 0 else [(sstS, ring["dn"][g - 1])])
                ring["dn"].append(ctx.do("gpsimd", lambda e, src=src, buf=buf, g=g: e.dma_start(
                    out=buf[:].rearrange("p (a c) -> p a c", c=128),
                    in_=src.rearrange("(a p) c -> p a c", p=128)), sem=sstS, inc=16, wait=wt))
                ring["g"] += 1

            def use():
                g = ring["u"]
                kind = CH[g][0]
                buf = wcol_b[g % 2] if kind == "gu" else wdcol_b[g % 2]
                return buf, [(sstS, ring["dn"][g])]

            def done_use():
                ring["cons"].append(ctx.val(sem))
                ring["u"] += 1
                pump()

            pump()
            pump()
            for ff in range(NF1):
                for (sc_off, _) in ((0, 0), (NF1, 0)):
                    rb, ex = use()
                    for tc in range(2):
                        chain("tensor", [
                            (lambda e, d=d, tc=tc, rb=rb: e.matmul(psC if tc == 0 else psC2, rb[:, d * 128:(d + 1) * 128], A[:, d * 1024 + tc * 512: d * 1024 + tc * 512 + 512], start=(d == 0), stop=(d == ND - 1)))
                            for d in range(ND)], extra=(ex if tc == 0 else ()))
                    done_use()
                    if sc_off == 0:
                        for tc in range(2):
                            do("scalar", lambda e, ff=ff, tc=tc: e.activation(out=GU[:, ff * 1024 + tc * 512: ff * 1024 + tc * 512 + 512], in_=(psC if tc == 0 else psC2), func=AF.Silu, scale=scol[:, ff:ff + 1]))
                    else:
                        for tc in range(2):
                            do("scalar", lambda e, ff=ff, tc=tc: e.activation(out=gt_sb[:], in_=(psC if tc == 0 else psC2), func=AF.Identity, scale=scol[:, NF1 + ff:NF1 + ff + 1]))
                            do("vector", lambda e, ff=ff, tc=tc: e.tensor_tensor(out=GU[:, ff * 1024 + tc * 512: ff * 1024 + tc * 512 + 512], in0=GU[:, ff * 1024 + tc * 512: ff * 1024 + tc * 512 + 512], in1=gt_sb[:], op=ALU.mult))
            for ft in range(ND):
                rb, ex = use()
                for tc in range(2):
                    chain("tensor", [
                        (lambda e, f=f2, tc=tc, rb=rb: e.matmul(psC if tc == 0 else psC2, rb[:, f * 128:(f + 1) * 128], GU[:, f * 1024 + tc * 512: f * 1024 + tc * 512 + 512], start=(f == 0), stop=(f == NF1 - 1)))
                        for f2 in range(NF1)], extra=(ex if tc == 0 else ()))
                done_use()
                for tc in range(2):
                    do("scalar", lambda e, ft=ft, tc=tc: e.activation(out=tf2[:], in_=(psC if tc == 0 else psC2), func=AF.Identity, scale=scol[:, 2 * NF1 + ft:2 * NF1 + ft + 1]))
                    dma(lambda e, ft=ft, tc=tc: e.dma_start(out=x2s[:], in_=x_d[ft * 128:(ft + 1) * 128, tc * 512:(tc + 1) * 512]))
                    do("vector", lambda e: e.tensor_tensor(out=x2s[:], in0=x2s[:], in1=tf2[:], op=ALU.add))
                    dma(lambda e, ft=ft, tc=tc: e.dma_start(out=x4_d[ft * 128:(ft + 1) * 128, tc * 512:(tc + 1) * 512], in_=x2s[:]))
    return emit


def build_u6():
    nc = bass.Bass("TRN2", target_bir_lowering=False)
    io = {
        "x_f": nc.dram_tensor("x", [D, TOK], F32, kind="ExternalInput").ap(),
        "wgq": nc.dram_tensor("wgq", [D, HID1], BF16, kind="ExternalInput").ap(),
        "wuq": nc.dram_tensor("wuq", [D, HID1], BF16, kind="ExternalInput").ap(),
        "wdq": nc.dram_tensor("wdq", [HID1, D], BF16, kind="ExternalInput").ap(),
        "sg": nc.dram_tensor("sg", [HID1], F32, kind="ExternalInput").ap(),
        "su": nc.dram_tensor("su", [HID1], F32, kind="ExternalInput").ap(),
        "sd": nc.dram_tensor("sd", [D], F32, kind="ExternalInput").ap(),
        "x4": nc.dram_tensor("x4", [D, TOK], F32, kind="ExternalOutput").ap(),
    }
    st = contextlib.ExitStack()
    io["PSA"] = st.enter_context(nc.psum_tensor("PSA", [128, 2048], F32))
    io["PSB"] = st.enter_context(nc.psum_tensor("PSB", [128, 2048], F32))
    io["sst"] = st.enter_context(nc.semaphore("sst"))
    ab = Arena(nc)
    run_phases(nc, [make_u6(nc, io, ab)])
    st.close()
    return nc


def build_full():
    nc = bass.Bass("TRN2", target_bir_lowering=False, detect_race_conditions=False)
    io = {}
    T = nc.dram_tensor
    io["x0"] = T("x0", [D, TOK], F32, kind="ExternalInput").ap()
    io["wqT"] = T("wqT", [D, HPC * HD], F32, kind="ExternalInput").ap()
    io["wkT"] = T("wkT", [D, HPC * HD], F32, kind="ExternalInput").ap()
    io["wvT"] = T("wvT", [D, HPC * HD], F32, kind="ExternalInput").ap()
    io["bq"] = T("bq", [HPC * HD], F32, kind="ExternalInput").ap()
    io["bk"] = T("bk", [HPC * HD], F32, kind="ExternalInput").ap()
    io["bv"] = T("bv", [HPC * HD], F32, kind="ExternalInput").ap()
    io["woT"] = T("woT", [D, D], F32, kind="ExternalInput").ap()
    io["bo"] = T("bo", [D], F32, kind="ExternalInput").ap()
    io["w1T"] = T("w1T", [D, DFF0], F32, kind="ExternalInput").ap()
    io["b1"] = T("b1", [DFF0], F32, kind="ExternalInput").ap()
    io["w2T"] = T("w2T", [DFF0, D], F32, kind="ExternalInput").ap()
    io["b2"] = T("b2", [D], F32, kind="ExternalInput").ap()
    io["ipT"] = T("ipT", [D, DIP], F32, kind="ExternalInput").ap()
    io["cw"] = T("cw", [DXBC, 4], F32, kind="ExternalInput").ap()
    io["cb"] = T("cb", [DXBC], F32, kind="ExternalInput").ap()
    io["alog"] = T("alog", [NH_M], F32, kind="ExternalInput").ap()
    io["dtb"] = T("dtb", [NH_M], F32, kind="ExternalInput").ap()
    io["dvec"] = T("dvec", [NH_M], F32, kind="ExternalInput").ap()
    io["nw"] = T("nw", [DIN], F32, kind="ExternalInput").ap()
    io["opT"] = T("opT", [DIN, D], F32, kind="ExternalInput").ap()
    io["maskr"] = T("maskr", [NH_M, NCORES], F32, kind="ExternalInput").ap()
    io["zmask"] = T("zmask", [128, 1], F32, kind="ExternalInput").ap()
    io["w1s"] = T("w1s", [D, S // NCORES], F32, kind="ExternalInput").ap()
    io["w2s"] = T("w2s", [DFF0, D // NCORES], F32, kind="ExternalInput").ap()
    io["wgT"] = T("wgT", [D, HID1], I32, kind="ExternalInput").ap()
    io["wuT"] = T("wuT", [D, HID1], I32, kind="ExternalInput").ap()
    io["wdT"] = T("wdT", [HID1, D], I32, kind="ExternalInput").ap()
    io["sg"] = T("sg", [HID1], F32, kind="ExternalInput").ap()
    io["su"] = T("su", [HID1], F32, kind="ExternalInput").ap()
    io["sd"] = T("sd", [D], F32, kind="ExternalInput").ap()
    io["x4"] = T("x4", [D, TOK], F32, kind="ExternalOutput").ap()
    if _FDBG:
        d_lnx = T("d_lnx", [D, TOK], BF16, kind="ExternalOutput").ap()
        d_attn = T("d_attn", [HPC * HD, S], BF16, kind="ExternalOutput").ap()
        d_x2 = T("d_x2", [D, TOK], F32, kind="ExternalOutput").ap()
        d_x3 = T("d_x3", [D, TOK], F32, kind="ExternalOutput").ap()
        d_sphi = T("d_sphi", [132, DIN], F32, kind="ExternalOutput").ap()
        io["d_A"] = T("d_A", [D, TOK], BF16, kind="ExternalOutput").ap()
        io["d_x1"] = T("d_x1", [D, TOK], F32, kind="ExternalOutput").ap()
        io["d_thr"] = T("d_thr", [128, 4], F32, kind="ExternalOutput").ap()
        d_ws = T("d_ws", [16, 2], F32, kind="ExternalOutput").ap()
    io["lnx_c"] = T("lnx_c", [D, TOK], BF16, kind="Internal").ap()
    lnx_all = T("lnx_all", [NCORES * D, TOK], BF16, kind="Internal").ap()
    io["wsum_c"] = T("wsum_c", [1, 2], F32, kind="Internal").ap()
    io["wsum_all"] = T("wsum_all", [NCORES, 2], F32, kind="Internal").ap()
    io["attn_c"] = T("attn_c", [HPC * HD, S], BF16, kind="Internal").ap()
    attn_all = T("attn_all", [D, S], BF16, kind="Internal").ap()
    io["xbce_c"] = T("xbce_c", [DXBC, 4], BF16, kind="Internal").ap()
    io["xbce_all"] = T("xbce_all", [NCORES * DXBC, 4], BF16, kind="Internal").ap()
    sphi_c = T("sphi_c", [132, DIN], F32, kind="Internal").ap()
    sphi_all = T("sphi_all", [NCORES * 132, DIN], F32, kind="Internal").ap()
    io["x2"] = T("x2_scr", [D, TOK], F32, kind="Internal").ap()
    io["x3"] = T("x3_scr", [D, TOK], F32, kind="Internal").ap()
    io["w1q"] = T("w1q_scr", [D, DFF0], BF16, kind="Internal").ap()
    io["w2q"] = T("w2q_scr", [DFF0, D], BF16, kind="Internal").ap()
    io["wgq"] = T("wgq_scr", [D, HID1], BF16, kind="Internal").ap()
    io["wuq"] = T("wuq_scr", [D, HID1], BF16, kind="Internal").ap()
    io["wdq"] = T("wdq_scr", [HID1, D], BF16, kind="Internal").ap()
    io["acs_scr"] = T("acs_scr", [NH_M, CHUNK], F32, kind="Internal").ap()
    io["xssm_scr"] = T("xssm_scr", [TOK, DIN], BF16, kind="Internal").ap()
    io["z_scr"] = T("z_scr", [TOK, DIN], BF16, kind="Internal").ap()
    io["ws_scr"] = T("ws_scr", [8, 64], F32, kind="Internal").ap()

    st = contextlib.ExitStack()
    io["PSA"] = st.enter_context(nc.psum_tensor("PSA", [128, 2048], F32))
    io["PSB"] = st.enter_context(nc.psum_tensor("PSB", [128, 2048], F32))
    io["sst"] = st.enter_context(nc.semaphore("sst"))
    io["pq_in"] = st.enter_context(nc.semaphore("pq_in"))
    io["pq_v"] = st.enter_context(nc.semaphore("pq_v"))
    io["pq_out"] = st.enter_context(nc.semaphore("pq_out"))

    io["lnx_tile"] = lambda d, qc: lnx_all[(qc // 2) * D + d * 128: (qc // 2) * D + (d + 1) * 128, (qc % 2) * 512:(qc % 2) * 512 + 512]

    def attn_dyn(eng, A):
        off = eng.snap(eng.partition_id() * TOK)
        src_ap = attn_all[:, bass.ds(off, TOK)].rearrange("(dp p) t -> p dp t", p=128)
        return eng.dma_start(out=A[:, 0:ND * TOK].rearrange("p (dp t) -> p dp t", t=TOK), in_=src_ap)
    io["attn_dyn"] = attn_dyn
    io["wsum_mode"] = "fused"
    io["x_m"] = io["x2"]
    io["x_f"] = io["x3"]
    io["sfin"] = sphi_c[0:128, :]
    io["phi"] = sphi_c[128:129, 0:64]
    io["phi_is_row"] = True
    io["sf_row"] = lambda j: sphi_all[j * 132:j * 132 + 128, :]
    io["phi_col"] = lambda j: sphi_all[j * 132 + 128:j * 132 + 129, 0:64]

    def prevoff_load(ctx):
        holder = {}

        def fn(ct):
            if "base" not in holder:
                cid = ctx.eng.partition_id()
                holder["base"] = ctx.eng.snap(((cid + 7) % 8) * DXBC)
            return holder["base"] + ct * 128
        return (None, fn)
    io["prevoff_load"] = prevoff_load

    def cc(ctx, sem, kind, ins_ap, outs_ap):
        ctx.do("gpsimd", lambda e: e.collective_compute(
            kind, ALU.bypass, replica_groups=[list(range(NCORES))],
            ins=[ins_ap.opt()], outs=[outs_ap.opt()]), sem=sem, inc=1,
            wait=[(sem, ctx.val(sem))])

    def mid_ag3(ctx, sem):
        cc(ctx, sem, "AllGather", io["xbce_c"], io["xbce_all"])
    io["mid_ag3"] = mid_ag3

    ab = Arena(nc)
    em_u1 = make_u1(nc, io, ab)
    em_u2 = make_u2(nc, io, ab)
    em_u3 = make_u3(nc, io, ab)
    em_u5a = make_u5(nc, io, ab, "a", halo_mode="gather")
    em_u5b = make_u5(nc, io, ab, "b", halo_mode="gather", skip_front=True)
    em_u6 = make_u6(nc, io, ab)

    def em_wsum(ctx, sem):
        w = lambda: [(sem, ctx.val(sem))]
        wf = wsum_buf
        acc = wsum_acc
        ctx.do("gpsimd", lambda e: e.memset(acc[:], 0.0), sem=sem, wait=w())
        for (wT, colacc, rows, width, grp) in ((io["w1s"], 0, D, 1024, 1), (io["w2s"], 1, DFF0, 256, 4)):
            for rg in range(rows // 128 // grp):
                ctx.burst("sync", [
                    (lambda e, r=rg * grp + i, wT=wT, width=width, i=i: e.dma_start(out=wf[:, i * width:(i + 1) * width], in_=wT[(rg * grp + i) * 128:(rg * grp + i + 1) * 128, :]))
                    for i in range(grp)], sem)
                for i in range(grp):
                    ctx.do("vector", lambda e, width=width, i=i: e.reduce_sum(out=acc[:, 2:3], in_=wf[:, i * width:(i + 1) * width], axis=AX.X, apply_absolute_value=True), sem=sem, wait=w())
                    ctx.do("vector", lambda e, colacc=colacc: e.tensor_add(out=acc[:, colacc:colacc + 1], in0=acc[:, colacc:colacc + 1], in1=acc[:, 2:3]), sem=sem, wait=w())
        ctx.do("gpsimd", lambda e: e.memset(wsum_onesf[:], 1.0), sem=sem, wait=w())
        for i in range(2):
            ctx.do("tensor", lambda e, i=i: e.matmul(io["PSB"][0:1, i:i + 1], wsum_onesf[:], acc[:, i:i + 1], start=True, stop=True), sem=sem, wait=w())
        ctx.do("scalar", lambda e: e.activation(out=acc[0:1, 0:2], in_=io["PSB"][0:1, 0:2], func=AF.Copy), sem=sem, wait=w())
        ctx.do("sync", lambda e: e.dma_start(out=io["wsum_c"][:, :], in_=acc[0:1, 0:2]), sem=sem, inc=16, wait=w())

    ab.reset("ws_")
    wsum_buf = ab.sb("wsum_buf", [128, 1024], F32)
    wsum_acc = ab.sb("wsum_acc", [128, 4], F32)
    wsum_onesf = ab.sb("wsum_onesf", [128, 1], F32)

    def em_ag1(ctx, sem):
        cc(ctx, sem, "AllGather", io["lnx_c"], lnx_all)
        cc(ctx, sem, "AllGather", io["wsum_c"], io["wsum_all"])

    def em_ag2(ctx, sem):
        cc(ctx, sem, "AllGather", io["attn_c"], attn_all)

    def em_ag4(ctx, sem):
        cc(ctx, sem, "AllGather", sphi_c, sphi_all)

    def em_dbg(ctx, sem):
        ctx.do("sync", lambda e: e.dma_start(out=d_ws[0:8, :], in_=io["wsum_all"][:, :]), sem=sem, inc=16, wait=[(sem, ctx.val(sem))])
        ctx.do("sync", lambda e: e.dma_start(out=d_ws[8:9, :], in_=io["wsum_c"][:, :]), sem=sem, inc=16, wait=[(sem, ctx.val(sem))])
        for (dst, srcb) in ((d_lnx, io["lnx_c"]), (d_attn, io["attn_c"]), (d_x2, io["x2"]), (d_x3, io["x3"]), (d_sphi, sphi_c)):
            ctx.do("sync", lambda e, dst=dst, srcb=srcb: e.dma_start(out=dst[:, :], in_=srcb[:, :]), sem=sem, inc=16, wait=[(sem, ctx.val(sem))])

    emits = [em_u1, em_wsum, em_ag1, em_u2, em_ag2, em_u3, em_u5a, em_ag4, em_u5b, em_u6] + ([em_dbg] if _FDBG else [])
    run_phases_multi(nc, emits)
    st.close()
    return nc


def run_phases_multi(nc, emits):
    st2 = contextlib.ExitStack()
    sems = [st2.enter_context(nc.semaphore(f"s{i}")) for i in range(len(emits))]

    def prog(eng_name):
        def f(eng):
            ctx = Ctx(eng_name, eng)
            prev_sem, prev_val = None, 0
            for i, em in enumerate(emits):
                if prev_sem is not None:
                    for en in ENGS:
                        ctx.wait(en, prev_sem, prev_val)
                em(ctx, sems[i])
                prev_sem, prev_val = sems[i], ctx.val(sems[i])
            ctx.wait("sync", prev_sem, prev_val)
        return f

    with nc.Block() as block:
        for en in ENGS:
            getattr(block, en)(prog(en))
    st2.close()


def kernel_fused(inputs):
    x = np.ascontiguousarray(np.asarray(inputs["L0_attn_hidden_states"]).reshape(S, D))
    nc = build_full()
    wqT = np.ascontiguousarray(np.asarray(inputs["L0_attn_Wq"]).T)
    wkT = np.ascontiguousarray(np.asarray(inputs["L0_attn_Wk"]).T)
    wvT = np.ascontiguousarray(np.asarray(inputs["L0_attn_Wv"]).T)
    in_maps = []
    for c in range(NCORES):
        sl = slice(c * HPC * HD, (c + 1) * HPC * HD)
        m = {
            "x0": np.ascontiguousarray(x[c * TOK:(c + 1) * TOK].T),
            "wqT": np.ascontiguousarray(wqT[:, sl]),
            "wkT": np.ascontiguousarray(wkT[:, sl]),
            "wvT": np.ascontiguousarray(wvT[:, sl]),
            "bq": np.ascontiguousarray(np.asarray(inputs["L0_attn_bq"])[sl]),
            "bk": np.ascontiguousarray(np.asarray(inputs["L0_attn_bk"])[sl]),
            "bv": np.ascontiguousarray(np.asarray(inputs["L0_attn_bv"])[sl]),
            "woT": np.ascontiguousarray(np.asarray(inputs["L0_attn_Wo"]).T),
            "bo": np.asarray(inputs["L0_attn_bo"]),
            "w1T": np.ascontiguousarray(np.asarray(inputs["L0_ffn_w1"]).T),
            "w1s": np.ascontiguousarray(np.asarray(inputs["L0_ffn_w1"]).T[:, c * 1024:(c + 1) * 1024]),
            "w2s": np.ascontiguousarray(np.asarray(inputs["L0_ffn_w2"]).T[:, c * 256:(c + 1) * 256]),
            "b1": np.asarray(inputs["L0_ffn_b1"]),
            "w2T": np.ascontiguousarray(np.asarray(inputs["L0_ffn_w2"]).T),
            "b2": np.asarray(inputs["L0_ffn_b2"]),
            "ipT": np.ascontiguousarray(np.asarray(inputs["L1_attn_in_proj_w"]).T),
            "cw": np.ascontiguousarray(np.asarray(inputs["L1_attn_conv_w"]).reshape(DXBC, 4)),
            "cb": np.asarray(inputs["L1_attn_conv_b"]),
            "alog": np.asarray(inputs["L1_attn_A_log"]),
            "dtb": np.asarray(inputs["L1_attn_dt_bias"]),
            "dvec": np.asarray(inputs["L1_attn_D"]),
            "nw": np.asarray(inputs["L1_attn_norm_w"]),
            "opT": np.ascontiguousarray(np.asarray(inputs["L1_attn_out_proj_w"]).T),
            "maskr": np.ascontiguousarray(np.tile((np.arange(NCORES) < c).astype(np.float32), (NH_M, 1))),
            "zmask": np.full((128, 1), 0.0 if c == 0 else 1.0, np.float32),
            "wgT": np.ascontiguousarray(np.asarray(inputs["L1_ffn_wq_gate"]).T),
            "wuT": np.ascontiguousarray(np.asarray(inputs["L1_ffn_wq_up"]).T),
            "wdT": np.ascontiguousarray(np.asarray(inputs["L1_ffn_wq_down"]).T),
            "sg": np.asarray(inputs["L1_ffn_s_gate"]),
            "su": np.asarray(inputs["L1_ffn_s_up"]),
            "sd": np.asarray(inputs["L1_ffn_s_down"]),
        }
        in_maps.append(m)
    res = run_bass_kernel_spmd(nc, in_maps, core_ids=list(range(NCORES)))
    if _FDBG:
        global _DBG
        _DBG = {k: [res.results[c][k] for c in range(NCORES)] for k in ("d_lnx", "d_attn", "d_x2", "d_x3", "d_sphi", "d_A", "d_x1", "d_thr", "d_ws")}
    x4_full = np.concatenate([res.results[c]["x4"] for c in range(NCORES)], axis=1)
    return np.ascontiguousarray(x4_full.T).reshape(1, S, D)


_BF = ml_dtypes.bfloat16


def kernel(**inputs):
    global _DBG
    if _STAGE == "fused":
        return kernel_fused(inputs)
    x = np.ascontiguousarray(np.asarray(inputs["L0_attn_hidden_states"]).reshape(S, D))
    if _X2_OVERRIDE is not None:
        x2_full = _X2_OVERRIDE
        return _run_u5(inputs, x2_full)
    nc1 = build_u1()
    in_maps = [{"x": np.ascontiguousarray(x[c * TOK:(c + 1) * TOK].T)} for c in range(NCORES)]
    res = run_bass_kernel_spmd(nc1, in_maps, core_ids=list(range(NCORES)))
    lnx_full = np.concatenate([res.results[c]["lnx"] for c in range(NCORES)], axis=1)
    if _STAGE == "u1":
        return lnx_full

    wqT = np.ascontiguousarray(np.asarray(inputs["L0_attn_Wq"]).T)
    wkT = np.ascontiguousarray(np.asarray(inputs["L0_attn_Wk"]).T)
    wvT = np.ascontiguousarray(np.asarray(inputs["L0_attn_Wv"]).T)
    bq = np.asarray(inputs["L0_attn_bq"]); bk = np.asarray(inputs["L0_attn_bk"]); bv = np.asarray(inputs["L0_attn_bv"])
    nc2 = build_u2()
    in_maps = []
    for c in range(NCORES):
        sl = slice(c * HPC * HD, (c + 1) * HPC * HD)
        in_maps.append({
            "lnx": lnx_full,
            "wqT": np.ascontiguousarray(wqT[:, sl]),
            "wkT": np.ascontiguousarray(wkT[:, sl]),
            "wvT": np.ascontiguousarray(wvT[:, sl]),
            "bq": np.ascontiguousarray(bq[sl]),
            "bk": np.ascontiguousarray(bk[sl]),
            "bv": np.ascontiguousarray(bv[sl]),
        })
    res = run_bass_kernel_spmd(nc2, in_maps, core_ids=list(range(NCORES)))
    attn_full = np.concatenate([res.results[c]["attn"] for c in range(NCORES)], axis=0)
    if _STAGE == "u2":
        return attn_full

    woT = np.ascontiguousarray(np.asarray(inputs["L0_attn_Wo"]).T)
    bo = np.asarray(inputs["L0_attn_bo"])
    w1T = np.ascontiguousarray(np.asarray(inputs["L0_ffn_w1"]).T)
    b1 = np.asarray(inputs["L0_ffn_b1"])
    w2T = np.ascontiguousarray(np.asarray(inputs["L0_ffn_w2"]).T)
    b2 = np.asarray(inputs["L0_ffn_b2"])
    nc3 = build_u3()
    in_maps = []
    for c in range(NCORES):
        in_maps.append({
            "x": np.ascontiguousarray(x[c * TOK:(c + 1) * TOK].T),
            "attn": np.ascontiguousarray(attn_full[:, c * TOK:(c + 1) * TOK]),
            "woT": woT, "bo": bo, "w1T": w1T, "b1": b1, "w2T": w2T, "b2": b2,
        })
    res = run_bass_kernel_spmd(nc3, in_maps, core_ids=list(range(NCORES)))
    x2_full = np.concatenate([res.results[c]["x2"] for c in range(NCORES)], axis=1)
    if _STAGE == "u3":
        return x2_full
    return _run_u5(inputs, x2_full)


def _run_u5(inputs, x2_full):
    global _DBG
    ipT = np.ascontiguousarray(np.asarray(inputs["L1_attn_in_proj_w"]).T)
    cw = np.ascontiguousarray(np.asarray(inputs["L1_attn_conv_w"]).reshape(DXBC, 4))
    cb = np.asarray(inputs["L1_attn_conv_b"])
    alog = np.asarray(inputs["L1_attn_A_log"])
    dtb = np.asarray(inputs["L1_attn_dt_bias"])
    dvec = np.asarray(inputs["L1_attn_D"])
    nw = np.asarray(inputs["L1_attn_norm_w"])
    opT = np.ascontiguousarray(np.asarray(inputs["L1_attn_out_proj_w"]).T)
    zero_h = np.zeros((D, 3), np.float32)
    nc5a = build_u5("a")
    in_maps = []
    for c in range(NCORES):
        xs = np.ascontiguousarray(x2_full[:, c * TOK:(c + 1) * TOK])
        xh = zero_h if c == 0 else np.ascontiguousarray(x2_full[:, c * TOK - 3:c * TOK])
        in_maps.append({"x": xs, "xh": xh, "ipT": ipT, "cw": cw, "cb": cb, "alog": alog, "dtb": dtb})
    res = run_bass_kernel_spmd(nc5a, in_maps, core_ids=list(range(NCORES)))
    sf = np.concatenate([res.results[c]["sfin"] for c in range(NCORES)], axis=0)
    phiT = np.concatenate([res.results[c]["phi"] for c in range(NCORES)], axis=1)
    if _STAGE == "u5a":
        return sf, phiT

    nc5b = build_u5("b")
    in_maps = []
    for c in range(NCORES):
        xs = np.ascontiguousarray(x2_full[:, c * TOK:(c + 1) * TOK])
        xh = zero_h if c == 0 else np.ascontiguousarray(x2_full[:, c * TOK - 3:c * TOK])
        maskr = np.ascontiguousarray(np.tile((np.arange(NCORES) < c).astype(np.float32), (NH_M, 1)))
        in_maps.append({"x": xs, "xh": xh, "ipT": ipT, "cw": cw, "cb": cb, "alog": alog, "dtb": dtb,
                        "dvec": dvec, "nw": nw, "opT": opT, "sf": sf, "phiT": np.ascontiguousarray(phiT), "maskr": maskr})
    res = run_bass_kernel_spmd(nc5b, in_maps, core_ids=list(range(NCORES)))
    x3_full = np.concatenate([res.results[c]["x3"] for c in range(NCORES)], axis=1)
    _DBG = {"x2": x2_full}
    if _STAGE == "u5":
        return x3_full

    wgT = np.ascontiguousarray(np.asarray(inputs["L1_ffn_wq_gate"]).T)
    wuT = np.ascontiguousarray(np.asarray(inputs["L1_ffn_wq_up"]).T)
    wdT = np.ascontiguousarray(np.asarray(inputs["L1_ffn_wq_down"]).T)
    sg = np.asarray(inputs["L1_ffn_s_gate"]); su = np.asarray(inputs["L1_ffn_s_up"]); sd = np.asarray(inputs["L1_ffn_s_down"])
    nc6 = build_u6()
    in_maps = []
    for c in range(NCORES):
        in_maps.append({"x": np.ascontiguousarray(x3_full[:, c * TOK:(c + 1) * TOK]),
                        "wgT": wgT, "wuT": wuT, "wdT": wdT, "sg": sg, "su": su, "sd": sd})
    res = run_bass_kernel_spmd(nc6, in_maps, core_ids=list(range(NCORES)))
    x4_full = np.concatenate([res.results[c]["x4"] for c in range(NCORES)], axis=1)
    if _STAGE == "u6":
        return x4_full
    return np.ascontiguousarray(x4_full.T).reshape(1, S, D)


_STAGE = "fused"
_FDBG = False
_DBG = None
_X2_OVERRIDE = None


# version 13 (final)
# speedup vs baseline: 1.0656x; 1.0157x over previous
import contextlib
import numpy as np
import ml_dtypes
import concourse.bass as bass
import concourse.mybir as mybir
from concourse.bass_utils import run_bass_kernel_spmd
from concourse.ap import AP

F32 = mybir.dt.float32
BF16 = mybir.dt.bfloat16
I32 = mybir.dt.int32
AF = mybir.ActivationFunctionType
ALU = mybir.AluOpType
AX = mybir.AxisListType

NCORES = 8
S = 8192
D = 2048
TOK = S // NCORES
NT = TOK // 128
ND = D // 128
HEADS = 16
HD = 128
HPC = HEADS // NCORES
DFF0 = 8192
DIN = 4096
DSTATE = 128
NH_M = 64
PH = 64
DXBC = 4352
DIP = 8512
CHUNK = 64
NCHUNK = TOK // CHUNK
HID1 = 5632

ENGS = ("sync", "tensor", "scalar", "vector", "gpsimd")


class Ctx:

    def __init__(self, eng_name, eng):
        self.en = eng_name
        self.eng = eng
        self.c = {}

    def val(self, sem):
        return self.c.get(sem.name, 0)

    def _bump(self, sem, inc):
        self.c[sem.name] = self.c.get(sem.name, 0) + inc
        return self.c[sem.name]

    def do(self, which, f, sem=None, inc=1, wait=()):
        if self.en == which:
            for (s, v) in wait:
                self.eng.wait_ge(s, v)
            ins = f(self.eng)
            if sem is not None:
                ins.then_inc(sem, inc)
        if sem is not None:
            return self._bump(sem, inc)
        return None

    def burst(self, which, fns, sem, wait=None):
        tgt = self.val(sem) if wait is None else wait
        if self.en == which:
            self.eng.wait_ge(sem, tgt)
            for fn in fns:
                fn(self.eng).then_inc(sem, 16)
        for _ in fns:
            self._bump(sem, 16)

    def wait(self, which, sem, v):
        if self.en == which:
            self.eng.wait_ge(sem, v)

    def barrier(self, bar_sem, engines=ENGS):
        n = len(engines)
        base = self.val(bar_sem)
        for e in engines:
            self.do(e, lambda eng: eng.nop(), sem=bar_sem, inc=1)
        for e in engines:
            self.wait(e, bar_sem, base + n)



class Arena:

    def __init__(self, nc, base=None):
        self.nc = nc
        self.base = ((nc.sbuf_base + 31) // 32 * 32) if base is None else base
        self.off = 0
        self.pfx = ""

    def reset(self, pfx):
        self.off = 0
        self.pfx = pfx

    def sb(self, name, shape, dtype):
        import math
        elem = {F32: 4, BF16: 2, I32: 4}[dtype]
        per_part = shape[1] * elem if len(shape) == 2 else int(np.prod(shape[1:])) * elem
        per_part = (per_part + 31) // 32 * 32
        t = self.nc.alloc_sbuf_tensor_at(self.pfx + name, list(shape), dtype, offset=self.base + self.off)
        self.off += per_part
        return t


def build_identity(ctx, g_ident):
    ctx.do("gpsimd", lambda e: e.memset(g_ident[:], 1.0))
    ctx.do("gpsimd", lambda e: e.affine_select(
        out=g_ident[:], in_=g_ident[:], compare_op=ALU.is_equal, fill=0.0,
        base=0, pattern=[[-1, 128]], channel_multiplier=1))


def emit_ln_fm(ctx, sem, x_fm, out_fm, hp, nd=ND, ntok=TOK):
    d_model = nd * 128
    w = lambda: [(sem, ctx.val(sem))]
    tb0, tb1 = hp["tb0"], hp["tb1"]
    abc, bbc = hp["abc"], hp["bbc"]
    row = hp["row"]
    psA, psB = hp["psA"], hp["psB"]
    oc, orow = hp["ones_col_b"], hp["ones_row"]
    for tc in range(ntok // 512):
        def xs(d):
            return x_fm[:, d * ntok + tc * 512: d * ntok + tc * 512 + 512]
        for d in range(nd):
            ctx.do("scalar", lambda e, d=d: e.activation(out=tb0[:], in_=xs(d), func=AF.Identity), sem=sem, wait=w())
            ctx.do("scalar", lambda e, d=d: e.activation(out=tb1[:], in_=xs(d), func=AF.Square), sem=sem, wait=w())
            ctx.do("tensor", lambda e, d=d: e.matmul(psA[0:1, 0:512], oc[:], tb0[:], start=(d == 0), stop=(d == nd - 1)), sem=sem, wait=w())
            ctx.do("tensor", lambda e, d=d: e.matmul(psA[0:1, 512:1024], oc[:], tb1[:], start=(d == 0), stop=(d == nd - 1)), sem=sem, wait=w())
        ctx.do("vector", lambda e: e.tensor_scalar(out=row[0:1, 0:512], in0=psA[0:1, 0:512], scalar1=1.0 / d_model, scalar2=None, op0=ALU.mult), sem=sem, wait=w())
        ctx.do("vector", lambda e: e.tensor_scalar(out=row[0:1, 512:1024], in0=psA[0:1, 512:1024], scalar1=1.0 / d_model, scalar2=None, op0=ALU.mult), sem=sem, wait=w())
        ctx.do("vector", lambda e: e.scalar_tensor_tensor(out=row[0:1, 512:1024], in0=row[0:1, 0:512], scalar=-1.0, in1=row[0:1, 512:1024], op0=ALU.mult if False else ALU.bypass, op1=ALU.bypass) if False else None, sem=None)
        ctx.do("vector", lambda e: e.tensor_tensor(out=row[0:1, 1024:1536], in0=row[0:1, 0:512], in1=row[0:1, 0:512], op=ALU.mult), sem=sem, wait=w())
        ctx.do("vector", lambda e: e.tensor_tensor(out=row[0:1, 512:1024], in0=row[0:1, 512:1024], in1=row[0:1, 1024:1536], op=ALU.subtract), sem=sem, wait=w())
        ctx.do("vector", lambda e: e.tensor_scalar(out=row[0:1, 512:1024], in0=row[0:1, 512:1024], scalar1=1e-5, scalar2=None, op0=ALU.add), sem=sem, wait=w())
        ctx.do("scalar", lambda e: e.activation(out=row[0:1, 512:1024], in_=row[0:1, 512:1024], func=AF.Ln), sem=sem, wait=w())
        ctx.do("scalar", lambda e: e.activation(out=row[0:1, 512:1024], in_=row[0:1, 512:1024], func=AF.Exp, scale=-0.5), sem=sem, wait=w())
        ctx.do("vector", lambda e: e.tensor_tensor(out=row[0:1, 1024:1536], in0=row[0:1, 0:512], in1=row[0:1, 512:1024], op=ALU.mult), sem=sem, wait=w())
        ctx.do("vector", lambda e: e.tensor_scalar(out=row[0:1, 1024:1536], in0=row[0:1, 1024:1536], scalar1=-1.0, scalar2=None, op0=ALU.mult), sem=sem, wait=w())
        ctx.do("tensor", lambda e: e.matmul(psB[:], orow[:], row[0:1, 512:1024], start=True, stop=True), sem=sem, wait=w())
        ctx.do("scalar", lambda e: e.activation(out=abc[:], in_=psB[:], func=AF.Copy), sem=sem, wait=w())
        ctx.do("tensor", lambda e: e.matmul(psB[:], orow[:], row[0:1, 1024:1536], start=True, stop=True), sem=sem, wait=w())
        ctx.do("scalar", lambda e: e.activation(out=bbc[:], in_=psB[:], func=AF.Copy), sem=sem, wait=w())
        for d in range(nd):
            ctx.do("vector", lambda e, d=d: e.tensor_tensor(out=tb0[:] if False else hp["tf"][:], in0=xs(d), in1=abc[:], op=ALU.mult), sem=sem, wait=w())
            ctx.do("vector", lambda e, d=d, tc=tc: e.tensor_tensor(out=out_fm[:, d * ntok + tc * 512: d * ntok + tc * 512 + 512], in0=hp["tf"][:], in1=bbc[:], op=ALU.add), sem=sem, wait=w())


def ln_helpers(ab, PSB):
    return {
        "tb0": ab.sb("tb0", [128, 512], BF16),
        "tb1": ab.sb("tb1", [128, 512], BF16),
        "tf": ab.sb("tf", [128, 512], F32),
        "abc": ab.sb("abc", [128, 512], F32),
        "bbc": ab.sb("bbc", [128, 512], F32),
        "row": ab.sb("row", [1, 1536], F32),
        "ones_col_b": ab.sb("ones_col_b", [128, 1], BF16),
        "ones_row": ab.sb("ones_row", [1, 128], F32),
        "psA": PSB[0:1, 0:1024],
        "psB": PSB[:, 1024:1536],
    }


def make_u1(nc, io, ab):
    ab.reset("u1_")
    x_d, lnx_d, PSB = io["x0"], io["lnx_c"], io["PSB"]
    x_fm = ab.sb("x_fm", [128, ND * TOK], F32)
    lnxfm = ab.sb("lnxfm", [128, ND * TOK], BF16)
    hp = ln_helpers(ab, PSB)

    def emit(ctx, sem):
        w = lambda: [(sem, ctx.val(sem))]
        ctx.do("gpsimd", lambda e: e.memset(hp["ones_col_b"][:], 1.0), sem=sem, wait=w())
        ctx.do("gpsimd", lambda e: e.memset(hp["ones_row"][:], 1.0), sem=sem, wait=w())
        ctx.burst("sync", [
            (lambda e, d=d: e.dma_start(out=x_fm[:, d * TOK:(d + 1) * TOK], in_=x_d[d * 128:(d + 1) * 128, :]))
            for d in range(ND)], sem)
        emit_ln_fm2(ctx, sem, lambda d, tc: x_fm[:, d * TOK + tc * 512: d * TOK + tc * 512 + 512], lnxfm, hp, [0, 1], TOK, TOK)
        ctx.burst("sync", [
            (lambda e, d=d: e.dma_start(out=lnx_d[d * 128:(d + 1) * 128, :], in_=lnxfm[:, d * TOK:(d + 1) * TOK]))
            for d in range(ND)], sem)
    return emit


def run_phases(nc, emits):
    st2 = contextlib.ExitStack()
    sem = st2.enter_context(nc.semaphore("s"))

    def prog(eng_name):
        def f(eng):
            ctx = Ctx(eng_name, eng)
            for em in emits:
                em(ctx, sem)
            ctx.wait("sync", sem, ctx.val(sem))
        return f

    with nc.Block() as block:
        for en in ENGS:
            getattr(block, en)(prog(en))
    st2.close()


def build_u1():
    nc = bass.Bass("TRN2", target_bir_lowering=False)
    io = {
        "x0": nc.dram_tensor("x", [D, TOK], F32, kind="ExternalInput").ap(),
        "lnx_c": nc.dram_tensor("lnx", [D, TOK], BF16, kind="ExternalOutput").ap(),
    }
    st = contextlib.ExitStack()
    io["PSA"] = st.enter_context(nc.psum_tensor("PSA", [128, 2048], F32))
    io["PSB"] = st.enter_context(nc.psum_tensor("PSB", [128, 2048], F32))
    ab = Arena(nc)
    run_phases(nc, [make_u1(nc, io, ab)])
    st.close()
    return nc


ATT_SCALE = float(HD) ** -0.5


def make_u2(nc, io, ab):
    ab.reset("u2_")
    lnx_tile = io["lnx_tile"]
    wqT_d, wkT_d, wvT_d = io["wqT"], io["wkT"], io["wvT"]
    bq_d, bk_d, bv_d = io["bq"], io["bk"], io["bv"]
    attn_d = io["attn_c"]
    PSA, PSB = io["PSA"], io["PSB"]
    sb = ab.sb
    wld_f = sb("wld_f", [128, 256], F32)
    wq_b = sb("wq_b", [128, ND * 256], BF16)
    wk_b = sb("wk_b", [128, ND * 256], BF16)
    wv_b = sb("wv_b", [128, ND * 256], BF16)
    bias_sb = sb("bias_sb", [128, 8], F32)
    bvbc = sb("bvbc", [128, 256], F32)
    lnx_sb = [sb(f"lnx_sb{i}", [128, ND * 512], BF16) for i in range(2)]
    q_fm = sb("q_fm", [128, HPC * S], BF16)
    k_fm = sb("k_fm", [128, HPC * S], BF16)
    v_sb = sb("v_sb", [128, (S // 128) * 260], BF16)
    pt_sb = sb("pt_sb", [128, 1024], BF16)
    atile2 = [sb(f"atile2_{i}", [128, 512], BF16) for i in range(2)]
    sq_sb = sb("sq_sb", [128, 2048], BF16)
    scal = sb("scal", [1, 16], F32)
    scal2 = sb("scal2", [1, 512], F32)
    negb = sb("negb", [128, 2], F32)
    ones_col = sb("ones_col", [128, 1], BF16)
    ones_row = sb("ones_row", [1, 128], F32)
    ident = sb("ident", [128, 128], BF16)
    pq_on = "w1q" in io
    if pq_on:
        pq_win = [sb(f"pq_win{i}", [128, 1024], F32) for i in range(2)]
        pq_ma = sb("pq_ma", [128, 1024], F32)
        pq_q = [sb(f"pq_q{i}", [128, 1024], BF16) for i in range(2)]
        pq_acc8 = sb("pq_acc8", [8, 2], F32)
        pq_ones8 = sb("pq_ones8", [8, 1], F32)
        pq_s4 = sb("pq_s4", [1, 4], F32)
        pq_thrc = sb("pq_thrc", [128, 4], F32)
    psS = PSA[:, 0:1024]
    psK2 = PSA[:, 1024:2048]
    psV = PSB[:, 0:1024]
    psT = PSB[:, 1024:1088].bitcast(BF16)
    psM = PSB[:, 1536:2048]
    NQC = S // 512

    def emit(ctx, sem):
        w = lambda: [(sem, ctx.val(sem))]

        def chain(which, fns, inc_last=1):
            if ctx.en == which:
                ctx.eng.wait_ge(sem, ctx.val(sem))
                for fn in fns:
                    ins = fn(ctx.eng)
                ins.then_inc(sem, inc_last)
            ctx._bump(sem, inc_last)

        build_identity(ctx, ident)
        ctx.do("gpsimd", lambda e: e.memset(ones_col[:], 1.0), sem=sem, wait=w())
        ctx.do("gpsimd", lambda e: e.memset(ones_row[:], 1.0), sem=sem, wait=w())
        ctx.do("gpsimd", lambda e: e.memset(v_sb[:], 0.0), sem=sem, wait=w())

        if pq_on:
            pq_in, pq_v, pq_out = io["pq_in"], io["pq_v"], io["pq_out"]
            ctx.do("gpsimd", lambda e: e.memset(pq_ones8[:], 1.0), sem=sem, wait=w())
            ctx.do("sync", lambda e: e.dma_start(out=pq_acc8[:, :], in_=io["wsum_all"][:, :]), sem=sem, inc=16, wait=w())
            ctx.do("tensor", lambda e: e.matmul(psM[0:1, 0:2], pq_ones8[:], pq_acc8[0:8, 0:2], start=True, stop=True), sem=sem, wait=w())
            ctx.do("vector", lambda e: e.tensor_scalar(out=pq_s4[0:1, 0:2], in0=psM[0:1, 0:2], scalar1=0.7 / (D * DFF0), scalar2=None, op0=ALU.mult), sem=sem, wait=w())
            ctx.do("vector", lambda e: e.tensor_scalar(out=pq_s4[0:1, 2:4], in0=pq_s4[0:1, 0:2], scalar1=-1.0, scalar2=None, op0=ALU.mult), sem=sem, wait=w())
            ctx.do("tensor", lambda e: e.matmul(psM[:, 0:4], ones_row[:], pq_s4[0:1, 0:4], start=True, stop=True), sem=sem, wait=w())
            ctx.do("scalar", lambda e: e.activation(out=pq_thrc[:], in_=psM[:, 0:4], func=AF.Copy), sem=sem, wait=w())
            pq_rounds = []
            for r in range(D // 128):
                for cc in range(DFF0 // 1024):
                    pq_rounds.append((io["w1T"][r * 128:(r + 1) * 128, cc * 1024:(cc + 1) * 1024],
                                      io["w1q"][r * 128:(r + 1) * 128, cc * 1024:(cc + 1) * 1024], 0))
            for r in range(DFF0 // 128):
                for cc in range(D // 1024):
                    pq_rounds.append((io["w2T"][r * 128:(r + 1) * 128, cc * 1024:(cc + 1) * 1024],
                                      io["w2q"][r * 128:(r + 1) * 128, cc * 1024:(cc + 1) * 1024], 1))
            pq_st = {"i": 0}

            def pq_step():
                i = pq_st["i"]
                if i >= len(pq_rounds):
                    return
                src, dst, tc_ = pq_rounds[i]
                wt_in = ([] if i < 2 else [(pq_v, 2 * (i - 1))]) + ([] if i == 0 else [(pq_in, 16 * i)])
                ctx.do("gpsimd", lambda e, src=src, i=i: e.dma_start(out=pq_win[i % 2][:], in_=src), sem=pq_in, inc=16, wait=wt_in)
                ctx.do("vector", lambda e, i=i, tc_=tc_: e.tensor_scalar(out=pq_ma[:], in0=pq_win[i % 2][:], scalar1=pq_thrc[:, tc_:tc_ + 1], scalar2=-1.0, op0=ALU.is_ge, op1=ALU.add), sem=pq_v, inc=1, wait=[(pq_in, 16 * (i + 1))])
                wt_q = [] if i < 2 else [(pq_out, 16 * (i - 1))]
                ctx.do("vector", lambda e, i=i, tc_=tc_: e.scalar_tensor_tensor(out=pq_q[i % 2][:], in0=pq_win[i % 2][:], scalar=pq_thrc[:, 2 + tc_:3 + tc_], in1=pq_ma[:], op0=ALU.is_ge, op1=ALU.add), sem=pq_v, inc=1, wait=wt_q)
                wt_o = [(pq_v, 2 * (i + 1))] + ([] if i == 0 else [(pq_out, 16 * i)])
                ctx.do("sync", lambda e, dst=dst, i=i: e.dma_start(out=dst, in_=pq_q[i % 2][:]), sem=pq_out, inc=16, wait=wt_o)
                pq_st["i"] += 1
        else:
            def pq_step():
                return
        for h in range(2):
            ctx.do("gpsimd", lambda e, h=h: e.memset(v_sb[:].rearrange("p (t c) -> p t c", c=260)[:, :, h * 129 + 128: h * 129 + 129], 1.0), sem=sem, wait=w())
        ctx.burst("gpsimd", [
            (lambda e, d=d, wT_d=wT_d, w_b=w_b: e.dma_start(out=w_b[:, d * 256:(d + 1) * 256], in_=wT_d[d * 128:(d + 1) * 128, :]))
            for d in range(ND) for (wT_d, w_b) in ((wqT_d, wq_b), (wkT_d, wk_b), (wvT_d, wv_b))], sem)
        ctx.burst("sync", [
            (lambda e, h=h: e.dma_start(out=bias_sb[:, h:h + 1], in_=bq_d[h * 128:(h + 1) * 128].unsqueeze(1)))
            for h in range(2)] + [
            (lambda e, h=h: e.dma_start(out=bias_sb[:, 2 + h:3 + h], in_=bk_d[h * 128:(h + 1) * 128].unsqueeze(1)))
            for h in range(2)], sem)
        ctx.do("vector", lambda e: e.tensor_scalar(out=bias_sb[:, 0:2], in0=bias_sb[:, 0:2], scalar1=ATT_SCALE, scalar2=None, op0=ALU.mult), sem=sem, wait=w())
        ctx.do("sync", lambda e: e.dma_start(out=bvbc[0:1, 0:256], in_=bv_d.unsqueeze(0)), sem=sem, inc=16, wait=w())
        ctx.do("tensor", lambda e: e.matmul(psM[:, 0:256], ones_row[:], bvbc[0:1, 0:256], start=True, stop=True), sem=sem, wait=w())
        ctx.do("scalar", lambda e: e.activation(out=bvbc[:], in_=psM[:, 0:256], func=AF.Copy), sem=sem, wait=w())

        for qc in range(NQC):
            buf = lnx_sb[qc % 2]
            ctx.burst("sync", [
                (lambda e, d=d, qc=qc, buf=buf: e.dma_start(out=buf[:, d * 512:(d + 1) * 512], in_=lnx_tile(d, qc)))
                for d in range(ND)], sem)
            for h in range(2):
                chain("tensor", [
                    (lambda e, h=h, d=d, buf=buf: e.matmul(psS[:, h * 512:(h + 1) * 512], wq_b[:, d * 256 + h * 128: d * 256 + h * 128 + 128], buf[:, d * 512:(d + 1) * 512], start=(d == 0), stop=(d == ND - 1)))
                    for d in range(ND)])
                chain("tensor", [
                    (lambda e, h=h, d=d, buf=buf: e.matmul(psK2[:, h * 512:(h + 1) * 512], wk_b[:, d * 256 + h * 128: d * 256 + h * 128 + 128], buf[:, d * 512:(d + 1) * 512], start=(d == 0), stop=(d == ND - 1)))
                    for d in range(ND)])
                ctx.do("scalar", lambda e, h=h, qc=qc: e.activation(out=q_fm[:, h * S + qc * 512: h * S + qc * 512 + 512], in_=psS[:, h * 512:(h + 1) * 512], func=AF.Identity, scale=ATT_SCALE, bias=bias_sb[:, h:h + 1]), sem=sem, wait=w())
                ctx.do("scalar", lambda e, h=h, qc=qc: e.activation(out=k_fm[:, h * S + qc * 512: h * S + qc * 512 + 512], in_=psK2[:, h * 512:(h + 1) * 512], func=AF.Identity, scale=1.0, bias=bias_sb[:, 2 + h:3 + h]), sem=sem, wait=w())
            for j in range(4):
                chain("tensor", [
                    (lambda e, j=j, d=d, buf=buf: e.matmul(psV[:, j * 256:(j + 1) * 256], buf[:, d * 512 + j * 128: d * 512 + j * 128 + 128], wv_b[:, d * 256:(d + 1) * 256], start=(d == 0), stop=(d == ND - 1)))
                    for d in range(ND)])
                ctx.do("vector", lambda e, j=j: e.tensor_add(out=psV[:, j * 256:(j + 1) * 256], in0=psV[:, j * 256:(j + 1) * 256], in1=bvbc[:]), sem=sem, wait=w())
                tt = qc * 4 + j
                for h in range(2):
                    ctx.do("vector", lambda e, j=j, h=h, tt=tt: e.tensor_copy(out=v_sb[:, tt * 260 + h * 129: tt * 260 + h * 129 + 128], in_=psV[:, j * 256 + h * 128: j * 256 + (h + 1) * 128]), sem=sem, wait=w())

        for h in range(2):
            for (srcb, outc) in ((q_fm, 0), (k_fm, 1)):
                for ch in range(S // 2048):
                    ctx.do("scalar", lambda e, h=h, ch=ch, srcb=srcb: e.activation(out=sq_sb[:], in_=srcb[:, h * S + ch * 2048: h * S + ch * 2048 + 2048], func=AF.Square), sem=sem, wait=w())
                    for sub in range(4):
                        ctx.do("tensor", lambda e, sub=sub: e.matmul(psM[0:1, 0:512], ones_col[:], sq_sb[:, sub * 512:(sub + 1) * 512], start=True, stop=True), sem=sem, wait=w())
                        first = (ch == 0 and sub == 0)
                        if first:
                            ctx.do("vector", lambda e, outc=outc: e.reduce_max(out=scal[0:1, 8 + outc:9 + outc], in_=psM[0:1, 0:512], axis=AX.X), sem=sem, wait=w())
                            ctx.do("vector", lambda e, outc=outc: e.tensor_copy(out=scal[0:1, outc:outc + 1], in_=scal[0:1, 8 + outc:9 + outc]), sem=sem, wait=w())
                        else:
                            ctx.do("vector", lambda e, outc=outc: e.reduce_max(out=scal[0:1, 8 + outc:9 + outc], in_=psM[0:1, 0:512], axis=AX.X), sem=sem, wait=w())
                            ctx.do("vector", lambda e, outc=outc: e.tensor_tensor(out=scal[0:1, outc:outc + 1], in0=scal[0:1, outc:outc + 1], in1=scal[0:1, 8 + outc:9 + outc], op=ALU.max), sem=sem, wait=w())
            ctx.do("scalar", lambda e: e.activation(out=scal[0:1, 2:3], in_=scal[0:1, 0:1], func=AF.Ln), sem=sem, wait=w())
            ctx.do("scalar", lambda e: e.activation(out=scal[0:1, 3:4], in_=scal[0:1, 1:2], func=AF.Ln), sem=sem, wait=w())
            ctx.do("vector", lambda e: e.tensor_add(out=scal[0:1, 4:5], in0=scal[0:1, 2:3], in1=scal[0:1, 3:4]), sem=sem, wait=w())
            ctx.do("scalar", lambda e: e.activation(out=scal[0:1, 5:6], in_=scal[0:1, 4:5], func=AF.Exp, scale=0.5), sem=sem, wait=w())
            ctx.do("vector", lambda e: e.tensor_scalar(out=scal[0:1, 6:7], in0=scal[0:1, 5:6], scalar1=-1.0, scalar2=None, op0=ALU.mult), sem=sem, wait=w())
            ctx.do("tensor", lambda e: e.matmul(psM[:, 0:1], ones_row[:], scal[0:1, 6:7], start=True, stop=True), sem=sem, wait=w())
            ctx.do("scalar", lambda e, h=h: e.activation(out=negb[:, h:h + 1], in_=psM[:, 0:1], func=AF.Copy), sem=sem, wait=w())

        bcast_f = sq_sb[:, 0:1024].bitcast(F32)
        sstS = io["sst"]
        outdn = []
        for h in range(2):
            for qc in range(NQC):
                kvt = 4 * (qc + 1)
                for pair in range(kvt // 2):
                    k0 = 2 * pair
                    chain("tensor", [
                        (lambda e, h=h, qc=qc, half=half: e.matmul(psS[:, half * 512:(half + 1) * 512], k_fm[:, h * S + (k0 + half) * 128: h * S + (k0 + half) * 128 + 128], q_fm[:, h * S + qc * 512: h * S + qc * 512 + 512], start=True, stop=True))
                        for half in range(2)])
                    ctx.do("scalar", lambda e, h=h: e.activation(out=pt_sb[:], in_=psS[:], func=AF.Exp, bias=negb[:, h:h + 1]), sem=sem, wait=w())
                    for half in range(2):
                        kt = k0 + half
                        if kt * 128 + 127 >= qc * 512:
                            ctx.do("gpsimd", lambda e, kt=kt, qc=qc, half=half: e.affine_select(
                                out=pt_sb[:, half * 512:(half + 1) * 512], in_=pt_sb[:, half * 512:(half + 1) * 512],
                                compare_op=ALU.is_ge, fill=0.0,
                                base=qc * 512 - kt * 128, pattern=[[1, 512]], channel_multiplier=-1), sem=sem, wait=w())
                    av = []
                    for half in range(2):
                        kt = k0 + half
                        av.append((lambda e, h=h, kt=kt, half=half, kvt=kvt: e.matmul(
                            psV[:, 0:512],
                            v_sb[:, kt * 260 + h * 129: kt * 260 + h * 129 + 128],
                            pt_sb[:, half * 512:(half + 1) * 512],
                            start=(kt == 0), stop=(kt == kvt - 1))))
                        av.append((lambda e, kt=kt, half=half, kvt=kvt: e.matmul(
                            psM[0:1, 0:512],
                            ones_col[:],
                            pt_sb[:, half * 512:(half + 1) * 512],
                            start=(kt == 0), stop=(kt == kvt - 1))))
                    chain("tensor", av)
                    pq_step()
                ctx.do("vector", lambda e: e.reciprocal(out=scal2[0:1, :], in_=psM[0:1, 0:512]), sem=sem, wait=w())
                ctx.do("tensor", lambda e: e.matmul(psK2[:, 0:512], ones_row[:], scal2[0:1, :], start=True, stop=True), sem=sem, wait=w())
                ctx.do("scalar", lambda e: e.activation(out=bcast_f[:], in_=psK2[:, 0:512], func=AF.Copy), sem=sem, wait=w())
                t = h * NQC + qc
                wt_a = w() if t < 2 else (w() + [(sstS, outdn[t - 2])])
                ctx.do("vector", lambda e, t=t: e.tensor_tensor(out=atile2[t % 2][:], in0=psV[:, 0:512], in1=bcast_f[:], op=ALU.mult), sem=sem, wait=wt_a)
                wt_d = w() + ([] if t == 0 else [(sstS, outdn[t - 1])])
                outdn.append(ctx.do("sync", lambda e, h=h, qc=qc, t=t: e.dma_start(out=attn_d[h * 128:(h + 1) * 128, qc * 512:(qc + 1) * 512], in_=atile2[t % 2][:]), sem=sstS, inc=16, wait=wt_d))

        if pq_on:
            while pq_st["i"] < len(pq_rounds):
                pq_step()
            for en in ("gpsimd", "sync", "vector"):
                ctx.wait(en, io["pq_out"], 16 * len(pq_rounds))
        for en in ("gpsimd", "sync"):
            ctx.wait(en, sstS, ctx.val(sstS))
        ctx.do("gpsimd", lambda e: e.nop(), sem=sem, wait=w())
    return emit


def build_u2():
    nc = bass.Bass("TRN2", target_bir_lowering=False)
    io = {
        "lnx": nc.dram_tensor("lnx", [D, S], BF16, kind="ExternalInput").ap(),
        "wqT": nc.dram_tensor("wqT", [D, HPC * HD], F32, kind="ExternalInput").ap(),
        "wkT": nc.dram_tensor("wkT", [D, HPC * HD], F32, kind="ExternalInput").ap(),
        "wvT": nc.dram_tensor("wvT", [D, HPC * HD], F32, kind="ExternalInput").ap(),
        "bq": nc.dram_tensor("bq", [HPC * HD], F32, kind="ExternalInput").ap(),
        "bk": nc.dram_tensor("bk", [HPC * HD], F32, kind="ExternalInput").ap(),
        "bv": nc.dram_tensor("bv", [HPC * HD], F32, kind="ExternalInput").ap(),
        "attn_c": nc.dram_tensor("attn", [HPC * HD, S], BF16, kind="ExternalOutput").ap(),
    }
    io["lnx_tile"] = lambda d, qc: io["lnx"][d * 128:(d + 1) * 128, qc * 512:(qc + 1) * 512]
    st = contextlib.ExitStack()
    io["PSA"] = st.enter_context(nc.psum_tensor("PSA", [128, 2048], F32))
    io["PSB"] = st.enter_context(nc.psum_tensor("PSB", [128, 2048], F32))
    io["sst"] = st.enter_context(nc.semaphore("sst"))
    ab = Arena(nc)
    run_phases(nc, [make_u2(nc, io, ab)])
    st.close()
    return nc


def make_u3(nc, io, ab):
    ab.reset("u3_")
    x_d = io["x0"]
    attn_tile = io.get("attn_tile")
    woT_d, bo_d = io["woT"], io["bo"]
    b1_d, b2_d = io["b1"], io["b2"]
    w1q_d, w2q_d = io["w1q"], io["w2q"]
    x2_d = io["x2"]
    PSA, PSB = io["PSA"], io["PSB"]
    sb = ab.sb
    x_fm = sb("x_fm", [128, ND * TOK], F32)
    A = sb("A", [128, ND * TOK], BF16)
    B = sb("B", [128, 32768], BF16)
    rbuf = [sb(f"rbuf{i}", [128, 2048], BF16) for i in range(2)]
    wf_sb = sb("wf_sb", [128, 1024], F32)
    bcol = sb("bcol", [128, 96], F32)
    if "wgq" in io:
        cin = [sb(f"cin{i}", [128, 512], I32) for i in range(2)]
        cout = [sb(f"cout{i}", [128, 512], BF16) for i in range(2)]
    hp = ln_helpers(ab, PSB)
    tf_sb = hp["tf"]
    psC = PSA[:, 0:512]

    def emit(ctx, sem):
        if True:
            w = lambda: [(sem, ctx.val(sem))]

            def chain(which, fns, inc_last=1, extra=()):
                if ctx.en == which:
                    ctx.eng.wait_ge(sem, ctx.val(sem))
                    for (s2, v2) in extra:
                        ctx.eng.wait_ge(s2, v2)
                    for fn in fns:
                        ins = fn(ctx.eng)
                    ins.then_inc(sem, inc_last)
                ctx._bump(sem, inc_last)

            cq_on = "wgq" in io
            if cq_on:
                pq_in, pq_v, pq_out = io["pq_in"], io["pq_v"], io["pq_out"]
                CQ = []
                for (srcT, dstT, rows, cols) in ((io["wgT"], io["wgq"], D, HID1), (io["wuT"], io["wuq"], D, HID1), (io["wdT"], io["wdq"], HID1, D)):
                    for r in range(rows // 128):
                        for c in range(cols // 512):
                            CQ.append((srcT[r * 128:(r + 1) * 128, c * 512:(c + 1) * 512], dstT[r * 128:(r + 1) * 128, c * 512:(c + 1) * 512]))
                cq_bases = {}
                cq_st = {"i": 0}

                def cq_step():
                    i = cq_st["i"]
                    if i >= len(CQ):
                        return
                    if i == 0:
                        cq_bases["in"] = ctx.val(pq_in)
                        cq_bases["v"] = ctx.val(pq_v)
                        cq_bases["out"] = ctx.val(pq_out)
                    bi, bv, bo = cq_bases["in"], cq_bases["v"], cq_bases["out"]
                    src, dst = CQ[i]
                    wt_in = ([] if i < 2 else [(pq_v, bv + (i - 1))]) + ([] if i == 0 else [(pq_in, bi + 16 * i)])
                    ctx.do("scalar", lambda e, src=src, i=i: e.dma_start(out=cin[i % 2][:], in_=src), sem=pq_in, inc=16, wait=wt_in)
                    wt_c = [(pq_in, bi + 16 * (i + 1))] + ([] if i < 2 else [(pq_out, bo + 16 * (i - 1))])
                    ctx.do("scalar", lambda e, i=i: e.activation(out=cout[i % 2][:], in_=cin[i % 2][:], func=AF.Copy, bias=-128.0), sem=pq_v, inc=1, wait=wt_c)
                    wt_o = [(pq_v, bv + i + 1)] + ([] if i == 0 else [(pq_out, bo + 16 * i)])
                    ctx.do("sync", lambda e, dst=dst, i=i: e.dma_start(out=dst, in_=cout[i % 2][:]), sem=pq_out, inc=16, wait=wt_o)
                    cq_st["i"] += 1
            else:
                def cq_step():
                    return

            ctx.do("gpsimd", lambda e: e.memset(hp["ones_col_b"][:], 1.0), sem=sem, wait=w())
            ctx.do("gpsimd", lambda e: e.memset(hp["ones_row"][:], 1.0), sem=sem, wait=w())
            ctx.burst("sync", [
                (lambda e, d=d: e.dma_start(out=x_fm[:, d * TOK:(d + 1) * TOK], in_=x_d[d * 128:(d + 1) * 128, :]))
                for d in range(ND)], sem)
            if io.get("attn_dyn") is not None:
                if ctx.en == "sync":
                    ctx.eng.wait_ge(sem, ctx.val(sem))
                    ins = io["attn_dyn"](ctx.eng, A)
                    ins.then_inc(sem, 16)
                ctx._bump(sem, 16)
            else:
                for d in range(ND):
                    ctx.do("sync", lambda e, d=d: e.dma_start(out=A[:, d * TOK:(d + 1) * TOK], in_=attn_tile(d)), sem=sem, inc=16, wait=w())
            ctx.burst("sync", [
                (lambda e, d=d: e.dma_start(out=bcol[:, d:d + 1], in_=bo_d[d * 128:(d + 1) * 128].unsqueeze(1)))
                for d in range(ND)] + [
                (lambda e, d=d: e.dma_start(out=bcol[:, 16 + d:17 + d], in_=b2_d[d * 128:(d + 1) * 128].unsqueeze(1)))
                for d in range(ND)] + [
                (lambda e, ff=ff: e.dma_start(out=bcol[:, 32 + ff:33 + ff], in_=b1_d[ff * 128:(ff + 1) * 128].unsqueeze(1)))
                for ff in range(64)], sem)
            ctx.burst("gpsimd", [
                (lambda e, dp=dp: e.dma_start(out=B[:, dp * 2048:(dp + 1) * 2048], in_=woT_d[dp * 128:(dp + 1) * 128, :]))
                for dp in range(ND)], sem)

            for ft in range(ND):
                for tc in range(2):
                    chain("tensor", [
                        (lambda e, dp=dp, ft=ft, tc=tc: e.matmul(PSA[:, tc * 512:(tc + 1) * 512], B[:, dp * 2048 + ft * 128: dp * 2048 + ft * 128 + 128], A[:, dp * TOK + tc * 512: dp * TOK + tc * 512 + 512], start=(dp == 0), stop=(dp == ND - 1)))
                        for dp in range(ND)])
                    cq_step()
                ctx.do("scalar", lambda e, ft=ft: e.activation(out=wf_sb[:, 0:1024], in_=PSA[:, 0:1024], func=AF.Identity, scale=1.0, bias=bcol[:, ft:ft + 1]), sem=sem, wait=w())
                ctx.do("vector", lambda e, ft=ft: e.tensor_add(out=x_fm[:, ft * TOK: ft * TOK + TOK], in0=x_fm[:, ft * TOK: ft * TOK + TOK], in1=wf_sb[:, 0:1024]), sem=sem, wait=w())

            emit_ln_fm2(ctx, sem, lambda d, tc: x_fm[:, d * TOK + tc * 512: d * TOK + tc * 512 + 512], A, hp, [0, 1], TOK, TOK)

            sstS = io["sst"]
            CH = []
            for half in range(2):
                for fb in range(16):
                    for dpg in range(4):
                        CH.append((w1q_d[dpg * 512:(dpg + 1) * 512, fb * 512:(fb + 1) * 512], 512))
                for ft in range(ND):
                    for sub in range(4):
                        CH.append((w2q_d[sub * 2048:(sub + 1) * 2048, ft * 128:(ft + 1) * 128], 128))
            ring = {"g": 0, "u": 0, "dn": [], "cons": []}

            def pump():
                g = ring["g"]
                if g >= len(CH):
                    return
                src, cdim = CH[g]
                wt = (w() if g < 2 else [(sem, ring["cons"][g - 2])]) + ([] if g == 0 else [(sstS, ring["dn"][g - 1])])
                ring["dn"].append(ctx.do("gpsimd", lambda e, src=src, cdim=cdim, g=g: e.dma_start(
                    out=rbuf[g % 2][:].rearrange("p (a c) -> p a c", c=cdim),
                    in_=src.rearrange("(a p) c -> p a c", p=128)), sem=sstS, inc=16, wait=wt))
                ring["g"] += 1

            def use():
                g = ring["u"]
                return rbuf[g % 2], [(sstS, ring["dn"][g])]

            def done_use():
                ring["cons"].append(ctx.val(sem))
                ring["u"] += 1
                pump()

            pump()
            pump()
            for half in range(2):
                hoff = half * 512
                for fb in range(16):
                    for dpg in range(4):
                        rb, ex = use()
                        for fs in range(4):
                            chain("tensor", [
                                (lambda e, i=i, fs=fs, rb=rb, dpg=dpg, hoff=hoff: e.matmul(
                                    PSA[:, fs * 512:(fs + 1) * 512],
                                    rb[:, i * 512 + fs * 128: i * 512 + fs * 128 + 128],
                                    A[:, (dpg * 4 + i) * TOK + hoff: (dpg * 4 + i) * TOK + hoff + 512],
                                    start=(dpg == 0 and i == 0), stop=(dpg == 3 and i == 3)))
                                for i in range(4)], extra=(ex if fs == 0 else ()))
                        done_use()
                        cq_step()
                        cq_step()
                    for fs in range(4):
                        ff = fb * 4 + fs
                        ctx.do("scalar", lambda e, ff=ff, fs=fs: e.activation(out=B[:, ff * 512:(ff + 1) * 512], in_=PSA[:, fs * 512:(fs + 1) * 512], func=AF.Gelu, bias=bcol[:, 32 + ff:33 + ff]), sem=sem, wait=w())
                for ft in range(ND):
                    for sub in range(4):
                        rb, ex = use()
                        chain("tensor", [
                            (lambda e, fi=fi, rb=rb, sub=sub, hoff=hoff: e.matmul(
                                psC,
                                rb[:, fi * 128:(fi + 1) * 128],
                                B[:, (sub * 16 + fi) * 512:(sub * 16 + fi + 1) * 512],
                                start=(sub == 0 and fi == 0), stop=(sub == 3 and fi == 15)))
                            for fi in range(16)], extra=ex)
                        done_use()
                        cq_step()
                        cq_step()
                    ctx.do("scalar", lambda e, ft=ft: e.activation(out=tf_sb[:], in_=psC, func=AF.Identity, scale=1.0, bias=bcol[:, 16 + ft:17 + ft]), sem=sem, wait=w())
                    ctx.do("vector", lambda e, ft=ft, hoff=hoff: e.tensor_add(out=x_fm[:, ft * TOK + hoff: ft * TOK + hoff + 512], in0=x_fm[:, ft * TOK + hoff: ft * TOK + hoff + 512], in1=tf_sb[:]), sem=sem, wait=w())
                    ctx.do("sync", lambda e, ft=ft, hoff=hoff: e.dma_start(out=x2_d[ft * 128:(ft + 1) * 128, hoff:hoff + 512], in_=x_fm[:, ft * TOK + hoff: ft * TOK + hoff + 512]), sem=sem, inc=16, wait=w())
            if cq_on:
                while cq_st["i"] < len(CQ):
                    cq_step()
                for en in ("gpsimd", "sync", "scalar"):
                    ctx.wait(en, io["pq_out"], cq_bases["out"] + 16 * len(CQ))
                ctx.do("gpsimd", lambda e: e.nop(), sem=sem, wait=w())
    return emit


def build_u3():
    nc = bass.Bass("TRN2", target_bir_lowering=False)
    io = {
        "x0": nc.dram_tensor("x", [D, TOK], F32, kind="ExternalInput").ap(),
        "attn": nc.dram_tensor("attn", [D, TOK], BF16, kind="ExternalInput").ap(),
        "woT": nc.dram_tensor("woT", [D, D], F32, kind="ExternalInput").ap(),
        "bo": nc.dram_tensor("bo", [D], F32, kind="ExternalInput").ap(),
        "w1q": nc.dram_tensor("w1q", [D, DFF0], BF16, kind="ExternalInput").ap(),
        "b1": nc.dram_tensor("b1", [DFF0], F32, kind="ExternalInput").ap(),
        "w2q": nc.dram_tensor("w2q", [DFF0, D], BF16, kind="ExternalInput").ap(),
        "b2": nc.dram_tensor("b2", [D], F32, kind="ExternalInput").ap(),
        "x2": nc.dram_tensor("x2", [D, TOK], F32, kind="ExternalOutput").ap(),
    }
    io["attn_tile"] = lambda d: io["attn"][d * 128:(d + 1) * 128, :]
    st = contextlib.ExitStack()
    io["PSA"] = st.enter_context(nc.psum_tensor("PSA", [128, 2048], F32))
    io["PSB"] = st.enter_context(nc.psum_tensor("PSB", [128, 2048], F32))
    io["sst"] = st.enter_context(nc.semaphore("sst"))
    ab = Arena(nc)
    run_phases(nc, [make_u3(nc, io, ab)])
    st.close()
    return nc


def emit_ln_fm2(ctx, sem, x_src, out_fm, hp, tc_list, in_stride, out_stride, nd=ND):
    d_model = nd * 128
    w = lambda: [(sem, ctx.val(sem))]
    tb0, tb1, abc, bbc, row = hp["tb0"], hp["tb1"], hp["abc"], hp["bbc"], hp["row"]
    psA, psB = hp["psA"], hp["psB"]
    oc, orow = hp["ones_col_b"], hp["ones_row"]
    for tc in tc_list:
        for d in range(nd):
            ctx.do("scalar", lambda e, d=d, tc=tc: e.activation(out=tb0[:], in_=x_src(d, tc), func=AF.Identity), sem=sem, wait=w())
            ctx.do("scalar", lambda e, d=d, tc=tc: e.activation(out=tb1[:], in_=x_src(d, tc), func=AF.Square), sem=sem, wait=w())
            ctx.do("tensor", lambda e, d=d: e.matmul(psA[0:1, 0:512], oc[:], tb0[:], start=(d == 0), stop=(d == nd - 1)), sem=sem, wait=w())
            ctx.do("tensor", lambda e, d=d: e.matmul(psA[0:1, 512:1024], oc[:], tb1[:], start=(d == 0), stop=(d == nd - 1)), sem=sem, wait=w())
        ctx.do("vector", lambda e: e.tensor_scalar(out=row[0:1, 0:512], in0=psA[0:1, 0:512], scalar1=1.0 / d_model, scalar2=None, op0=ALU.mult), sem=sem, wait=w())
        ctx.do("vector", lambda e: e.tensor_scalar(out=row[0:1, 512:1024], in0=psA[0:1, 512:1024], scalar1=1.0 / d_model, scalar2=None, op0=ALU.mult), sem=sem, wait=w())
        ctx.do("vector", lambda e: e.tensor_tensor(out=row[0:1, 1024:1536], in0=row[0:1, 0:512], in1=row[0:1, 0:512], op=ALU.mult), sem=sem, wait=w())
        ctx.do("vector", lambda e: e.tensor_tensor(out=row[0:1, 512:1024], in0=row[0:1, 512:1024], in1=row[0:1, 1024:1536], op=ALU.subtract), sem=sem, wait=w())
        ctx.do("vector", lambda e: e.tensor_scalar(out=row[0:1, 512:1024], in0=row[0:1, 512:1024], scalar1=1e-5, scalar2=None, op0=ALU.add), sem=sem, wait=w())
        ctx.do("scalar", lambda e: e.activation(out=row[0:1, 512:1024], in_=row[0:1, 512:1024], func=AF.Ln), sem=sem, wait=w())
        ctx.do("scalar", lambda e: e.activation(out=row[0:1, 512:1024], in_=row[0:1, 512:1024], func=AF.Exp, scale=-0.5), sem=sem, wait=w())
        ctx.do("vector", lambda e: e.tensor_tensor(out=row[0:1, 1024:1536], in0=row[0:1, 0:512], in1=row[0:1, 512:1024], op=ALU.mult), sem=sem, wait=w())
        ctx.do("vector", lambda e: e.tensor_scalar(out=row[0:1, 1024:1536], in0=row[0:1, 1024:1536], scalar1=-1.0, scalar2=None, op0=ALU.mult), sem=sem, wait=w())
        ctx.do("tensor", lambda e: e.matmul(psB[:, 0:512], orow[:], row[0:1, 512:1024], start=True, stop=True), sem=sem, wait=w())
        ctx.do("scalar", lambda e: e.activation(out=abc[:], in_=psB[:, 0:512], func=AF.Copy), sem=sem, wait=w())
        ctx.do("tensor", lambda e: e.matmul(psB[:, 0:512], orow[:], row[0:1, 1024:1536], start=True, stop=True), sem=sem, wait=w())
        ctx.do("scalar", lambda e: e.activation(out=bbc[:], in_=psB[:, 0:512], func=AF.Copy), sem=sem, wait=w())
        for d in range(nd):
            ctx.do("vector", lambda e, d=d, tc=tc: e.tensor_tensor(out=hp["tf"][:], in0=x_src(d, tc), in1=abc[:], op=ALU.mult), sem=sem, wait=w())
            ctx.do("vector", lambda e, d=d, tc=tc: e.tensor_tensor(out=out_fm[:, d * out_stride + tc * 512: d * out_stride + tc * 512 + 512], in0=hp["tf"][:], in1=bbc[:], op=ALU.add), sem=sem, wait=w())


def make_u5(nc, io, ab, mode, halo_mode="compute", skip_front=False):
    ab.reset("u5_")
    x_d = io["x_m"]
    ipT_d, cw_d, cb_d, alog_d, dtb_d = io["ipT"], io["cw"], io["cb"], io["alog"], io["dtb"]
    xh_d = io.get("xh")
    if mode == "b":
        dvec_d, nw_d, opT_d = io["dvec"], io["nw"], io["opT"]
        sf_d, phiT_d, mask_d = io.get("sf"), io.get("phiT"), io["maskr"]
        ws_scr = io["ws_scr"]
        x3_d = io["x3"]
    else:
        sfin_d = io["sfin"]
        phi_d = io["phi"]
    acs_scr = io["acs_scr"]
    zmask_d = io.get("zmask")
    xssm_scr = io["xssm_scr"]
    z_scr = io.get("z_scr")
    PSA, PSB = io["PSA"], io["PSB"]
    sb = ab.sb
    A32 = sb("A32", [128, 16384], BF16)
    XCH = sb("XCH", [128, 8192], F32)
    BIG1 = sb("BIG1", [128, 4096], F32)
    BIG2 = sb("BIG2", [128, 4096], BF16)
    BIG3 = sb("BIG3", [128, 4096], BF16)
    BIG4 = sb("BIG4", [128, 4096], BF16)
    y_tile = sb("y_tile", [128, 4096], BF16)
    hp = {
        "tb0": sb("tb0", [128, 512], BF16),
        "tb1": sb("tb1", [128, 512], BF16),
        "tf": sb("tf", [128, 512], F32),
        "abc": sb("abc", [128, 512], F32),
        "bbc": sb("bbc", [128, 512], F32),
        "row": sb("row", [1, 1536], F32),
        "ones_col_b": sb("ones_col_b", [128, 1], BF16),
        "ones_row": sb("ones_row", [1, 128], F32),
    }
    identb = sb("identb", [128, 128], BF16)
    identf = sb("identf", [128, 128], F32)
    xbcr = sb("xbcr", [128, 1028], BF16)
    xbs = sb("xbs", [128, 1024], BF16)
    BC = sb("BC", [128, 2 * TOK], BF16)
    dt_fm = sb("dt_fm", [64, TOK], F32)
    adisc = sb("adisc", [64, TOK], F32)
    acs = sb("acs", [64, TOK], F32)
    cw_sb = sb("cw_sb", [128, 136], F32)
    cb_sb = sb("cb_sb", [128, 34], F32)
    small = sb("small", [64, 16], F32)
    acsT = sb("acsT", [64, 64], F32)
    eacsT = sb("eacsT", [64, 64], F32)
    ad63 = sb("ad63", [64, 64], F32)
    w_sh = sb("w_sh", [64, 64], F32)
    dtT = sb("dtT", [64, 64], F32)
    btm = sb("btm", [64, 128], BF16)
    gt = sb("gt", [64, 64], BF16)
    er63 = sb("er63", [1, 64], F32)
    E63 = sb("E63", [128, 64], F32)
    D64 = sb("D64", [128, 64], F32)
    S_run = sb("S_run", [128, DIN], F32)
    st_tmp = sb("st_tmp", [128, 2048], F32)
    syb = sb("syb", [128, DIN], BF16)
    x2s = sb("x2s", [128, 512], F32)
    lnxh = sb("lnxh", [128, ND * 4], BF16)
    xht = sb("xht", [128, ND * 4], F32)
    rs_col = sb("rs_col", [128, 2], F32)
    ph8 = sb("ph8", [64, 32], F32)
    ws8 = sb("ws8", [8, 64], F32)
    wsrow = sb("wsrow", [1, 512], F32)
    zt = sb("zt", [128, 1024], BF16) if mode == "b" else None
    stag4 = sb("stag4", [128, 512], BF16)
    rawf3 = sb("rawf3", [128, 34 * 4], BF16)
    hal_all = sb("hal_all", [128, 34 * 4], BF16)
    zmask_sb = sb("zmask_sb", [128, 1], F32)
    hal3 = sb("hal3", [128, 8], BF16)

    psC = PSA[:, 0:512]
    psA_ln = PSB[0:1, 0:1024]
    psB_ln = PSB[:, 1024:1536]
    psT = PSB[:, 512:576].bitcast(BF16)
    psF = PSB[:, 576:704]
    hp["psA"] = psA_ln
    hp["psB"] = psB_ln

    def emit(ctx, sem):
        if True:
            w = lambda: [(sem, ctx.val(sem))]

            def chain(which, fns, inc_last=1, extra=()):
                if ctx.en == which:
                    ctx.eng.wait_ge(sem, ctx.val(sem))
                    for (s2, v2) in extra:
                        ctx.eng.wait_ge(s2, v2)
                    for fn in fns:
                        ins = fn(ctx.eng)
                    ins.then_inc(sem, inc_last)
                ctx._bump(sem, inc_last)

            def do(which, f2):
                ctx.do(which, f2, sem=sem, wait=w())

            def dma(f2, which="sync"):
                ctx.do(which, f2, sem=sem, inc=16, wait=w())

            do("gpsimd", lambda e: e.memset(hp["ones_col_b"][:], 1.0))
            do("gpsimd", lambda e: e.memset(hp["ones_row"][:], 1.0))
            do("gpsimd", lambda e: e.memset(identb[:], 1.0))
            do("gpsimd", lambda e: e.affine_select(out=identb[:], in_=identb[:], compare_op=ALU.is_equal, fill=0.0, base=0, pattern=[[-1, 128]], channel_multiplier=1))
            do("gpsimd", lambda e: e.memset(identf[:], 1.0))
            do("gpsimd", lambda e: e.affine_select(out=identf[:], in_=identf[:], compare_op=ALU.is_equal, fill=0.0, base=0, pattern=[[-1, 128]], channel_multiplier=1))
            for ct in range(34):
                dma(lambda e, ct=ct: e.dma_start(out=cw_sb[:, ct * 4:(ct + 1) * 4], in_=cw_d[ct * 128:(ct + 1) * 128, :]))
                dma(lambda e, ct=ct: e.dma_start(out=cb_sb[:, ct:ct + 1], in_=cb_d[ct * 128:(ct + 1) * 128].unsqueeze(1)))
            if halo_mode == "gather":
                dma(lambda e: e.dma_start(out=zmask_sb[:, :], in_=zmask_d[:, :]))
            dma(lambda e: e.dma_start(out=small[:, 0:1], in_=alog_d[:].unsqueeze(1)))
            dma(lambda e: e.dma_start(out=small[:, 2:3], in_=dtb_d[:].unsqueeze(1)))
            do("scalar", lambda e: e.activation(out=small[:, 1:2], in_=small[:, 0:1], func=AF.Exp))
            do("vector", lambda e: e.tensor_scalar(out=small[:, 1:2], in0=small[:, 1:2], scalar1=-1.0, scalar2=None, op0=ALU.mult))

            if mode == "b":
                if "phi_col" in io:
                    for j in range(NCORES):
                        dma(lambda e, j=j: e.dma_start(out=ph8[:, j:j + 1], in_=io["phi_col"](j)))
                else:
                    dma(lambda e: e.dma_start(out=ph8[:, 0:8], in_=phiT_d[:, :]))
                dma(lambda e: e.dma_start(out=ph8[:, 8:16], in_=mask_d[:, :]))
                do("vector", lambda e: e.tensor_tensor(out=ph8[:, 16:24], in0=ph8[:, 0:8], in1=ph8[:, 8:16], op=ALU.mult))
                do("gpsimd", lambda e: e.memset(ph8[:, 24:32], 0.0))
                for j in range(6, -1, -1):
                    do("vector", lambda e, j=j: e.tensor_tensor(out=ph8[:, 24 + j:25 + j], in0=ph8[:, 25 + j:26 + j], in1=ph8[:, 17 + j:18 + j], op=ALU.add))
                do("scalar", lambda e: e.activation(out=ph8[:, 24:32], in_=ph8[:, 24:32], func=AF.Exp))
                do("vector", lambda e: e.tensor_tensor(out=ph8[:, 24:32], in0=ph8[:, 24:32], in1=ph8[:, 8:16], op=ALU.mult))
                do("tensor", lambda e: e.transpose(psF[0:8, 0:64], ph8[0:64, 24:32], identf[0:64, 0:64]))
                do("scalar", lambda e: e.activation(out=ws8[:, :], in_=psF[0:8, 0:64], func=AF.Copy))
                dma(lambda e: e.dma_start(out=ws_scr[:, :], in_=ws8[:, :]))
                dma(lambda e: e.dma_start(out=er63[0:1, :].broadcast_to([1, 64]) if False else ph8[0:1, 0:0] if False else wsrow[0:1, :], in_=ws_scr.rearrange("a b -> (a b)").unsqueeze(0)))
                do("gpsimd", lambda e: e.memset(S_run[:], 0.0))
                sfb = [BIG1[:, 0:4096], XCH[:, 0:4096]]
                sf_src = lambda j: (io["sf_row"](j) if "sf_row" in io else sf_d[j * 128:(j + 1) * 128, :])
                sdn = [None] * NCORES
                cons = [None] * NCORES
                sdn[0] = ctx.do("sync", lambda e: e.dma_start(out=sfb[0], in_=sf_src(0)), sem=io["sst"], inc=16, wait=w())
                for j in range(NCORES):
                    if j + 1 < NCORES:
                        wt = (w() if j == 0 else [(sem, cons[j - 1])]) + [(io["sst"], sdn[j])]
                        sdn[j + 1] = ctx.do("sync", lambda e, j=j: e.dma_start(out=sfb[(j + 1) % 2], in_=sf_src(j + 1)), sem=io["sst"], inc=16, wait=wt)
                    do("tensor", lambda e, j=j: e.matmul(psC[:, 0:64], hp["ones_row"][:], wsrow[0:1, j * 64:(j + 1) * 64], start=True, stop=True))
                    do("scalar", lambda e: e.activation(out=E63[:], in_=psC[:, 0:64], func=AF.Copy))
                    for hh in range(2):
                        ctx.do("vector", lambda e, hh=hh, j=j: e.tensor_tensor(out=st_tmp[:], in0=sfb[j % 2][:, hh * 2048:(hh + 1) * 2048], in1=E63[:, hh * 32:(hh + 1) * 32].unsqueeze(2).broadcast_to([128, 32, 64]), op=ALU.mult), sem=sem, wait=w() + [(io["sst"], sdn[j])])
                        do("vector", lambda e, hh=hh: e.tensor_tensor(out=S_run[:, hh * 2048:(hh + 1) * 2048], in0=S_run[:, hh * 2048:(hh + 1) * 2048], in1=st_tmp[:], op=ALU.add))
                    cons[j] = ctx.val(sem)
            else:
                do("gpsimd", lambda e: e.memset(S_run[:], 0.0))

            for tc in range(2 if not skip_front else 0):
                for d in range(ND):
                    dma(lambda e, d=d, tc=tc: e.dma_start(out=XCH[:, d * 512:(d + 1) * 512], in_=x_d[d * 128:(d + 1) * 128, tc * 512:(tc + 1) * 512]))
                emit_ln_fm2(ctx, sem, lambda d, tcc: XCH[:, d * 512:(d + 1) * 512], A32, hp, [tc], 512, 1024)
            for d in range(ND if (halo_mode == "compute" and not skip_front) else 0):
                dma(lambda e, d=d: e.dma_start(out=xht[:, d * 4:d * 4 + 3], in_=xh_d[d * 128:(d + 1) * 128, :]))
            for d in range(ND if halo_mode == "compute" else 0):
                do("scalar", lambda e, d=d: e.activation(out=hp["tb0"][:, 0:3], in_=xht[:, d * 4:d * 4 + 3], func=AF.Identity))
                do("scalar", lambda e, d=d: e.activation(out=hp["tb1"][:, 0:3], in_=xht[:, d * 4:d * 4 + 3], func=AF.Square))
                do("tensor", lambda e, d=d: e.matmul(psA_ln[0:1, 0:3], hp["ones_col_b"][:], hp["tb0"][:, 0:3], start=(d == 0), stop=(d == ND - 1)))
                do("tensor", lambda e, d=d: e.matmul(psA_ln[0:1, 4:7], hp["ones_col_b"][:], hp["tb1"][:, 0:3], start=(d == 0), stop=(d == ND - 1)))
            R = hp["row"]
            if halo_mode == "compute":
              do("vector", lambda e: e.tensor_scalar(out=R[0:1, 0:3], in0=psA_ln[0:1, 0:3], scalar1=1.0 / D, scalar2=None, op0=ALU.mult))
              do("vector", lambda e: e.tensor_scalar(out=R[0:1, 4:7], in0=psA_ln[0:1, 4:7], scalar1=1.0 / D, scalar2=None, op0=ALU.mult))
              do("vector", lambda e: e.tensor_tensor(out=R[0:1, 8:11], in0=R[0:1, 0:3], in1=R[0:1, 0:3], op=ALU.mult))
              do("vector", lambda e: e.tensor_tensor(out=R[0:1, 4:7], in0=R[0:1, 4:7], in1=R[0:1, 8:11], op=ALU.subtract))
              do("vector", lambda e: e.tensor_scalar(out=R[0:1, 4:7], in0=R[0:1, 4:7], scalar1=1e-5, scalar2=None, op0=ALU.add))
              do("scalar", lambda e: e.activation(out=R[0:1, 4:7], in_=R[0:1, 4:7], func=AF.Ln))
              do("scalar", lambda e: e.activation(out=R[0:1, 4:7], in_=R[0:1, 4:7], func=AF.Exp, scale=-0.5))
              do("vector", lambda e: e.tensor_tensor(out=R[0:1, 8:11], in0=R[0:1, 0:3], in1=R[0:1, 4:7], op=ALU.mult))
              do("vector", lambda e: e.tensor_scalar(out=R[0:1, 8:11], in0=R[0:1, 8:11], scalar1=-1.0, scalar2=None, op0=ALU.mult))
              do("tensor", lambda e: e.matmul(psB_ln[:, 0:3], hp["ones_row"][:], R[0:1, 4:7], start=True, stop=True))
              do("scalar", lambda e: e.activation(out=hp["abc"][:, 0:3], in_=psB_ln[:, 0:3], func=AF.Copy))
              do("tensor", lambda e: e.matmul(psB_ln[:, 0:3], hp["ones_row"][:], R[0:1, 8:11], start=True, stop=True))
              do("scalar", lambda e: e.activation(out=hp["bbc"][:, 0:3], in_=psB_ln[:, 0:3], func=AF.Copy))
              for d in range(ND):
                  do("vector", lambda e, d=d: e.tensor_tensor(out=hp["tf"][:, 0:3], in0=xht[:, d * 4:d * 4 + 3], in1=hp["abc"][:, 0:3], op=ALU.mult))
                  do("vector", lambda e, d=d: e.tensor_tensor(out=lnxh[:, d * 4:d * 4 + 3], in0=hp["tf"][:, 0:3], in1=hp["bbc"][:, 0:3], op=ALU.add))

            sst = io["sst"]
            sring = {"n": 0, "base": ctx.val(sst)}
            ot_list = (list(range(32)) if mode == "b" else []) + ([] if skip_front else list(range(32, 67)))
            ipsem = io.get("pq_in", sst)
            ipb = [BIG2, BIG4]
            ipdn = []
            ipcons = []

            def ip_pump(k):
                if k >= len(ot_list):
                    return
                oc2 = ot_list[k]
                nc2 = 64 if oc2 == 66 else 128
                wt = (w() if k < 2 else [(sem, ipcons[k - 2])]) + ([] if not ipdn else [(ipsem, ipdn[-1])])
                ipdn.append(ctx.do("gpsimd", lambda e, oc2=oc2, nc2=nc2, k=k: e.dma_start(
                    out=ipb[k % 2][:, 0:16 * nc2].rearrange("p (a c) -> p a c", c=nc2),
                    in_=ipT_d[:, oc2 * 128: oc2 * 128 + nc2].rearrange("(a p) c -> p a c", p=128)), sem=ipsem, inc=16, wait=wt))

            ip_pump(0)
            ip_pump(1)
            for k, oc in enumerate(ot_list):
                ncols = 64 if oc == 66 else 128
                ipbuf = ipb[k % 2]
                for tc in range(2):
                    chain("tensor", [
                        (lambda e, d=d, tc=tc, ncols=ncols, ipbuf=ipbuf: e.matmul(PSA[0:ncols, tc * 512:tc * 512 + 512], ipbuf[:, d * ncols:(d + 1) * ncols], A32[:, d * 1024 + tc * 512: d * 1024 + tc * 512 + 512], start=(d == 0), stop=(d == ND - 1)))
                        for d in range(ND)], extra=([(ipsem, ipdn[k])] if tc == 0 else ()))
                ipcons.append(ctx.val(sem))
                ip_pump(k + 2)
                if oc < 32:
                    do("scalar", lambda e: e.activation(out=zt[:, 0:512], in_=PSA[:, 0:512], func=AF.Silu))
                    do("scalar", lambda e: e.activation(out=zt[:, 512:1024], in_=PSA[:, 512:1024], func=AF.Silu))
                    for tt in range(8):
                        do("tensor", lambda e, tt=tt: e.transpose(psT[:], zt[:, tt * 128:(tt + 1) * 128], identb[:]))
                        slot = sring["n"] % 4
                        wv = [(sem, ctx.val(sem))]
                        if sring["n"] >= 4:
                            wv.append((sst, sring["base"] + (sring["n"] - 3) * 16))
                        ctx.do("vector", lambda e, slot=slot: e.tensor_copy(out=stag4[:, slot * 128:(slot + 1) * 128], in_=psT[:]), sem=sem, wait=wv)
                        ctx.do("gpsimd", lambda e, tt=tt, oc=oc, slot=slot: e.dma_start(out=z_scr[tt * 128:(tt + 1) * 128, oc * 128:(oc + 1) * 128], in_=stag4[:, slot * 128:(slot + 1) * 128]), sem=sst, inc=16, wait=[(sem, ctx.val(sem))])
                        sring["n"] += 1
                elif oc < 66:
                    ct = oc - 32
                    if halo_mode == "compute":
                        chain("tensor", [
                            (lambda e, d=d: e.matmul(PSB[0:128, 0:3], BIG2[:, d * 128:(d + 1) * 128], lnxh[:, d * 4:d * 4 + 3], start=(d == 0), stop=(d == ND - 1)))
                            for d in range(ND)])
                        do("scalar", lambda e: e.activation(out=xbcr[:, 0:3], in_=PSB[0:128, 0:3], func=AF.Copy))
                    else:
                        do("gpsimd", lambda e: e.memset(xbcr[:, 0:3], 0.0))
                    do("scalar", lambda e: e.activation(out=xbcr[:, 3:515], in_=PSA[:, 0:512], func=AF.Copy))
                    do("scalar", lambda e: e.activation(out=xbcr[:, 515:1027], in_=PSA[:, 512:1024], func=AF.Copy))
                    if halo_mode == "gather":
                        dma(lambda e, ct=ct: e.dma_start(out=io["xbce_c"][ct * 128:(ct + 1) * 128, 0:3], in_=xbcr[:, 1024:1027]))
                        do("vector", lambda e, ct=ct: e.tensor_copy(out=rawf3[:, ct * 4:ct * 4 + 3], in_=xbcr[:, 3:6]))
                    do("vector", lambda e, ct=ct: e.tensor_scalar(out=xbs[:], in0=xbcr[:, 0:1024], scalar1=cw_sb[:, ct * 4:ct * 4 + 1], scalar2=None, op0=ALU.mult))
                    for j in range(1, 4):
                        do("vector", lambda e, ct=ct, j=j: e.scalar_tensor_tensor(out=xbs[:], in0=xbcr[:, j:j + 1024], scalar=cw_sb[:, ct * 4 + j:ct * 4 + j + 1], in1=xbs[:], op0=ALU.mult, op1=ALU.add))
                    do("scalar", lambda e, ct=ct: e.activation(out=xbs[:], in_=xbs[:], func=AF.Silu, bias=cb_sb[:, ct:ct + 1]))
                    if ct < 32:
                        for tt in range(8):
                            do("tensor", lambda e, tt=tt: e.transpose(psT[:], xbs[:, tt * 128:(tt + 1) * 128], identb[:]))
                            slot = sring["n"] % 4
                            wv = [(sem, ctx.val(sem))]
                            if sring["n"] >= 4:
                                wv.append((sst, sring["base"] + (sring["n"] - 3) * 16))
                            ctx.do("vector", lambda e, slot=slot: e.tensor_copy(out=stag4[:, slot * 128:(slot + 1) * 128], in_=psT[:]), sem=sem, wait=wv)
                            ctx.do("gpsimd", lambda e, tt=tt, ct=ct, slot=slot: e.dma_start(out=xssm_scr[tt * 128:(tt + 1) * 128, ct * 128:(ct + 1) * 128], in_=stag4[:, slot * 128:(slot + 1) * 128]), sem=sst, inc=16, wait=[(sem, ctx.val(sem))])
                            sring["n"] += 1
                    else:
                        do("vector", lambda e, ct=ct: e.tensor_copy(out=BC[:, (ct - 32) * 1024:(ct - 31) * 1024], in_=xbs[:]))
                else:
                    do("scalar", lambda e: e.activation(out=dt_fm[:, 0:512], in_=PSA[0:64, 0:512], func=AF.Identity, bias=small[:, 2:3]))
                    do("scalar", lambda e: e.activation(out=dt_fm[:, 512:1024], in_=PSA[0:64, 512:1024], func=AF.Identity, bias=small[:, 2:3]))
                    do("scalar", lambda e: e.activation(out=adisc[:, 0:1024], in_=dt_fm[:], func=AF.Abs))
                    do("scalar", lambda e: e.activation(out=adisc[:, 0:1024], in_=adisc[:, 0:1024], func=AF.Exp, scale=-1.0))
                    do("vector", lambda e: e.tensor_scalar(out=adisc[:, 0:1024], in0=adisc[:, 0:1024], scalar1=1.0, scalar2=None, op0=ALU.add))
                    do("scalar", lambda e: e.activation(out=adisc[:, 0:1024], in_=adisc[:, 0:1024], func=AF.Ln))
                    do("vector", lambda e: e.tensor_scalar(out=dt_fm[:], in0=dt_fm[:], scalar1=0.0, scalar2=None, op0=ALU.max))
                    do("vector", lambda e: e.tensor_tensor(out=dt_fm[:], in0=dt_fm[:], in1=adisc[:, 0:1024], op=ALU.add))
            if halo_mode == "gather" and not skip_front:
                if io.get("mid_ag3") is not None and mode == "a":
                    io["mid_ag3"](ctx, sem)
                xbce_view = io["xbce_all"].rearrange("(j ct p) c -> j p ct c", p=128, ct=34)
                if ctx.en == "sync":
                    ctx.eng.wait_ge(sem, ctx.val(sem))
                    pprev = ctx.eng.snap((ctx.eng.partition_id() + 7) % 8)
                    ins = ctx.eng.dma_start(out=hal_all[:].rearrange("p (ct c) -> p ct c", c=4), in_=xbce_view[bass.ds(pprev, 1)].squeeze(0))
                    ins.then_inc(sem, 16)
                ctx._bump(sem, 16)
                do("vector", lambda e: e.tensor_scalar(out=hal_all[:], in0=hal_all[:], scalar1=zmask_sb[:, 0:1], scalar2=None, op0=ALU.mult))
                for ct in range(34):
                    do("vector", lambda e, ct=ct: e.tensor_copy(out=hal3[:, 0:3], in_=hal_all[:, ct * 4:ct * 4 + 3]))
                    do("vector", lambda e, ct=ct: e.tensor_copy(out=hal3[:, 3:6], in_=rawf3[:, ct * 4:ct * 4 + 3]))
                    do("vector", lambda e, ct=ct: e.tensor_scalar(out=xbs[:, 0:3], in0=hal3[:, 0:3], scalar1=cw_sb[:, ct * 4:ct * 4 + 1], scalar2=None, op0=ALU.mult))
                    for j in range(1, 4):
                        do("vector", lambda e, ct=ct, j=j: e.scalar_tensor_tensor(out=xbs[:, 0:3], in0=hal3[:, j:j + 3], scalar=cw_sb[:, ct * 4 + j:ct * 4 + j + 1], in1=xbs[:, 0:3], op0=ALU.mult, op1=ALU.add))
                    do("scalar", lambda e, ct=ct: e.activation(out=xbs[:, 0:3], in_=xbs[:, 0:3], func=AF.Silu, bias=cb_sb[:, ct:ct + 1]))
                    if ct < 32:
                        do("tensor", lambda e: e.transpose(psT[0:3, 0:128], xbs[:, 0:3], identb[:]))
                        do("scalar", lambda e: e.activation(out=xbcr[0:3, 0:128], in_=psT[0:3, 0:128], func=AF.Copy))
                        dma(lambda e, ct=ct: e.dma_start(out=xssm_scr[0:3, ct * 128:(ct + 1) * 128], in_=xbcr[0:3, 0:128]))
                    else:
                        do("vector", lambda e, ct=ct: e.tensor_copy(out=BC[:, (ct - 32) * 1024:(ct - 32) * 1024 + 3], in_=xbs[:, 0:3]))
            if not skip_front:
                do("vector", lambda e: e.tensor_scalar(out=adisc[:], in0=dt_fm[:], scalar1=small[:, 1:2], scalar2=None, op0=ALU.mult))
                for c in range(NCHUNK):
                    do("vector", lambda e, c=c: e.tensor_tensor_scan(out=acs[:, c * 64:(c + 1) * 64], data0=adisc[:, c * 64:(c + 1) * 64], data1=adisc[:, c * 64:(c + 1) * 64], initial=0.0, op0=ALU.add, op1=ALU.bypass))
            if mode == "a":
                do("vector", lambda e: e.reduce_sum(out=small[:, 3:4], in_=adisc[:], axis=AX.X))
                if io.get("phi_is_row"):
                    dma(lambda e: e.dma_start(out=phi_d[0:1, 0:64], in_=small[:, 3:4].transpose([1, 0]) if False else small[:, 3:4]))
                else:
                    dma(lambda e: e.dma_start(out=phi_d[:, :], in_=small[:, 3:4]))
            if mode == "b":
                dma(lambda e: e.dma_start(out=er63[0:1, :], in_=dvec_d[:].unsqueeze(0)))
                do("tensor", lambda e: e.matmul(psC[:, 0:64], hp["ones_row"][:], er63[0:1, :], start=True, stop=True))
                do("scalar", lambda e: e.activation(out=D64[:], in_=psC[:, 0:64], func=AF.Copy))

            ctx.wait("sync", sst, ctx.val(sst))
            ctx.wait("gpsimd", sst, ctx.val(sst))
            for c in range(NCHUNK):
                ac = acs[:, c * 64:(c + 1) * 64]
                dtc = dt_fm[:, c * 64:(c + 1) * 64]
                dma(lambda e, c=c: e.dma_start(out=acs_scr[:, :], in_=ac))
                do("tensor", lambda e: e.transpose(psF[0:64, 0:64], ac, identf[0:64, 0:64]))
                do("scalar", lambda e: e.activation(out=acsT[:], in_=psF[0:64, 0:64], func=AF.Copy))
                do("tensor", lambda e: e.transpose(psF[0:64, 0:64], dtc, identf[0:64, 0:64]))
                do("scalar", lambda e: e.activation(out=dtT[:], in_=psF[0:64, 0:64], func=AF.Copy))
                if mode == "b":
                    do("scalar", lambda e: e.activation(out=eacsT[:], in_=acsT[:], func=AF.Exp))
                do("vector", lambda e: e.tensor_tensor(out=ad63[:], in0=ac[:, 63:64].broadcast_to([64, 64]), in1=ac, op=ALU.subtract))
                do("scalar", lambda e: e.activation(out=ad63[:], in_=ad63[:], func=AF.Exp))
                do("vector", lambda e: e.tensor_tensor(out=ad63[:], in0=ad63[:], in1=dtc, op=ALU.mult))
                do("tensor", lambda e: e.transpose(psF[0:64, 0:64], ad63[:], identf[0:64, 0:64]))
                do("scalar", lambda e: e.activation(out=w_sh[:], in_=psF[0:64, 0:64], func=AF.Copy))
                do("scalar", lambda e: e.activation(out=small[:, 4:5], in_=ac[:, 63:64], func=AF.Exp))
                do("tensor", lambda e: e.transpose(psF[0:1, 0:64], small[:, 4:5], identf[0:64, 0:64]))
                do("scalar", lambda e: e.activation(out=er63[0:1, :], in_=psF[0:1, 0:64], func=AF.Copy))
                do("tensor", lambda e: e.matmul(psC[:, 0:64], hp["ones_row"][:], er63[0:1, :], start=True, stop=True))
                do("scalar", lambda e: e.activation(out=E63[:], in_=psC[:, 0:64], func=AF.Copy))
                do("tensor", lambda e, c=c: e.transpose(psT[0:64, 0:128], BC[:, c * 64:(c + 1) * 64], identb[:]))
                do("vector", lambda e: e.tensor_copy(out=btm[:], in_=psT[0:64, :]))
                if mode == "b":
                    do("tensor", lambda e, c=c: e.matmul(psF[0:64, 0:64], BC[:, c * 64:(c + 1) * 64], BC[:, 1024 + c * 64: 1024 + (c + 1) * 64], start=True, stop=True))
                    do("scalar", lambda e: e.activation(out=gt[:], in_=psF[0:64, 0:64], func=AF.Copy))
                    do("gpsimd", lambda e: e.affine_select(out=gt[:], in_=gt[:], compare_op=ALU.is_ge, fill=0.0, base=0, pattern=[[1, 64]], channel_multiplier=-1))
                    dma(lambda e: e.dma_start(out=BIG1[0:64, :].rearrange("s (h l) -> s h l", h=64), in_=AP(tensor=acs_scr.tensor, offset=0, ap=[[0, 64], [64, 64], [1, 64]])))
                    do("vector", lambda e: e.tensor_tensor(out=BIG1[0:64, :], in0=BIG1[0:64, :], in1=acsT[:].unsqueeze(2).broadcast_to([64, 64, 64]), op=ALU.subtract))
                    do("vector", lambda e: e.tensor_scalar(out=BIG1[0:64, :], in0=BIG1[0:64, :], scalar1=0.0, scalar2=None, op0=ALU.min))
                    do("scalar", lambda e: e.activation(out=BIG4[0:64, :], in_=BIG1[0:64, :], func=AF.Exp))
                    do("gpsimd", lambda e: e.tensor_tensor(out=BIG4[0:64, :], in0=BIG4[0:64, :], in1=gt[:].unsqueeze(1).broadcast_to([64, 64, 64]), op=ALU.mult))
                dma(lambda e, c=c: e.dma_start(out=BIG2[0:64, :], in_=xssm_scr[c * 64:(c + 1) * 64, :]))
                do("vector", lambda e: e.tensor_tensor(out=BIG2[0:64, :], in0=BIG2[0:64, :], in1=dtT[:].unsqueeze(2).broadcast_to([64, 64, 64]), op=ALU.mult))
                do("vector", lambda e: e.tensor_tensor(out=BIG3[0:64, :], in0=BIG2[0:64, :], in1=w_sh[:].unsqueeze(2).broadcast_to([64, 64, 64]), op=ALU.mult))
                if mode == "b":
                    do("scalar", lambda e: e.activation(out=syb[:], in_=S_run[:], func=AF.Copy))
                    for hh in range(2):
                        hc = hh * 2048
                        mms = []
                        for hl in range(32):
                            h = hh * 32 + hl
                            mms.append(lambda e, h=h, hl=hl: e.matmul(PSA[0:64, hl * 64:(hl + 1) * 64], BIG4[0:64, h * 64:(h + 1) * 64], BIG2[0:64, h * 64:(h + 1) * 64], start=True, stop=True))
                        chain("tensor", mms)
                        chain("tensor", [
                            (lambda e, i=i, c=c, hc=hc: e.matmul(PSB[0:64, i * 512:(i + 1) * 512], BC[:, 1024 + c * 64:1024 + (c + 1) * 64], syb[:, hc + i * 512: hc + (i + 1) * 512], start=True, stop=True))
                            for i in range(4)])
                        do("vector", lambda e, hh=hh: e.tensor_tensor(out=st_tmp[0:64, :], in0=PSB[0:64, 0:2048], in1=eacsT[:, hh * 32:(hh + 1) * 32].unsqueeze(2).broadcast_to([64, 32, 64]), op=ALU.mult))
                        do("vector", lambda e, c=c, hc=hc: e.tensor_tensor(out=y_tile[(c % 2) * 64:(c % 2) * 64 + 64, hc:hc + 2048], in0=PSA[0:64, 0:2048], in1=st_tmp[0:64, :], op=ALU.add))
                for hh in range(2):
                    hc = hh * 2048
                    chain("tensor", [
                        (lambda e, i=i, hc=hc: e.matmul(PSA[:, i * 512:(i + 1) * 512], btm[:], BIG3[0:64, hc + i * 512: hc + (i + 1) * 512], start=True, stop=True))
                        for i in range(4)])
                    do("vector", lambda e, hh=hh: e.tensor_tensor(out=st_tmp[:], in0=S_run[:, hc:hc + 2048], in1=E63[:, hh * 32:(hh + 1) * 32].unsqueeze(2).broadcast_to([128, 32, 64]), op=ALU.mult))
                    do("vector", lambda e, hc=hc: e.tensor_tensor(out=S_run[:, hc:hc + 2048], in0=st_tmp[:], in1=PSA[:, 0:2048], op=ALU.add))
                if mode == "b" and c % 2 == 1:
                    tt = c // 2
                    xsb = BIG1[:, 0:2048].bitcast(BF16)
                    dma(lambda e, tt=tt: e.dma_start(out=xsb, in_=xssm_scr[tt * 128:(tt + 1) * 128, :]))
                    do("vector", lambda e: e.tensor_tensor(out=xsb, in0=xsb, in1=D64[:].unsqueeze(2).broadcast_to([128, 64, 64]), op=ALU.mult))
                    do("vector", lambda e: e.tensor_tensor(out=y_tile[:], in0=y_tile[:], in1=xsb, op=ALU.add))
                    zl = BIG3[:, 0:4096]
                    dma(lambda e, tt=tt: e.dma_start(out=zl, in_=z_scr[tt * 128:(tt + 1) * 128, :]))
                    do("vector", lambda e: e.tensor_tensor(out=y_tile[:], in0=y_tile[:], in1=zl, op=ALU.mult))
                    sqb = BIG2[:, 0:4096]
                    do("scalar", lambda e: e.activation(out=sqb, in_=y_tile[:], func=AF.Square))
                    do("vector", lambda e: e.reduce_sum(out=rs_col[:, 0:1], in_=sqb, axis=AX.X))
                    do("vector", lambda e: e.tensor_scalar(out=rs_col[:, 0:1], in0=rs_col[:, 0:1], scalar1=1.0 / DIN, scalar2=1e-5, op0=ALU.mult, op1=ALU.add))
                    do("scalar", lambda e: e.activation(out=rs_col[:, 1:2], in_=rs_col[:, 0:1], func=AF.Ln))
                    do("scalar", lambda e: e.activation(out=rs_col[:, 1:2], in_=rs_col[:, 1:2], func=AF.Exp, scale=-0.5))
                    do("scalar", lambda e: e.activation(out=y_tile[:], in_=y_tile[:], func=AF.Identity, scale=rs_col[:, 1:2]))
                    nwb = BIG3[:, 0:4096]
                    dma(lambda e: e.dma_start(out=nwb.rearrange("p (a c) -> p a c", a=1), in_=AP(tensor=nw_d.tensor, offset=0, ap=[[0, 128], [0, 1], [1, 4096]])), which="gpsimd")
                    do("vector", lambda e: e.tensor_tensor(out=y_tile[:], in0=y_tile[:], in1=nwb, op=ALU.mult))
                    for k in range(32):
                        do("tensor", lambda e, k=k: e.transpose(psT[:], y_tile[:, k * 128:(k + 1) * 128], identb[:]))
                        do("vector", lambda e, k=k, tt=tt: e.tensor_copy(out=A32[:, k * 512 + (tt % 4) * 128: k * 512 + (tt % 4) * 128 + 128], in_=psT[:]))
                if mode == "b" and c in (7, 15):
                    tchunk = 0 if c == 7 else 1
                    for ft in range(ND):
                        dma(lambda e, ft=ft: e.dma_start(
                            out=BIG2[:].rearrange("p (a c2) -> p a c2", c2=128),
                            in_=opT_d[:, ft * 128:(ft + 1) * 128].rearrange("(a p) c2 -> p a c2", p=128)), which="gpsimd")
                        chain("tensor", [
                            (lambda e, k=k: e.matmul(psC[:], BIG2[:, k * 128:(k + 1) * 128], A32[:, k * 512:(k + 1) * 512], start=(k == 0), stop=(k == 31)))
                            for k in range(32)])
                        dma(lambda e, ft=ft, tchunk=tchunk: e.dma_start(out=x2s[:], in_=x_d[ft * 128:(ft + 1) * 128, tchunk * 512:(tchunk + 1) * 512]))
                        do("vector", lambda e: e.tensor_tensor(out=x2s[:], in0=x2s[:], in1=psC[:], op=ALU.add))
                        dma(lambda e, ft=ft, tchunk=tchunk: e.dma_start(out=x3_d[ft * 128:(ft + 1) * 128, tchunk * 512:(tchunk + 1) * 512], in_=x2s[:]))
            if mode == "a":
                dma(lambda e: e.dma_start(out=sfin_d[:, :], in_=S_run[:]))
    return emit


def build_u5(mode):
    nc = bass.Bass("TRN2", target_bir_lowering=False)
    io = {
        "x_m": nc.dram_tensor("x", [D, TOK], F32, kind="ExternalInput").ap(),
        "xh": nc.dram_tensor("xh", [D, 3], F32, kind="ExternalInput").ap(),
        "ipT": nc.dram_tensor("ipT", [D, DIP], F32, kind="ExternalInput").ap(),
        "cw": nc.dram_tensor("cw", [DXBC, 4], F32, kind="ExternalInput").ap(),
        "cb": nc.dram_tensor("cb", [DXBC], F32, kind="ExternalInput").ap(),
        "alog": nc.dram_tensor("alog", [NH_M], F32, kind="ExternalInput").ap(),
        "dtb": nc.dram_tensor("dtb", [NH_M], F32, kind="ExternalInput").ap(),
        "acs_scr": nc.dram_tensor("acs_scr", [NH_M, CHUNK], F32, kind="Internal").ap(),
        "xssm_scr": nc.dram_tensor("xssm_scr", [TOK, DIN], BF16, kind="Internal").ap(),
    }
    if mode == "b":
        io["dvec"] = nc.dram_tensor("dvec", [NH_M], F32, kind="ExternalInput").ap()
        io["nw"] = nc.dram_tensor("nw", [DIN], F32, kind="ExternalInput").ap()
        io["opT"] = nc.dram_tensor("opT", [DIN, D], F32, kind="ExternalInput").ap()
        io["sf"] = nc.dram_tensor("sf", [NCORES * 128, DIN], F32, kind="ExternalInput").ap()
        io["phiT"] = nc.dram_tensor("phiT", [NH_M, NCORES], F32, kind="ExternalInput").ap()
        io["maskr"] = nc.dram_tensor("maskr", [NH_M, NCORES], F32, kind="ExternalInput").ap()
        io["ws_scr"] = nc.dram_tensor("ws_scr", [8, 64], F32, kind="Internal").ap()
        io["x3"] = nc.dram_tensor("x3", [D, TOK], F32, kind="ExternalOutput").ap()
        io["z_scr"] = nc.dram_tensor("z_scr", [TOK, DIN], BF16, kind="Internal").ap()
    else:
        io["sfin"] = nc.dram_tensor("sfin", [128, DIN], F32, kind="ExternalOutput").ap()
        io["phi"] = nc.dram_tensor("phi", [NH_M, 1], F32, kind="ExternalOutput").ap()
    st = contextlib.ExitStack()
    io["PSA"] = st.enter_context(nc.psum_tensor("PSA", [128, 2048], F32))
    io["PSB"] = st.enter_context(nc.psum_tensor("PSB", [128, 2048], F32))
    io["sst"] = st.enter_context(nc.semaphore("sst"))
    ab = Arena(nc)
    run_phases(nc, [make_u5(nc, io, ab, mode)])
    st.close()
    return nc


NF1 = HID1 // 128


def make_u6(nc, io, ab):
    ab.reset("u6_")
    x_d = io["x_f"]
    wgq_d, wuq_d, wdq_d = io["wgq"], io["wuq"], io["wdq"]
    sg_d, su_d, sd_d = io["sg"], io["su"], io["sd"]
    x4_d = io["x4"]
    PSA, PSB = io["PSA"], io["PSB"]
    sb = ab.sb
    A = sb("A", [128, ND * TOK], BF16)
    GU = sb("GU", [128, NF1 * TOK], BF16)
    XCH = GU[:, 0:16384].bitcast(F32)
    wcol_b = [sb(f"wcol_b{i}", [128, 2048], BF16) for i in range(2)]
    wdcol_b = [sb(f"wdcol_b{i}", [128, NF1 * 128], BF16) for i in range(2)]
    gt_sb = sb("gt_sb", [128, 512], BF16)
    tf2 = sb("tf2", [128, 512], F32)
    scol = sb("scol", [128, 2 * NF1 + ND], F32)
    x2s = sb("x2s", [128, 512], F32)
    hp = ln_helpers(ab, PSB)
    psC = PSA[:, 0:512]
    psC2 = PSA[:, 512:1024]

    def emit(ctx, sem):
        if True:
            w = lambda: [(sem, ctx.val(sem))]

            def chain(which, fns, inc_last=1, extra=()):
                if ctx.en == which:
                    ctx.eng.wait_ge(sem, ctx.val(sem))
                    for (s2, v2) in extra:
                        ctx.eng.wait_ge(s2, v2)
                    for fn in fns:
                        ins = fn(ctx.eng)
                    ins.then_inc(sem, inc_last)
                ctx._bump(sem, inc_last)

            def do(which, f2):
                ctx.do(which, f2, sem=sem, wait=w())

            def dma(f2, which="sync"):
                ctx.do(which, f2, sem=sem, inc=16, wait=w())

            do("gpsimd", lambda e: e.memset(hp["ones_col_b"][:], 1.0))
            do("gpsimd", lambda e: e.memset(hp["ones_row"][:], 1.0))
            ctx.burst("sync", [
                (lambda e, ff=ff: e.dma_start(out=scol[:, ff:ff + 1], in_=sg_d[ff * 128:(ff + 1) * 128].unsqueeze(1)))
                for ff in range(NF1)] + [
                (lambda e, ff=ff: e.dma_start(out=scol[:, NF1 + ff:NF1 + ff + 1], in_=su_d[ff * 128:(ff + 1) * 128].unsqueeze(1)))
                for ff in range(NF1)] + [
                (lambda e, ft=ft: e.dma_start(out=scol[:, 2 * NF1 + ft:2 * NF1 + ft + 1], in_=sd_d[ft * 128:(ft + 1) * 128].unsqueeze(1)))
                for ft in range(ND)], sem)
            for tc in range(2):
                for d in range(ND):
                    dma(lambda e, d=d, tc=tc: e.dma_start(out=XCH[:, d * 512:(d + 1) * 512], in_=x_d[d * 128:(d + 1) * 128, tc * 512:(tc + 1) * 512]))
                emit_ln_fm2(ctx, sem, lambda d, tcc: XCH[:, d * 512:(d + 1) * 512], A, hp, [tc], 512, 1024)

            sstS = io["sst"]
            CH = []
            for ff in range(NF1):
                CH.append(("gu", wgq_d[:, ff * 128:(ff + 1) * 128]))
                CH.append(("gu", wuq_d[:, ff * 128:(ff + 1) * 128]))
            for ft in range(ND):
                CH.append(("d", wdq_d[:, ft * 128:(ft + 1) * 128]))
            ring = {"g": 0, "u": 0, "dn": [], "cons": []}

            def pump():
                g = ring["g"]
                if g >= len(CH):
                    return
                kind, src = CH[g]
                buf = wcol_b[g % 2] if kind == "gu" else wdcol_b[g % 2]
                wt = (w() if g < 2 else [(sem, ring["cons"][g - 2])]) + ([] if g == 0 else [(sstS, ring["dn"][g - 1])])
                ring["dn"].append(ctx.do("gpsimd", lambda e, src=src, buf=buf, g=g: e.dma_start(
                    out=buf[:].rearrange("p (a c) -> p a c", c=128),
                    in_=src.rearrange("(a p) c -> p a c", p=128)), sem=sstS, inc=16, wait=wt))
                ring["g"] += 1

            def use():
                g = ring["u"]
                kind = CH[g][0]
                buf = wcol_b[g % 2] if kind == "gu" else wdcol_b[g % 2]
                return buf, [(sstS, ring["dn"][g])]

            def done_use():
                ring["cons"].append(ctx.val(sem))
                ring["u"] += 1
                pump()

            pump()
            pump()
            for ff in range(NF1):
                for (sc_off, _) in ((0, 0), (NF1, 0)):
                    rb, ex = use()
                    for tc in range(2):
                        chain("tensor", [
                            (lambda e, d=d, tc=tc, rb=rb: e.matmul(psC if tc == 0 else psC2, rb[:, d * 128:(d + 1) * 128], A[:, d * 1024 + tc * 512: d * 1024 + tc * 512 + 512], start=(d == 0), stop=(d == ND - 1)))
                            for d in range(ND)], extra=(ex if tc == 0 else ()))
                    done_use()
                    if sc_off == 0:
                        for tc in range(2):
                            do("scalar", lambda e, ff=ff, tc=tc: e.activation(out=GU[:, ff * 1024 + tc * 512: ff * 1024 + tc * 512 + 512], in_=(psC if tc == 0 else psC2), func=AF.Silu, scale=scol[:, ff:ff + 1]))
                    else:
                        for tc in range(2):
                            do("scalar", lambda e, ff=ff, tc=tc: e.activation(out=gt_sb[:], in_=(psC if tc == 0 else psC2), func=AF.Identity, scale=scol[:, NF1 + ff:NF1 + ff + 1]))
                            do("vector", lambda e, ff=ff, tc=tc: e.tensor_tensor(out=GU[:, ff * 1024 + tc * 512: ff * 1024 + tc * 512 + 512], in0=GU[:, ff * 1024 + tc * 512: ff * 1024 + tc * 512 + 512], in1=gt_sb[:], op=ALU.mult))
            for ft in range(ND):
                rb, ex = use()
                for tc in range(2):
                    chain("tensor", [
                        (lambda e, f=f2, tc=tc, rb=rb: e.matmul(psC if tc == 0 else psC2, rb[:, f * 128:(f + 1) * 128], GU[:, f * 1024 + tc * 512: f * 1024 + tc * 512 + 512], start=(f == 0), stop=(f == NF1 - 1)))
                        for f2 in range(NF1)], extra=(ex if tc == 0 else ()))
                done_use()
                for tc in range(2):
                    do("scalar", lambda e, ft=ft, tc=tc: e.activation(out=tf2[:], in_=(psC if tc == 0 else psC2), func=AF.Identity, scale=scol[:, 2 * NF1 + ft:2 * NF1 + ft + 1]))
                    dma(lambda e, ft=ft, tc=tc: e.dma_start(out=x2s[:], in_=x_d[ft * 128:(ft + 1) * 128, tc * 512:(tc + 1) * 512]))
                    do("vector", lambda e: e.tensor_tensor(out=x2s[:], in0=x2s[:], in1=tf2[:], op=ALU.add))
                    dma(lambda e, ft=ft, tc=tc: e.dma_start(out=x4_d[ft * 128:(ft + 1) * 128, tc * 512:(tc + 1) * 512], in_=x2s[:]))
    return emit


def build_u6():
    nc = bass.Bass("TRN2", target_bir_lowering=False)
    io = {
        "x_f": nc.dram_tensor("x", [D, TOK], F32, kind="ExternalInput").ap(),
        "wgq": nc.dram_tensor("wgq", [D, HID1], BF16, kind="ExternalInput").ap(),
        "wuq": nc.dram_tensor("wuq", [D, HID1], BF16, kind="ExternalInput").ap(),
        "wdq": nc.dram_tensor("wdq", [HID1, D], BF16, kind="ExternalInput").ap(),
        "sg": nc.dram_tensor("sg", [HID1], F32, kind="ExternalInput").ap(),
        "su": nc.dram_tensor("su", [HID1], F32, kind="ExternalInput").ap(),
        "sd": nc.dram_tensor("sd", [D], F32, kind="ExternalInput").ap(),
        "x4": nc.dram_tensor("x4", [D, TOK], F32, kind="ExternalOutput").ap(),
    }
    st = contextlib.ExitStack()
    io["PSA"] = st.enter_context(nc.psum_tensor("PSA", [128, 2048], F32))
    io["PSB"] = st.enter_context(nc.psum_tensor("PSB", [128, 2048], F32))
    io["sst"] = st.enter_context(nc.semaphore("sst"))
    ab = Arena(nc)
    run_phases(nc, [make_u6(nc, io, ab)])
    st.close()
    return nc


def build_full():
    nc = bass.Bass("TRN2", target_bir_lowering=False, detect_race_conditions=False)
    io = {}
    T = nc.dram_tensor
    io["x0"] = T("x0", [D, TOK], F32, kind="ExternalInput").ap()
    io["wqT"] = T("wqT", [D, HPC * HD], F32, kind="ExternalInput").ap()
    io["wkT"] = T("wkT", [D, HPC * HD], F32, kind="ExternalInput").ap()
    io["wvT"] = T("wvT", [D, HPC * HD], F32, kind="ExternalInput").ap()
    io["bq"] = T("bq", [HPC * HD], F32, kind="ExternalInput").ap()
    io["bk"] = T("bk", [HPC * HD], F32, kind="ExternalInput").ap()
    io["bv"] = T("bv", [HPC * HD], F32, kind="ExternalInput").ap()
    io["woT"] = T("woT", [D, D], F32, kind="ExternalInput").ap()
    io["bo"] = T("bo", [D], F32, kind="ExternalInput").ap()
    io["w1T"] = T("w1T", [D, DFF0], F32, kind="ExternalInput").ap()
    io["b1"] = T("b1", [DFF0], F32, kind="ExternalInput").ap()
    io["w2T"] = T("w2T", [DFF0, D], F32, kind="ExternalInput").ap()
    io["b2"] = T("b2", [D], F32, kind="ExternalInput").ap()
    io["ipT"] = T("ipT", [D, DIP], F32, kind="ExternalInput").ap()
    io["cw"] = T("cw", [DXBC, 4], F32, kind="ExternalInput").ap()
    io["cb"] = T("cb", [DXBC], F32, kind="ExternalInput").ap()
    io["alog"] = T("alog", [NH_M], F32, kind="ExternalInput").ap()
    io["dtb"] = T("dtb", [NH_M], F32, kind="ExternalInput").ap()
    io["dvec"] = T("dvec", [NH_M], F32, kind="ExternalInput").ap()
    io["nw"] = T("nw", [DIN], F32, kind="ExternalInput").ap()
    io["opT"] = T("opT", [DIN, D], F32, kind="ExternalInput").ap()
    io["maskr"] = T("maskr", [NH_M, NCORES], F32, kind="ExternalInput").ap()
    io["zmask"] = T("zmask", [128, 1], F32, kind="ExternalInput").ap()
    io["w1s"] = T("w1s", [D, S // NCORES], F32, kind="ExternalInput").ap()
    io["w2s"] = T("w2s", [DFF0, D // NCORES], F32, kind="ExternalInput").ap()
    io["wgT"] = T("wgT", [D, HID1], I32, kind="ExternalInput").ap()
    io["wuT"] = T("wuT", [D, HID1], I32, kind="ExternalInput").ap()
    io["wdT"] = T("wdT", [HID1, D], I32, kind="ExternalInput").ap()
    io["sg"] = T("sg", [HID1], F32, kind="ExternalInput").ap()
    io["su"] = T("su", [HID1], F32, kind="ExternalInput").ap()
    io["sd"] = T("sd", [D], F32, kind="ExternalInput").ap()
    io["x4"] = T("x4", [D, TOK], F32, kind="ExternalOutput").ap()
    if _FDBG:
        d_lnx = T("d_lnx", [D, TOK], BF16, kind="ExternalOutput").ap()
        d_attn = T("d_attn", [HPC * HD, S], BF16, kind="ExternalOutput").ap()
        d_x2 = T("d_x2", [D, TOK], F32, kind="ExternalOutput").ap()
        d_x3 = T("d_x3", [D, TOK], F32, kind="ExternalOutput").ap()
        d_sphi = T("d_sphi", [132, DIN], F32, kind="ExternalOutput").ap()
        io["d_A"] = T("d_A", [D, TOK], BF16, kind="ExternalOutput").ap()
        io["d_x1"] = T("d_x1", [D, TOK], F32, kind="ExternalOutput").ap()
        io["d_thr"] = T("d_thr", [128, 4], F32, kind="ExternalOutput").ap()
        d_ws = T("d_ws", [16, 2], F32, kind="ExternalOutput").ap()
    io["lnx_c"] = T("lnx_c", [D, TOK], BF16, kind="Internal").ap()
    lnx_all = T("lnx_all", [NCORES * D, TOK], BF16, kind="Internal").ap()
    io["wsum_c"] = T("wsum_c", [1, 2], F32, kind="Internal").ap()
    io["wsum_all"] = T("wsum_all", [NCORES, 2], F32, kind="Internal").ap()
    io["attn_c"] = T("attn_c", [HPC * HD, S], BF16, kind="Internal").ap()
    attn_all = T("attn_all", [D, S], BF16, kind="Internal").ap()
    io["xbce_c"] = T("xbce_c", [DXBC, 4], BF16, kind="Internal").ap()
    io["xbce_all"] = T("xbce_all", [NCORES * DXBC, 4], BF16, kind="Internal").ap()
    sphi_c = T("sphi_c", [132, DIN], F32, kind="Internal").ap()
    sphi_all = T("sphi_all", [NCORES * 132, DIN], F32, kind="Internal").ap()
    io["x2"] = T("x2_scr", [D, TOK], F32, kind="Internal").ap()
    io["x3"] = T("x3_scr", [D, TOK], F32, kind="Internal").ap()
    io["w1q"] = T("w1q_scr", [D, DFF0], BF16, kind="Internal").ap()
    io["w2q"] = T("w2q_scr", [DFF0, D], BF16, kind="Internal").ap()
    io["wgq"] = T("wgq_scr", [D, HID1], BF16, kind="Internal").ap()
    io["wuq"] = T("wuq_scr", [D, HID1], BF16, kind="Internal").ap()
    io["wdq"] = T("wdq_scr", [HID1, D], BF16, kind="Internal").ap()
    io["acs_scr"] = T("acs_scr", [NH_M, CHUNK], F32, kind="Internal").ap()
    io["xssm_scr"] = T("xssm_scr", [TOK, DIN], BF16, kind="Internal").ap()
    io["z_scr"] = T("z_scr", [TOK, DIN], BF16, kind="Internal").ap()
    io["ws_scr"] = T("ws_scr", [8, 64], F32, kind="Internal").ap()

    st = contextlib.ExitStack()
    io["PSA"] = st.enter_context(nc.psum_tensor("PSA", [128, 2048], F32))
    io["PSB"] = st.enter_context(nc.psum_tensor("PSB", [128, 2048], F32))
    io["sst"] = st.enter_context(nc.semaphore("sst"))
    io["pq_in"] = st.enter_context(nc.semaphore("pq_in"))
    io["pq_v"] = st.enter_context(nc.semaphore("pq_v"))
    io["pq_out"] = st.enter_context(nc.semaphore("pq_out"))

    io["lnx_tile"] = lambda d, qc: lnx_all[(qc // 2) * D + d * 128: (qc // 2) * D + (d + 1) * 128, (qc % 2) * 512:(qc % 2) * 512 + 512]

    def attn_dyn(eng, A):
        off = eng.snap(eng.partition_id() * TOK)
        src_ap = attn_all[:, bass.ds(off, TOK)].rearrange("(dp p) t -> p dp t", p=128)
        return eng.dma_start(out=A[:, 0:ND * TOK].rearrange("p (dp t) -> p dp t", t=TOK), in_=src_ap)
    io["attn_dyn"] = attn_dyn
    io["wsum_mode"] = "fused"
    io["x_m"] = io["x2"]
    io["x_f"] = io["x3"]
    io["sfin"] = sphi_c[0:128, :]
    io["phi"] = sphi_c[128:129, 0:64]
    io["phi_is_row"] = True
    io["sf_row"] = lambda j: sphi_all[j * 132:j * 132 + 128, :]
    io["phi_col"] = lambda j: sphi_all[j * 132 + 128:j * 132 + 129, 0:64]

    def prevoff_load(ctx):
        holder = {}

        def fn(ct):
            if "base" not in holder:
                cid = ctx.eng.partition_id()
                holder["base"] = ctx.eng.snap(((cid + 7) % 8) * DXBC)
            return holder["base"] + ct * 128
        return (None, fn)
    io["prevoff_load"] = prevoff_load

    def cc(ctx, sem, kind, ins_ap, outs_ap):
        ctx.do("gpsimd", lambda e: e.collective_compute(
            kind, ALU.bypass, replica_groups=[list(range(NCORES))],
            ins=[ins_ap.opt()], outs=[outs_ap.opt()]), sem=sem, inc=1,
            wait=[(sem, ctx.val(sem))])

    def mid_ag3(ctx, sem):
        cc(ctx, sem, "AllGather", io["xbce_c"], io["xbce_all"])
    io["mid_ag3"] = mid_ag3

    ab = Arena(nc)
    em_u1 = make_u1(nc, io, ab)
    em_u2 = make_u2(nc, io, ab)
    em_u3 = make_u3(nc, io, ab)
    em_u5a = make_u5(nc, io, ab, "a", halo_mode="gather")
    em_u5b = make_u5(nc, io, ab, "b", halo_mode="gather", skip_front=True)
    em_u6 = make_u6(nc, io, ab)

    def em_wsum(ctx, sem):
        w = lambda: [(sem, ctx.val(sem))]
        wf = wsum_buf
        acc = wsum_acc
        ctx.do("gpsimd", lambda e: e.memset(acc[:], 0.0), sem=sem, wait=w())
        for (wT, colacc, rows, width, grp) in ((io["w1s"], 0, D, 1024, 1), (io["w2s"], 1, DFF0, 256, 4)):
            for rg in range(rows // 128 // grp):
                ctx.burst("sync", [
                    (lambda e, r=rg * grp + i, wT=wT, width=width, i=i: e.dma_start(out=wf[:, i * width:(i + 1) * width], in_=wT[(rg * grp + i) * 128:(rg * grp + i + 1) * 128, :]))
                    for i in range(grp)], sem)
                for i in range(grp):
                    ctx.do("vector", lambda e, width=width, i=i: e.reduce_sum(out=acc[:, 2:3], in_=wf[:, i * width:(i + 1) * width], axis=AX.X, apply_absolute_value=True), sem=sem, wait=w())
                    ctx.do("vector", lambda e, colacc=colacc: e.tensor_add(out=acc[:, colacc:colacc + 1], in0=acc[:, colacc:colacc + 1], in1=acc[:, 2:3]), sem=sem, wait=w())
        ctx.do("gpsimd", lambda e: e.memset(wsum_onesf[:], 1.0), sem=sem, wait=w())
        for i in range(2):
            ctx.do("tensor", lambda e, i=i: e.matmul(io["PSB"][0:1, i:i + 1], wsum_onesf[:], acc[:, i:i + 1], start=True, stop=True), sem=sem, wait=w())
        ctx.do("scalar", lambda e: e.activation(out=acc[0:1, 0:2], in_=io["PSB"][0:1, 0:2], func=AF.Copy), sem=sem, wait=w())
        ctx.do("sync", lambda e: e.dma_start(out=io["wsum_c"][:, :], in_=acc[0:1, 0:2]), sem=sem, inc=16, wait=w())

    ab.reset("ws_")
    wsum_buf = ab.sb("wsum_buf", [128, 1024], F32)
    wsum_acc = ab.sb("wsum_acc", [128, 4], F32)
    wsum_onesf = ab.sb("wsum_onesf", [128, 1], F32)

    def em_ag1(ctx, sem):
        cc(ctx, sem, "AllGather", io["lnx_c"], lnx_all)
        cc(ctx, sem, "AllGather", io["wsum_c"], io["wsum_all"])

    def em_ag2(ctx, sem):
        cc(ctx, sem, "AllGather", io["attn_c"], attn_all)

    def em_ag4(ctx, sem):
        cc(ctx, sem, "AllGather", sphi_c, sphi_all)

    def em_dbg(ctx, sem):
        ctx.do("sync", lambda e: e.dma_start(out=d_ws[0:8, :], in_=io["wsum_all"][:, :]), sem=sem, inc=16, wait=[(sem, ctx.val(sem))])
        ctx.do("sync", lambda e: e.dma_start(out=d_ws[8:9, :], in_=io["wsum_c"][:, :]), sem=sem, inc=16, wait=[(sem, ctx.val(sem))])
        for (dst, srcb) in ((d_lnx, io["lnx_c"]), (d_attn, io["attn_c"]), (d_x2, io["x2"]), (d_x3, io["x3"]), (d_sphi, sphi_c)):
            ctx.do("sync", lambda e, dst=dst, srcb=srcb: e.dma_start(out=dst[:, :], in_=srcb[:, :]), sem=sem, inc=16, wait=[(sem, ctx.val(sem))])

    emits = [em_u1, em_wsum, em_ag1, em_u2, em_ag2, em_u3, em_u5a, em_ag4, em_u5b, em_u6] + ([em_dbg] if _FDBG else [])
    run_phases_multi(nc, emits)
    st.close()
    return nc


def run_phases_multi(nc, emits):
    st2 = contextlib.ExitStack()
    sems = [st2.enter_context(nc.semaphore(f"s{i}")) for i in range(len(emits))]

    def prog(eng_name):
        def f(eng):
            ctx = Ctx(eng_name, eng)
            prev_sem, prev_val = None, 0
            for i, em in enumerate(emits):
                if prev_sem is not None:
                    for en in ENGS:
                        ctx.wait(en, prev_sem, prev_val)
                em(ctx, sems[i])
                prev_sem, prev_val = sems[i], ctx.val(sems[i])
            ctx.wait("sync", prev_sem, prev_val)
        return f

    with nc.Block() as block:
        for en in ENGS:
            getattr(block, en)(prog(en))
    st2.close()


def kernel_fused(inputs):
    x = np.ascontiguousarray(np.asarray(inputs["L0_attn_hidden_states"]).reshape(S, D))
    nc = build_full()
    wqT = np.ascontiguousarray(np.asarray(inputs["L0_attn_Wq"]).T)
    wkT = np.ascontiguousarray(np.asarray(inputs["L0_attn_Wk"]).T)
    wvT = np.ascontiguousarray(np.asarray(inputs["L0_attn_Wv"]).T)
    in_maps = []
    for c in range(NCORES):
        sl = slice(c * HPC * HD, (c + 1) * HPC * HD)
        m = {
            "x0": np.ascontiguousarray(x[c * TOK:(c + 1) * TOK].T),
            "wqT": np.ascontiguousarray(wqT[:, sl]),
            "wkT": np.ascontiguousarray(wkT[:, sl]),
            "wvT": np.ascontiguousarray(wvT[:, sl]),
            "bq": np.ascontiguousarray(np.asarray(inputs["L0_attn_bq"])[sl]),
            "bk": np.ascontiguousarray(np.asarray(inputs["L0_attn_bk"])[sl]),
            "bv": np.ascontiguousarray(np.asarray(inputs["L0_attn_bv"])[sl]),
            "woT": np.ascontiguousarray(np.asarray(inputs["L0_attn_Wo"]).T),
            "bo": np.asarray(inputs["L0_attn_bo"]),
            "w1T": np.ascontiguousarray(np.asarray(inputs["L0_ffn_w1"]).T),
            "w1s": np.ascontiguousarray(np.asarray(inputs["L0_ffn_w1"]).T[:, c * 1024:(c + 1) * 1024]),
            "w2s": np.ascontiguousarray(np.asarray(inputs["L0_ffn_w2"]).T[:, c * 256:(c + 1) * 256]),
            "b1": np.asarray(inputs["L0_ffn_b1"]),
            "w2T": np.ascontiguousarray(np.asarray(inputs["L0_ffn_w2"]).T),
            "b2": np.asarray(inputs["L0_ffn_b2"]),
            "ipT": np.ascontiguousarray(np.asarray(inputs["L1_attn_in_proj_w"]).T),
            "cw": np.ascontiguousarray(np.asarray(inputs["L1_attn_conv_w"]).reshape(DXBC, 4)),
            "cb": np.asarray(inputs["L1_attn_conv_b"]),
            "alog": np.asarray(inputs["L1_attn_A_log"]),
            "dtb": np.asarray(inputs["L1_attn_dt_bias"]),
            "dvec": np.asarray(inputs["L1_attn_D"]),
            "nw": np.asarray(inputs["L1_attn_norm_w"]),
            "opT": np.ascontiguousarray(np.asarray(inputs["L1_attn_out_proj_w"]).T),
            "maskr": np.ascontiguousarray(np.tile((np.arange(NCORES) < c).astype(np.float32), (NH_M, 1))),
            "zmask": np.full((128, 1), 0.0 if c == 0 else 1.0, np.float32),
            "wgT": np.ascontiguousarray(np.asarray(inputs["L1_ffn_wq_gate"]).T),
            "wuT": np.ascontiguousarray(np.asarray(inputs["L1_ffn_wq_up"]).T),
            "wdT": np.ascontiguousarray(np.asarray(inputs["L1_ffn_wq_down"]).T),
            "sg": np.asarray(inputs["L1_ffn_s_gate"]),
            "su": np.asarray(inputs["L1_ffn_s_up"]),
            "sd": np.asarray(inputs["L1_ffn_s_down"]),
        }
        in_maps.append(m)
    res = run_bass_kernel_spmd(nc, in_maps, core_ids=list(range(NCORES)))
    if _FDBG:
        global _DBG
        _DBG = {k: [res.results[c][k] for c in range(NCORES)] for k in ("d_lnx", "d_attn", "d_x2", "d_x3", "d_sphi", "d_A", "d_x1", "d_thr", "d_ws")}
    x4_full = np.concatenate([res.results[c]["x4"] for c in range(NCORES)], axis=1)
    return np.ascontiguousarray(x4_full.T).reshape(1, S, D)


_BF = ml_dtypes.bfloat16


def kernel(**inputs):
    global _DBG
    if _STAGE == "fused":
        return kernel_fused(inputs)
    x = np.ascontiguousarray(np.asarray(inputs["L0_attn_hidden_states"]).reshape(S, D))
    if _X2_OVERRIDE is not None:
        x2_full = _X2_OVERRIDE
        return _run_u5(inputs, x2_full)
    nc1 = build_u1()
    in_maps = [{"x": np.ascontiguousarray(x[c * TOK:(c + 1) * TOK].T)} for c in range(NCORES)]
    res = run_bass_kernel_spmd(nc1, in_maps, core_ids=list(range(NCORES)))
    lnx_full = np.concatenate([res.results[c]["lnx"] for c in range(NCORES)], axis=1)
    if _STAGE == "u1":
        return lnx_full

    wqT = np.ascontiguousarray(np.asarray(inputs["L0_attn_Wq"]).T)
    wkT = np.ascontiguousarray(np.asarray(inputs["L0_attn_Wk"]).T)
    wvT = np.ascontiguousarray(np.asarray(inputs["L0_attn_Wv"]).T)
    bq = np.asarray(inputs["L0_attn_bq"]); bk = np.asarray(inputs["L0_attn_bk"]); bv = np.asarray(inputs["L0_attn_bv"])
    nc2 = build_u2()
    in_maps = []
    for c in range(NCORES):
        sl = slice(c * HPC * HD, (c + 1) * HPC * HD)
        in_maps.append({
            "lnx": lnx_full,
            "wqT": np.ascontiguousarray(wqT[:, sl]),
            "wkT": np.ascontiguousarray(wkT[:, sl]),
            "wvT": np.ascontiguousarray(wvT[:, sl]),
            "bq": np.ascontiguousarray(bq[sl]),
            "bk": np.ascontiguousarray(bk[sl]),
            "bv": np.ascontiguousarray(bv[sl]),
        })
    res = run_bass_kernel_spmd(nc2, in_maps, core_ids=list(range(NCORES)))
    attn_full = np.concatenate([res.results[c]["attn"] for c in range(NCORES)], axis=0)
    if _STAGE == "u2":
        return attn_full

    woT = np.ascontiguousarray(np.asarray(inputs["L0_attn_Wo"]).T)
    bo = np.asarray(inputs["L0_attn_bo"])
    w1T = np.ascontiguousarray(np.asarray(inputs["L0_ffn_w1"]).T)
    b1 = np.asarray(inputs["L0_ffn_b1"])
    w2T = np.ascontiguousarray(np.asarray(inputs["L0_ffn_w2"]).T)
    b2 = np.asarray(inputs["L0_ffn_b2"])
    nc3 = build_u3()
    in_maps = []
    for c in range(NCORES):
        in_maps.append({
            "x": np.ascontiguousarray(x[c * TOK:(c + 1) * TOK].T),
            "attn": np.ascontiguousarray(attn_full[:, c * TOK:(c + 1) * TOK]),
            "woT": woT, "bo": bo, "w1T": w1T, "b1": b1, "w2T": w2T, "b2": b2,
        })
    res = run_bass_kernel_spmd(nc3, in_maps, core_ids=list(range(NCORES)))
    x2_full = np.concatenate([res.results[c]["x2"] for c in range(NCORES)], axis=1)
    if _STAGE == "u3":
        return x2_full
    return _run_u5(inputs, x2_full)


def _run_u5(inputs, x2_full):
    global _DBG
    ipT = np.ascontiguousarray(np.asarray(inputs["L1_attn_in_proj_w"]).T)
    cw = np.ascontiguousarray(np.asarray(inputs["L1_attn_conv_w"]).reshape(DXBC, 4))
    cb = np.asarray(inputs["L1_attn_conv_b"])
    alog = np.asarray(inputs["L1_attn_A_log"])
    dtb = np.asarray(inputs["L1_attn_dt_bias"])
    dvec = np.asarray(inputs["L1_attn_D"])
    nw = np.asarray(inputs["L1_attn_norm_w"])
    opT = np.ascontiguousarray(np.asarray(inputs["L1_attn_out_proj_w"]).T)
    zero_h = np.zeros((D, 3), np.float32)
    nc5a = build_u5("a")
    in_maps = []
    for c in range(NCORES):
        xs = np.ascontiguousarray(x2_full[:, c * TOK:(c + 1) * TOK])
        xh = zero_h if c == 0 else np.ascontiguousarray(x2_full[:, c * TOK - 3:c * TOK])
        in_maps.append({"x": xs, "xh": xh, "ipT": ipT, "cw": cw, "cb": cb, "alog": alog, "dtb": dtb})
    res = run_bass_kernel_spmd(nc5a, in_maps, core_ids=list(range(NCORES)))
    sf = np.concatenate([res.results[c]["sfin"] for c in range(NCORES)], axis=0)
    phiT = np.concatenate([res.results[c]["phi"] for c in range(NCORES)], axis=1)
    if _STAGE == "u5a":
        return sf, phiT

    nc5b = build_u5("b")
    in_maps = []
    for c in range(NCORES):
        xs = np.ascontiguousarray(x2_full[:, c * TOK:(c + 1) * TOK])
        xh = zero_h if c == 0 else np.ascontiguousarray(x2_full[:, c * TOK - 3:c * TOK])
        maskr = np.ascontiguousarray(np.tile((np.arange(NCORES) < c).astype(np.float32), (NH_M, 1)))
        in_maps.append({"x": xs, "xh": xh, "ipT": ipT, "cw": cw, "cb": cb, "alog": alog, "dtb": dtb,
                        "dvec": dvec, "nw": nw, "opT": opT, "sf": sf, "phiT": np.ascontiguousarray(phiT), "maskr": maskr})
    res = run_bass_kernel_spmd(nc5b, in_maps, core_ids=list(range(NCORES)))
    x3_full = np.concatenate([res.results[c]["x3"] for c in range(NCORES)], axis=1)
    _DBG = {"x2": x2_full}
    if _STAGE == "u5":
        return x3_full

    wgT = np.ascontiguousarray(np.asarray(inputs["L1_ffn_wq_gate"]).T)
    wuT = np.ascontiguousarray(np.asarray(inputs["L1_ffn_wq_up"]).T)
    wdT = np.ascontiguousarray(np.asarray(inputs["L1_ffn_wq_down"]).T)
    sg = np.asarray(inputs["L1_ffn_s_gate"]); su = np.asarray(inputs["L1_ffn_s_up"]); sd = np.asarray(inputs["L1_ffn_s_down"])
    nc6 = build_u6()
    in_maps = []
    for c in range(NCORES):
        in_maps.append({"x": np.ascontiguousarray(x3_full[:, c * TOK:(c + 1) * TOK]),
                        "wgT": wgT, "wuT": wuT, "wdT": wdT, "sg": sg, "su": su, "sd": sd})
    res = run_bass_kernel_spmd(nc6, in_maps, core_ids=list(range(NCORES)))
    x4_full = np.concatenate([res.results[c]["x4"] for c in range(NCORES)], axis=1)
    if _STAGE == "u6":
        return x4_full
    return np.ascontiguousarray(x4_full.T).reshape(1, S, D)


_STAGE = "fused"
_FDBG = False
_DBG = None
_X2_OVERRIDE = None
